# version 01
# baseline (speedup 1.0000x reference)
import numpy as np
import ml_dtypes
import concourse.bass as bass
import concourse.mybir as mybir
from concourse.bass_utils import run_bass_kernel_spmd
from concourse.library_config import mlp as _mlp_lib

F32 = mybir.dt.float32
BF16 = mybir.dt.bfloat16
I16 = mybir.dt.int16
AF = mybir.ActivationFunctionType
ALU = mybir.AluOpType
AX = mybir.AxisListType
BF = ml_dtypes.bfloat16

NCORES = 8
NTOK = 8192
D = 2048
H = 8
DH = 128
WIN = 512
NE = 32
TPC = NTOK // NCORES
SL = TPC + WIN
NT = SL // 128
NSLOT = 4
MASKVAL = -30000.0
YROWS = NTOK + 128

ENGINES = ("sync", "tensor", "vector", "scalar", "gpsimd")
DEBUG_DUMPS = False
SKIP_ATTN = False
SKIP_MOE = False



def _pack_idx(vals, ncols):
    arr = np.zeros((128, ncols), np.int16)
    j = np.arange(len(vals))
    v = np.asarray(vals, np.int16)
    arr[j % 16, j // 16] = v
    arr[16 + j % 16, j // 16] = v
    return arr


def _route(topk_idx, topk_weight):
    ti = np.asarray(topk_idx).reshape(-1).astype(np.int64)
    tw = np.asarray(topk_weight).reshape(-1).astype(np.float32)
    nreg = NTOK * 4
    reg_e, reg_w = ti[:nreg], tw[:nreg]
    reg_t = np.repeat(np.arange(NTOK), 4)
    ext_e, ext_w = ti[nreg:], tw[nreg:]

    experts = []
    for e in range(NE):
        m = reg_e == e
        toks = reg_t[m]
        ws = reg_w[m]
        rows = [(NTOK - 1, 0.0, True)]
        rows += list(zip(toks.tolist(), ws.tolist(), [False] * len(toks)))
        groups, group_toks = [], []
        for r in rows:
            t = r[0]
            for ci in range(len(groups)):
                if len(groups[ci]) < 256 and t not in group_toks[ci]:
                    groups[ci].append(r)
                    group_toks[ci].add(t)
                    break
            else:
                groups.append([r])
                group_toks.append({t})
        chunks = []
        for g in groups:
            chunks.append(g[0:128])
            chunks.append(g[128:256])
        experts.append(chunks)
    extw = [ext_w[ext_e == e] for e in range(NE)]
    return experts, extw


def _plan(experts, extw):
    nch = np.array([len(c) for c in experts])
    order = np.argsort(-nch, kind="stable")
    expert_of = [[int(order[s * NCORES + c]) for s in range(NSLOT)]
                 for c in range(NCORES)]
    RS = [int(max(nch[order[s * NCORES + c]] for c in range(NCORES)))
          for s in range(NSLOT)]
    EWC = max(1, -(-int(max(len(w) for w in extw)) // 128))
    NCH = sum(RS)

    batches = []
    base = 0
    for s in range(NSLOT):
        c0 = 0
        while c0 < RS[s]:
            nb = min(4, RS[s] - c0)
            batches.append((s, base + c0, nb))
            c0 += nb
        base += RS[s]

    cores = []
    spec_pos = None
    for c in range(NCORES):
        gi = np.zeros(NCH * 128, np.int64)
        si = np.zeros(NCH * 128, np.int64)
        wr = np.zeros((128, NCH), np.float32)
        sp = []
        gch = 0
        for s in range(NSLOT):
            e = expert_of[c][s]
            chunks = experts[e]
            for lc in range(RS[s]):
                rows = chunks[lc] if lc < len(chunks) else []
                for j in range(128):
                    p = gch * 128 + j
                    if j < len(rows):
                        t, w, spc = rows[j]
                        gi[p] = t
                        si[p] = t
                        wr[j, gch] = w
                        if spc:
                            sp.append((gch, j))
                    else:
                        gi[p] = 0
                        si[p] = NTOK + j
                gch += 1
        assert len(sp) == NSLOT
        ew = np.zeros((128, NSLOT * EWC), np.float32)
        for s in range(NSLOT):
            e = expert_of[c][s]
            assert sp[s][1] == 0
            vals = extw[e]
            tmp = np.zeros(128 * EWC, np.float32)
            tmp[:len(vals)] = vals
            ew[:, s * EWC:(s + 1) * EWC] = tmp.reshape(EWC, 128).T
        if spec_pos is None:
            spec_pos = sp
        else:
            assert sp == spec_pos, "special positions must be core-uniform"
        cores.append(dict(expert_of=expert_of[c],
                          gidx=_pack_idx(gi, NCH * 8),
                          sidx=_pack_idx(si, NCH * 8),
                          wrow=wr, extw=ew))
    return dict(RS=RS, NCH=NCH, EWC=EWC, batches=batches, spec_pos=spec_pos), cores



class _Sem:
    __slots__ = ("h", "n", "name")

    def __init__(self, h, name):
        self.h = h
        self.n = 0
        self.name = name

    @property
    def done(self):
        return (self, self.n)


class _Builder:
    def __init__(self):
        self.ops = {e: [] for e in ENGINES}

    def op(self, eng, fn, waits=(), incs=()):
        for s, k in incs:
            s.n += k
        self.ops[eng].append((tuple(w for w in waits if w is not None),
                              fn, tuple(incs)))

    def emit(self, block):
        for eng in ENGINES:
            oplist = self.ops[eng]

            def body(e, _ops=oplist):
                for waits, fn, incs in _ops:
                    for s, v in waits:
                        if v > 0:
                            e.wait_ge(s.h, v)
                    ins = fn(e)
                    for s, k in incs:
                        ins.then_inc(s.h, k)
            getattr(block, eng)(body)


class _Span:
    def __init__(self, t, cols):
        self.t = t
        self.cols = cols
        self.pos = 0

    def take(self, n):
        assert self.pos + n <= self.cols, f"arena overflow {self.pos}+{n}"
        o = self.pos
        self.pos += n
        return o

    def mark(self):
        return self.pos

    def reset(self, pos):
        self.pos = pos



def _build(meta):
    RS = meta["RS"]
    NCH = meta["NCH"]
    EWC = meta["EWC"]
    batches = meta["batches"]
    spec_pos = meta["spec_pos"]
    RSMAX = max(RS)

    nc = bass.Bass("TRN2", target_bir_lowering=False, num_devices=NCORES)

    def din(name, shape, dt):
        return nc.dram_tensor(name, list(shape), dt, kind="ExternalInput").ap()

    x_sl = din("x_sl", (SL, D), F32)
    wqT = din("wqT", (D, H * DH), F32)
    wkvT = din("wkvT", (D, 2 * H * DH), F32)
    wgaT = din("wgaT", (D, H), F32)
    woT = din("woT", (H * DH, D), F32)
    bgr = din("bgr", (1, H), F32)
    qsc = din("qsc", (128, 1), F32)
    ksc = din("ksc", (128, 1), F32)
    maskt = din("maskt", (5, 128, 640), F32)
    identb = din("identb", (128, 128), BF16)
    onesc = din("onesc", (128, 1), BF16)
    onesr = din("onesr", (1, 128), BF16)
    wgS = din("wgS", (NSLOT, D, 1024), F32)
    wuS = din("wuS", (NSLOT, D, 1024), F32)
    wdS = din("wdS", (NSLOT, 1024, D), F32)
    gidx = din("gidx", (128, NCH * 8), I16)
    sidx = din("sidx", (128, NCH * 8), I16)
    wrow = din("wrow", (128, NCH), F32)
    extw = din("extw", (128, NSLOT * EWC), F32)

    out = nc.dram_tensor("out", [TPC, D], F32, kind="ExternalOutput").ap()
    dbg = {}
    if DEBUG_DUMPS:
        for nm, shp in (("d_xn", [SL, D]), ("d_ag", [NTOK, D]),
                        ("d_s", [TPC, D]), ("d_ym", [NTOK, D]),
                        ("d_yr", [TPC, D])):
            dbg[nm] = nc.dram_tensor(nm, shp, BF16, kind="ExternalOutput").ap()

    xn_loc = nc.dram_tensor("xn_loc", [SL, D], BF16).ap()
    ag_in = nc.dram_tensor("ag_in", [TPC, D], BF16).ap()
    ag_out = nc.dram_tensor("ag_out", [NTOK, D], BF16).ap()
    y_full = nc.dram_tensor("y_full", [YROWS, D], BF16).ap()
    y_main = nc.dram_tensor("y_main", [NTOK, D], BF16).ap()
    y_red = nc.dram_tensor("y_red", [TPC, D], BF16).ap()
    s_spill = nc.dram_tensor("s_spill", [TPC, D], BF16).ap()

    from contextlib import ExitStack
    stack = ExitStack()
    with stack:
        def sem(name):
            return _Sem(stack.enter_context(nc.semaphore(name)), name)

        def sbuf(name, shape, dt):
            return stack.enter_context(nc.sbuf_tensor(name, list(shape), dt))

        def psum(name, shape, dt):
            return stack.enter_context(nc.psum_tensor(name, list(shape), dt))

        LD = sem("LD")
        WSM = sem("WSM")
        GSM = sem("GSM")
        SSM = sem("SSM")
        CC = sem("CC")
        TEN = sem("TEN")
        VEC = sem("VEC")
        SCA = sem("SCA")
        XLW = sem("XLW")
        AGW = sem("AGW")
        TRS = sem("TRS")
        SSP = sem("SSP")
        OS = sem("OS")

        ident_sb = sbuf("ident_sb", (128, 128), BF16)
        ones128_sb = sbuf("ones128_sb", (128, 128), BF16)
        onesc_sb = sbuf("onesc_sb", (128, 1), BF16)
        onesr_sb = sbuf("onesr_sb", (1, 128), BF16)
        bgr_sb = sbuf("bgr_sb", (1, H), BF16)
        wga_sb = sbuf("wga_sb", (128, 16 * H), BF16)
        ccol_sb = sbuf("ccol_sb", (128, 1), F32)
        sc1_sb = sbuf("sc1_sb", (128, 4), F32)
        gidx_sb = sbuf("gidx_sb", (128, NCH * 8), I16)
        sidx_sb = sbuf("sidx_sb", (128, NCH * 8), I16)
        wrow_sb = sbuf("wrow_sb", (128, NCH), F32)
        cext_sb = sbuf("cext_sb", (128, NSLOT), F32)
        extw_sb = sbuf("extw_sb", (128, NSLOT * EWC), F32)
        extb_sb = sbuf("extb_sb", (128, NSLOT * EWC), BF16)
        qsc_sb = sbuf("qsc_sb", (128, 1), F32)
        ksc_sb = sbuf("ksc_sb", (128, 1), F32)

        ABF_COLS = 88100
        AF32_COLS = 6700
        abf_t = sbuf("abf", (128, ABF_COLS), BF16)
        af32_t = sbuf("af32", (128, AF32_COLS), F32)
        abf = _Span(abf_t, ABF_COLS)
        af32 = _Span(af32_t, AF32_COLS)

        psA = psum("psA", (128, 512), F32)
        psB = psum("psB", (128, 512), F32)
        psC = psum("psC", (128, 512), F32)
        psD = psum("psD", (128, 512), F32)
        psE = psum("psE", (128, 640), F32)
        psF1 = psum("psF1", (128, 128), BF16)
        psF2 = psum("psF2", (128, 128), BF16)

        b = _Builder()

        for dst, src in ((ident_sb, identb), (onesc_sb, onesc),
                         (onesr_sb, onesr), (gidx_sb, gidx), (sidx_sb, sidx),
                         (wrow_sb, wrow), (extw_sb, extw), (qsc_sb, qsc),
                         (ksc_sb, ksc)):
            b.op("sync", lambda e, d=dst, s=src: e.dma_start(out=d[:, :], in_=s[:, :]),
                 incs=[(LD, 16)])
        ph0_ld = LD.done

        b.op("gpsimd", lambda e: e.load_library(_mlp_lib))
        b.op("gpsimd", lambda e: e.dma_start(out=bgr_sb[:, :], in_=bgr[:, :]),
             incs=[(WSM, 16)])
        b.op("gpsimd", lambda e: e.dma_start(
            out=wga_sb[:, :].rearrange("p (c m) -> p c m", m=H),
            in_=wgaT.rearrange("(c p) m -> p c m", p=128)),
            incs=[(WSM, 16)])
        wga_done = WSM.done

        wreg_o = abf.take(8192)
        reg2_o = abf.take(6144)
        xnT_o = abf.take(16 * SL)
        q_o = abf.take(H * TPC)
        k_o = abf.take(H * SL)
        v_o = abf.take(12 * 1024)
        o_o = abf.take(H * TPC)

        wreg = abf_t[:, wreg_o:wreg_o + 8192]
        zz = wreg
        xnT = [abf_t[:, xnT_o + hc * SL: xnT_o + (hc + 1) * SL] for hc in range(16)]
        q_sb = abf_t[:, q_o:q_o + H * TPC]
        k_sb = abf_t[:, k_o:k_o + H * SL]
        v_sb = abf_t[:, v_o:v_o + 12 * 1024]
        o_sb = abf_t[:, o_o:o_o + H * TPC]

        xn2 = [abf_t[:, reg2_o + i * D: reg2_o + (i + 1) * D] for i in range(2)]
        squt = abf_t[:, reg2_o + 2 * D: reg2_o + 3 * D]
        qsq = [abf_t[:, reg2_o + i * 512: reg2_o + (i + 1) * 512] for i in range(2)]
        ksq = [abf_t[:, reg2_o + 1024 + i * 512: reg2_o + 1024 + (i + 1) * 512]
               for i in range(2)]
        bcT = abf_t[:, reg2_o + 2048: reg2_o + 2176]
        attbf = abf_t[:, reg2_o + 2176: reg2_o + 2816]
        attT = abf_t[:, reg2_o + 2816: reg2_o + 3456]
        s_half = [abf_t[:, reg2_o + 3456 + i * 1024: reg2_o + 3456 + (i + 1) * 1024]
                  for i in range(2)]

        x2_o = af32.take(2 * D)
        lst_o = af32.take(16)
        rq_o = af32.take(64)
        gt_o = af32.take(64)
        rk_o = af32.take(96)
        ms_o = af32.take(640)
        atf_o = af32.take(640)
        st_o = af32.take(16)
        sil_o = af32.take(1024)
        x2 = [af32_t[:, x2_o + i * D: x2_o + (i + 1) * D] for i in range(2)]
        lst = af32_t[:, lst_o:lst_o + 16]
        rq_tok = af32_t[:, rq_o:rq_o + 64]
        gate_tok = af32_t[:, gt_o:gt_o + 64]
        rk_tok = af32_t[:, rk_o:rk_o + 96]
        mask_sb = af32_t[:, x2_o:x2_o + 5 * 640]
        ms = af32_t[:, ms_o:ms_o + 640]
        attf = af32_t[:, atf_o:atf_o + 640]
        stt = af32_t[:, st_o:st_o + 16]
        sil = [af32_t[:, sil_o:sil_o + 512], af32_t[:, sil_o + 512:sil_o + 1024]]

        def _ones(e):
            e.memset(zz, 0.0)
            return e.memset(ones128_sb[:, :], 1.0)
        b.op("vector", _ones, incs=[(VEC, 1)])
        zz_done = VEC.done

        b.op("vector", lambda e: e.tensor_mul(sc1_sb[:, 0:1], qsc_sb[:, :],
                                              ksc_sb[:, :]),
             waits=[ph0_ld], incs=[(VEC, 1)])
        b.op("vector", lambda e: e.tensor_scalar_mul(ccol_sb[:, :],
                                                     sc1_sb[:, 0:1], 8.0),
             waits=[VEC.done], incs=[(VEC, 1)])
        ccol_done = VEC.done

        b.op("vector", lambda e: e.tensor_copy(extb_sb[:, :], extw_sb[:, :]),
             waits=[ph0_ld], incs=[(VEC, 1)])
        extb_done = VEC.done

        def _extmm(e):
            last = None
            for s in range(NSLOT):
                last = e.matmul(psC[0:1, s * EWC:(s + 1) * EWC],
                                onesc_sb[:, :],
                                extb_sb[:, s * EWC:(s + 1) * EWC])
            return last
        b.op("tensor", _extmm, waits=[extb_done], incs=[(TEN, 1)])
        extmm_done = TEN.done

        def _extred(e):
            last = None
            for s in range(NSLOT):
                last = e.reduce_sum(cext_sb[0:1, s:s + 1],
                                    psC[0:1, s * EWC:(s + 1) * EWC], axis=AX.X)
            return last
        b.op("vector", _extred, waits=[extmm_done], incs=[(VEC, 1)])

        def _extcp(e):
            last = None
            for s in range(NSLOT):
                gch, p = spec_pos[s]
                last = e.tensor_copy(wrow_sb[0:1, gch:gch + 1],
                                     cext_sb[0:1, s:s + 1])
            return last
        b.op("vector", _extcp, waits=[VEC.done], incs=[(VEC, 1)])
        wfill_done = VEC.done

        yflat = y_full.rearrange("a m -> (a m)")

        def _yz(e):
            for ci in range(16):
                e.dma_start(
                    out=yflat[ci * 128 * 8192:(ci + 1) * 128 * 8192]
                    .rearrange("(p m) -> p m", p=128),
                    in_=zz).then_inc(LD.h, 16)
            return e.dma_start(
                out=yflat[16 * 128 * 8192:].rearrange("(p m) -> p m", p=128),
                in_=zz[:, 0:2048])
        LD.n += 16 * 16
        b.op("sync", _yz, waits=[zz_done], incs=[(LD, 16)])
        yzero_done = LD.done

        xld, xvec, xsca, xwr = [None] * NT, [None] * NT, [None] * NT, [None] * NT
        for i in range(NT):
            waits = [xvec[i - 2], xsca[i - 2]] if i >= 2 else []
            b.op("sync", lambda e, i=i: e.dma_start(
                out=x2[i % 2], in_=x_sl[i * 128:(i + 1) * 128, :]),
                waits=waits, incs=[(LD, 16)])
            xld[i] = LD.done

            sqc = 1 + 8 * (i % 2)
            b.op("scalar", lambda e, i=i, sqc=sqc: e.activation(
                squt, x2[i % 2], AF.Square, accum_out=lst[:, sqc:sqc + 1]),
                waits=[xld[i]], incs=[(SCA, 1)])
            ssq_done = SCA.done

            wv = [xld[i], ssq_done]
            if i >= 2:
                wv.append(xwr[i - 2])
            b.op("vector", lambda e, i=i: e.reduce_sum(lst[:, 0:1], x2[i % 2],
                                                       axis=AX.X),
                 waits=wv, incs=[(VEC, 1)])
            b.op("vector", lambda e: e.tensor_scalar_mul(lst[:, 2:3],
                                                         lst[:, 0:1], 1.0 / D),
                 waits=[VEC.done], incs=[(VEC, 1)])
            b.op("vector", lambda e, sqc=sqc: e.tensor_scalar_mul(
                lst[:, 3:4], lst[:, sqc:sqc + 1], 1.0 / D),
                waits=[VEC.done], incs=[(VEC, 1)])
            b.op("vector", lambda e: e.tensor_mul(lst[:, 4:5], lst[:, 2:3],
                                                  lst[:, 2:3]),
                 waits=[VEC.done], incs=[(VEC, 1)])
            b.op("vector", lambda e: e.tensor_sub(lst[:, 5:6], lst[:, 3:4],
                                                  lst[:, 4:5]),
                 waits=[VEC.done], incs=[(VEC, 1)])
            var_done = VEC.done

            b.op("vector", lambda e: e.tensor_scalar_add(
                lst[:, 8:9], lst[:, 5:6], 1e-5),
                waits=[var_done], incs=[(VEC, 1)])
            epsd = VEC.done
            b.op("scalar", lambda e: e.activation(
                lst[:, 6:7], lst[:, 8:9], AF.Sqrt),
                waits=[epsd], incs=[(SCA, 1)])
            xsca[i] = SCA.done

            b.op("vector", lambda e: e.reciprocal(lst[:, 7:8], lst[:, 6:7]),
                 waits=[xsca[i]], incs=[(VEC, 1)])
            rstd_done = VEC.done
            b.op("vector", lambda e, i=i: e.tensor_scalar(
                xn2[i % 2], x2[i % 2], lst[:, 2:3], lst[:, 7:8],
                op0=ALU.subtract, op1=ALU.mult),
                waits=[rstd_done], incs=[(VEC, 1)])
            xvec[i] = VEC.done

            def _lnw(e, i=i):
                ins = e.dma_start(out=xn_loc[i * 128:(i + 1) * 128, :],
                                  in_=xn2[i % 2])
                ins.then_inc(XLW.h, 16)
                if i >= 4:
                    ins = e.dma_start(out=ag_in[(i - 4) * 128:(i - 3) * 128, :],
                                      in_=xn2[i % 2])
                    ins.then_inc(AGW.h, 16)
                return ins
            XLW.n += 16
            if i >= 4:
                AGW.n += 16
            b.op("sync", _lnw, waits=[xvec[i]])
            xwr[i] = (XLW, XLW.n)
        ln_done = (VEC, VEC.n)
        ln_sca_done = (SCA, SCA.n)

        def _trd(e):
            last = None
            for hc in range(16):
                last = e.dma_start(out=xnT[hc],
                                   in_=xn_loc[:, hc * 128:(hc + 1) * 128],
                                   transpose=True)
                last.then_inc(TRS.h, 16)
            return last
        TRS.n += 16 * 16
        b.op("sync", _trd, waits=[(XLW, XLW.n)])
        xnT_done = TRS.done

        b.op("gpsimd", lambda e: e.collective_compute(
            "AllGather", ALU.bypass, replica_groups=[list(range(NCORES))],
            ins=[ag_in.opt()], outs=[ag_out.opt()]),
            waits=[(AGW, AGW.n)], incs=[(CC, 1)])
        ag_done = CC.done

        if SKIP_ATTN:
            s_done = (SSP, 0)
            attn_skip = True
        else:
            attn_skip = False
        if not attn_skip:
            prev_half_last = None
            ev_q = {}
            for half in range(2):
                wsrc = wqT.rearrange("(c p) m -> p c m", p=128)[:, :, half * 512:(half + 1) * 512]
                wts = [xnT_done, yzero_done] if half == 0 else [prev_half_last]
                b.op("gpsimd", lambda e, s=wsrc: e.dma_start(
                    out=wreg[:, :].rearrange("p (c m) -> p c m", m=512), in_=s),
                    waits=wts, incs=[(WSM, 16)])
                wq_ld = WSM.done
                for hh in range(4):
                    h = half * 4 + hh
                    for bb in range(2):
                        u = h * 2 + bb
                        ps = psA if u % 2 == 0 else psB

                        def _qmm(e, hh=hh, bb=bb, ps=ps):
                            last = None
                            for hc in range(16):
                                last = e.matmul(
                                    ps[:, :],
                                    wreg[:, hc * 512 + hh * 128: hc * 512 + (hh + 1) * 128],
                                    xnT[hc][:, WIN + bb * 512: WIN + (bb + 1) * 512],
                                    start=(hc == 0), stop=(hc == 15))
                            return last
                        wts = [wq_ld, ccol_done]
                        if u >= 2:
                            wts.append(ev_q[u - 2])
                        b.op("tensor", _qmm, waits=wts, incs=[(TEN, 1)])
                        mm_done = TEN.done

                        b.op("vector", lambda e, h=h, bb=bb, ps=ps: e.tensor_scalar_mul(
                            q_sb[:, h * TPC + bb * 512: h * TPC + (bb + 1) * 512],
                            ps[:, :], ccol_sb[:, :]),
                            waits=[mm_done], incs=[(VEC, 1)])
                        evd = VEC.done

                        b.op("scalar", lambda e, bb=bb, ps=ps: e.activation(
                            qsq[bb], ps[:, :], AF.Square),
                            waits=[evd], incs=[(SCA, 1)])
                        sqd = SCA.done

                        def _ssmm(e, bb=bb):
                            last = None
                            for j in range(4):
                                last = e.matmul(psC[:, j:j + 1],
                                                qsq[bb][:, j * 128:(j + 1) * 128],
                                                onesc_sb[:, :])
                            return last
                        b.op("tensor", _ssmm, waits=[sqd], incs=[(TEN, 1)])
                        ssd = TEN.done

                        def _ssev(e, h=h, bb=bb):
                            last = None
                            for j in range(4):
                                t = bb * 4 + j
                                last = e.tensor_copy(
                                    rq_tok[:, t * 8 + h: t * 8 + h + 1],
                                    psC[:, j:j + 1])
                            return last
                        b.op("vector", _ssev, waits=[ssd], incs=[(VEC, 1)])
                        ev_q[u] = VEC.done
                prev_half_last = ev_q[half * 8 + 7]

            b.op("scalar", lambda e: e.activation(rq_tok[:, :], rq_tok[:, :], AF.Sqrt),
                 waits=[ev_q[15]], incs=[(SCA, 1)])

            b.op("vector", lambda e: e.tensor_scalar_max(
                rq_tok[:, :], rq_tok[:, :], 1e-12),
                waits=[SCA.done], incs=[(VEC, 1)])
            b.op("vector", lambda e: e.reciprocal(rq_tok[:, :], rq_tok[:, :]),
                 waits=[VEC.done], incs=[(VEC, 1)])
            rq_done = VEC.done

            prev_gate = None
            for t in range(8):
                def _gmm(e, t=t):
                    for hc in range(16):
                        e.matmul(psD[:, 0:8],
                                 xnT[hc][:, WIN + t * 128: WIN + (t + 1) * 128],
                                 wga_sb[:, hc * H:(hc + 1) * H],
                                 start=(hc == 0), stop=False)
                    return e.matmul(psD[:, 0:8], onesr_sb[:, :], bgr_sb[:, :],
                                    start=False, stop=True)
                wts = [xnT_done, wga_done, ph0_ld]
                if prev_gate is not None:
                    wts.append(prev_gate)
                b.op("tensor", _gmm, waits=wts, incs=[(TEN, 1)])
                b.op("scalar", lambda e, t=t: e.activation(
                    gate_tok[:, t * 8:(t + 1) * 8], psD[:, 0:8], AF.Sigmoid),
                    waits=[TEN.done], incs=[(SCA, 1)])
                prev_gate = SCA.done
            gates_done = prev_gate

            ev_k = {}
            for half in range(2):
                wsrc = wkvT.rearrange("(c p) m -> p c m", p=128)[:, :, half * 512:(half + 1) * 512]
                b.op("gpsimd", lambda e, s=wsrc: e.dma_start(
                    out=wreg[:, :].rearrange("p (c m) -> p c m", m=512), in_=s),
                    waits=[ev_q[15] if half == 0 else ev_k[11]],
                    incs=[(WSM, 16)])
                wk_ld = WSM.done
                for hh in range(4):
                    h = half * 4 + hh
                    for b3 in range(3):
                        u = h * 3 + b3
                        ps = psA if u % 2 == 0 else psB

                        def _kmm(e, hh=hh, b3=b3, ps=ps):
                            last = None
                            for hc in range(16):
                                last = e.matmul(
                                    ps[:, :],
                                    wreg[:, hc * 512 + hh * 128: hc * 512 + (hh + 1) * 128],
                                    xnT[hc][:, b3 * 512:(b3 + 1) * 512],
                                    start=(hc == 0), stop=(hc == 15))
                            return last
                        wts = [wk_ld]
                        if u >= 2:
                            wts.append(ev_k[u - 2])
                        b.op("tensor", _kmm, waits=wts, incs=[(TEN, 1)])
                        mm_done = TEN.done

                        b.op("vector", lambda e, h=h, b3=b3, ps=ps: e.tensor_copy(
                            k_sb[:, h * SL + b3 * 512: h * SL + (b3 + 1) * 512],
                            ps[:, :]),
                            waits=[mm_done], incs=[(VEC, 1)])
                        evd = VEC.done

                        b.op("scalar", lambda e, b3=b3, ps=ps: e.activation(
                            ksq[b3 % 2], ps[:, :], AF.Square),
                            waits=[evd], incs=[(SCA, 1)])
                        sqd = SCA.done

                        def _ssmm(e, b3=b3):
                            last = None
                            for j in range(4):
                                last = e.matmul(psC[:, j:j + 1],
                                                ksq[b3 % 2][:, j * 128:(j + 1) * 128],
                                                onesc_sb[:, :])
                            return last
                        b.op("tensor", _ssmm, waits=[sqd], incs=[(TEN, 1)])
                        ssd = TEN.done

                        def _ssev(e, h=h, b3=b3):
                            last = None
                            for j in range(4):
                                ch = b3 * 4 + j
                                last = e.tensor_copy(
                                    rk_tok[:, ch * 8 + h: ch * 8 + h + 1],
                                    psC[:, j:j + 1])
                            return last
                        b.op("vector", _ssev, waits=[ssd], incs=[(VEC, 1)])
                        ev_k[u] = VEC.done
            k_done = ev_k[23]

            ev_v = {}
            for half in range(2):
                wsrc = wkvT.rearrange("(c p) m -> p c m", p=128)[:, :, 1024 + half * 512: 1024 + (half + 1) * 512]
                b.op("gpsimd", lambda e, s=wsrc: e.dma_start(
                    out=wreg[:, :].rearrange("p (c m) -> p c m", m=512), in_=s),
                    waits=[k_done if half == 0 else ev_v[11]],
                    incs=[(WSM, 16)])
                wv_ld = WSM.done
                for ch in range(12):
                    u = half * 12 + ch
                    ps = psA if u % 2 == 0 else psB

                    def _vmm(e, ch=ch, ps=ps):
                        last = None
                        for hc in range(16):
                            last = e.matmul(
                                ps[:, :],
                                xnT[hc][:, ch * 128:(ch + 1) * 128],
                                wreg[:, hc * 512:(hc + 1) * 512],
                                start=(hc == 0), stop=(hc == 15))
                        return last
                    wts = [wv_ld]
                    if u >= 2:
                        wts.append(ev_v[u - 2])
                    b.op("tensor", _vmm, waits=wts, incs=[(TEN, 1)])
                    b.op("vector", lambda e, ch=ch, half=half, ps=ps: e.tensor_copy(
                        v_sb[:, ch * 1024 + half * 512: ch * 1024 + (half + 1) * 512],
                        ps[:, :]),
                        waits=[TEN.done], incs=[(VEC, 1)])
                    ev_v[u] = VEC.done
            v_done = ev_v[23]

            b.op("scalar", lambda e: e.activation(rk_tok[:, :], rk_tok[:, :], AF.Sqrt),
                 waits=[k_done], incs=[(SCA, 1)])

            b.op("vector", lambda e: e.tensor_scalar_max(
                rk_tok[:, :], rk_tok[:, :], 1e-12),
                waits=[SCA.done], incs=[(VEC, 1)])
            b.op("vector", lambda e: e.reciprocal(rk_tok[:, :], rk_tok[:, :]),
                 waits=[VEC.done], incs=[(VEC, 1)])
            rk_done = VEC.done

            prev_kh = {}
            for h in range(H):
                for ch in range(12):
                    u = h * 12 + ch
                    psf = psF1 if u % 2 == 0 else psF2

                    wts = [rk_done, zz_done]
                    if u >= 1:
                        wts.append(prev_kh[u - 1][0])
                    b.op("vector", lambda e, h=h, ch=ch: e.tensor_scalar_mul(
                        bcT, ones128_sb[:, :],
                        rk_tok[:, (ch * 8 + h):(ch * 8 + h) + 1]),
                        waits=wts, incs=[(VEC, 1)])
                    bct_done = VEC.done

                    wts = [bct_done, ph0_ld]
                    if u >= 2:
                        wts.append(prev_kh[u - 2][1])
                    if u == 0:
                        wts.append(gates_done)
                    b.op("tensor", lambda e, psf=psf: e.transpose(
                        psf[:, :], bcT, ident_sb[:, :]),
                        waits=wts, incs=[(TEN, 1)])
                    tr_done = TEN.done

                    b.op("vector", lambda e, h=h, ch=ch, psf=psf: e.tensor_mul(
                        k_sb[:, h * SL + ch * 128: h * SL + (ch + 1) * 128],
                        k_sb[:, h * SL + ch * 128: h * SL + (ch + 1) * 128],
                        psf[:, :]),
                        waits=[tr_done], incs=[(VEC, 1)])
                    prev_kh[u] = (tr_done, VEC.done)
            kh_done = prev_kh[H * 12 - 1][1]

            b.op("sync", lambda e: e.dma_start(
                out=mask_sb[:, :].rearrange("p (c m) -> p c m", m=640),
                in_=maskt.rearrange("c p m -> p c m")),
                waits=[ln_done, ln_sca_done], incs=[(LD, 16)])
            mask_done = LD.done

            prev_mask_add = None
            prev_exp = None
            prev_oev = None
            prev_t_ev = {}
            for h in range(H):
                for t in range(8):
                    u = h * 8 + t

                    def _simm(e, h=h, t=t):
                        e.matmul(psE[:, 0:512],
                                 q_sb[:, h * TPC + t * 128: h * TPC + (t + 1) * 128],
                                 k_sb[:, h * SL + t * 128: h * SL + t * 128 + 512],
                                 start=True, stop=True)
                        return e.matmul(
                            psE[:, 512:640],
                            q_sb[:, h * TPC + t * 128: h * TPC + (t + 1) * 128],
                            k_sb[:, h * SL + t * 128 + 512: h * SL + t * 128 + 640],
                            start=True, stop=True)
                    wts = [kh_done, mask_done]
                    if prev_mask_add is not None:
                        wts.append(prev_mask_add)
                    b.op("tensor", _simm, waits=wts, incs=[(TEN, 1)])
                    sim_done = TEN.done

                    mslot = t if t < 4 else 4

                    wts = [sim_done, rq_done]
                    if prev_exp is not None:
                        wts.append(prev_exp)
                    b.op("vector", lambda e, mslot=mslot: e.tensor_add(
                        ms, psE[:, :],
                        mask_sb[:, mslot * 640:(mslot + 1) * 640]),
                        waits=wts, incs=[(VEC, 1)])
                    b.op("vector", lambda e: e.reduce_max(
                        stt[:, 0:1], ms, axis=AX.X, negate=True),
                        waits=[VEC.done], incs=[(VEC, 1)])
                    b.op("vector", lambda e, h=h, t=t: e.tensor_mul(
                        stt[:, 1:2], stt[:, 0:1],
                        rq_tok[:, t * 8 + h: t * 8 + h + 1]),
                        waits=[VEC.done], incs=[(VEC, 1)])
                    prev_mask_add = VEC.done

                    b.op("scalar", lambda e, h=h, t=t: e.activation(
                        attf, ms, AF.Exp, bias=stt[:, 1:2],
                        scale=rq_tok[:, t * 8 + h: t * 8 + h + 1],
                        accum_out=stt[:, 2:3]),
                        waits=[prev_mask_add], incs=[(SCA, 1)])
                    prev_exp = SCA.done
                    exp_done = prev_exp

                    b.op("vector", lambda e: e.reciprocal(stt[:, 3:4],
                                                          stt[:, 2:3]),
                         waits=[exp_done], incs=[(VEC, 1)])
                    b.op("vector", lambda e, h=h, t=t: e.tensor_mul(
                        stt[:, 4:5], stt[:, 3:4],
                        gate_tok[:, t * 8 + h: t * 8 + h + 1]),
                        waits=[VEC.done], incs=[(VEC, 1)])
                    scl1_done = VEC.done
                    b.op("vector", lambda e: e.tensor_scalar_mul(
                        attbf, attf, stt[:, 4:5]),
                        waits=[scl1_done], incs=[(VEC, 1)])
                    attbf_done = VEC.done

                    tev = {}
                    for kk in range(5):
                        psf = psF1 if kk % 2 == 0 else psF2
                        wts = [attbf_done]
                        if kk >= 2:
                            wts.append(tev[kk - 2])
                        elif u > 0:
                            wts.append(prev_t_ev[(u - 1, 4 if kk == 0 else 3)])
                        b.op("tensor", lambda e, kk=kk, psf=psf: e.transpose(
                            psf[:, :], attbf[:, kk * 128:(kk + 1) * 128],
                            ident_sb[:, :]),
                            waits=wts, incs=[(TEN, 1)])
                        b.op("vector", lambda e, kk=kk, psf=psf: e.tensor_copy(
                            attT[:, kk * 128:(kk + 1) * 128], psf[:, :]),
                            waits=[TEN.done], incs=[(VEC, 1)])
                        tev[kk] = VEC.done
                        prev_t_ev[(u, kk)] = VEC.done

                    def _avmm(e, h=h, t=t):
                        last = None
                        for kk in range(5):
                            last = e.matmul(
                                psD[:, 0:128],
                                v_sb[:, (t + kk) * 1024 + h * 128:
                                     (t + kk) * 1024 + (h + 1) * 128],
                                attT[:, kk * 128:(kk + 1) * 128],
                                start=(kk == 0), stop=(kk == 4))
                        return last
                    wts = [tev[4], v_done]
                    if u == 0:
                        wts.append(gates_done)
                    if prev_oev is not None:
                        wts.append(prev_oev)
                    b.op("tensor", _avmm, waits=wts, incs=[(TEN, 1)])
                    b.op("vector", lambda e, h=h, t=t: e.tensor_copy(
                        o_sb[:, h * TPC + t * 128: h * TPC + (t + 1) * 128],
                        psD[:, 0:128]),
                        waits=[TEN.done], incs=[(VEC, 1)])
                    prev_oev = VEC.done
            attn_done = prev_oev

            prev_sev = {}
            for half in range(2):
                wsrc = woT.rearrange("(c p) m -> p c m", p=128)[:, :, half * 1024:(half + 1) * 1024]
                b.op("gpsimd", lambda e, s=wsrc: e.dma_start(
                    out=wreg[:, :].rearrange("p (c m) -> p c m", m=1024), in_=s),
                    waits=[v_done if half == 0 else prev_sev[(0, 15)]],
                    incs=[(WSM, 16)])
                wo_ld = WSM.done
                for t in range(8):
                    for dci in range(2):
                        u = t * 2 + dci
                        ps = psA if u % 2 == 0 else psB

                        def _smm(e, t=t, dci=dci, ps=ps):
                            last = None
                            for ic in range(8):
                                last = e.matmul(
                                    ps[:, :],
                                    o_sb[:, ic * TPC + t * 128: ic * TPC + (t + 1) * 128],
                                    wreg[:, ic * 1024 + dci * 512: ic * 1024 + (dci + 1) * 512],
                                    start=(ic == 0), stop=(ic == 7))
                            return last
                        wts = [wo_ld, attn_done]
                        if u >= 2:
                            wts.append(prev_sev[(half, u - 2)])
                        b.op("tensor", _smm, waits=wts, incs=[(TEN, 1)])
                        b.op("vector", lambda e, t=t, dci=dci, ps=ps: e.tensor_copy(
                            s_half[t % 2][:, dci * 512:(dci + 1) * 512], ps[:, :]),
                            waits=[TEN.done, (SSP, max(0, SSP.n - 16))],
                            incs=[(VEC, 1)])
                        prev_sev[(half, u)] = VEC.done
                    b.op("sync", lambda e, t=t, half=half: e.dma_start(
                        out=s_spill[t * 128:(t + 1) * 128,
                                    half * 1024:(half + 1) * 1024],
                        in_=s_half[t % 2]),
                        waits=[prev_sev[(half, t * 2 + 1)]], incs=[(SSP, 16)])
            s_done = SSP.done

        if SKIP_MOE:
            rs_done = (CC, CC.n)
            moe_skip = True
        else:
            moe_skip = False
        abf.reset(0)
        if not moe_skip:
            wg_o = abf.take(16 * 1024)
            wu_o = abf.take(16 * 1024)
            wd_o = abf.take(16 * 1024)
            xs_o = abf.take(2 * 16 * 512)
            mid_o = abf.take(8 * 128 * RSMAX)
            or_o = abf.take(2 * 2 * D)
            wg_sb = abf_t[:, wg_o:wg_o + 16 * 1024]
            wu_sb = abf_t[:, wu_o:wu_o + 16 * 1024]
            wd_sb = abf_t[:, wd_o:wd_o + 16 * 1024]
            xsT = [abf_t[:, xs_o + i * 8192: xs_o + (i + 1) * 8192] for i in range(2)]
            orow = [abf_t[:, or_o + i * 2 * D: or_o + (i + 1) * 2 * D] for i in range(2)]

            p2_bar = [(TEN, TEN.n), (VEC, VEC.n), (SCA, SCA.n), (SSP, SSP.n),
                  (TRS, TRS.n), (LD, LD.n)]

            slot_batches = [[bt for bt in batches if bt[0] == s] for s in range(NSLOT)]
            gather_ten = {}
            prev_scatter = None
            last_gu_mm = None
            last_dn_mm = None
            bid_global = 0
            ch_base = [0]
            for s in range(1, NSLOT):
                ch_base.append(ch_base[-1] + RS[s - 1])

            for s in range(NSLOT):
                wgsrc = wgS[s].rearrange("(c p) m -> p c m", p=128)
                wusrc = wuS[s].rearrange("(c p) m -> p c m", p=128)
                wdsrc = wdS[s].rearrange("(c p) m -> p c m", p=128)
                wts_w = list(p2_bar) if s == 0 else [last_gu_mm]
                b.op("gpsimd", lambda e, src=wgsrc: e.dma_start(
                    out=wg_sb[:, :].rearrange("p (c m) -> p c m", m=1024), in_=src),
                    waits=wts_w, incs=[(WSM, 16)])
                b.op("gpsimd", lambda e, src=wusrc: e.dma_start(
                    out=wu_sb[:, :].rearrange("p (c m) -> p c m", m=1024), in_=src),
                    incs=[(WSM, 16)])
                wgu_ld = WSM.done
                wts_d = [] if s == 0 else [last_dn_mm]
                b.op("gpsimd", lambda e, src=wdsrc: e.dma_start(
                    out=wd_sb[:, :].rearrange("p (c m) -> p c m", m=1024), in_=src),
                    waits=wts_d, incs=[(WSM, 16)])
                wd_ld = WSM.done

                dn_barrier = last_dn_mm
                prev_ev = {}
                for bi, (s_, gch0, nb) in enumerate(slot_batches[s]):
                    nidx = nb * 128
                    buf = bid_global % 2
                    wts_g = [ag_done, ph0_ld]
                    if s == 0 and bi == 0:
                        wts_g += p2_bar
                    if bid_global >= 2:
                        wts_g.append(gather_ten[bid_global - 2])
                    b.op("gpsimd", lambda e, buf=buf, gch0=gch0, nidx=nidx: e.dma_gather(
                        xsT[buf][:, 0:16 * nidx].rearrange("p (f i) -> p f i", f=16),
                        ag_out[:, :],
                        gidx_sb[:, gch0 * 8: gch0 * 8 + (nidx + 15) // 16],
                        nidx, nidx, 2048, transpose=True),
                        waits=wts_g, incs=[(GSM, 16)])
                    g_done = GSM.done

                    lc0 = gch0 - ch_base[s]
                    for ic in range(8):
                        psg = psA if ic % 2 == 0 else psC
                        psu = psB if ic % 2 == 0 else psD

                        def _gumm(e, buf=buf, ic=ic, nidx=nidx, psg=psg, psu=psu):
                            for hc in range(16):
                                e.matmul(psg[:, 0:nidx],
                                         wg_sb[:, hc * 1024 + ic * 128: hc * 1024 + (ic + 1) * 128],
                                         xsT[buf][:, hc * nidx:(hc + 1) * nidx],
                                         start=(hc == 0), stop=(hc == 15))
                            last = None
                            for hc in range(16):
                                last = e.matmul(psu[:, 0:nidx],
                                                wu_sb[:, hc * 1024 + ic * 128: hc * 1024 + (ic + 1) * 128],
                                                xsT[buf][:, hc * nidx:(hc + 1) * nidx],
                                                start=(hc == 0), stop=(hc == 15))
                            return last
                        wts = [g_done, wgu_ld]
                        if ic >= 2:
                            wts.append(prev_ev[(bi, ic - 2)])
                        elif bi > 0:
                            wts.append(prev_ev[(bi - 1, 6 + ic)])
                        b.op("tensor", _gumm, waits=wts, incs=[(TEN, 1)])
                        gu_done = TEN.done
                        if ic == 7:
                            gather_ten[bid_global] = gu_done
                            last_gu_mm = gu_done

                        sb_ = sil[ic % 2]
                        b.op("scalar", lambda e, nidx=nidx, psg=psg, sb_=sb_: e.activation(
                            sb_[:, 0:nidx], psg[:, 0:nidx], AF.Silu),
                            waits=[gu_done], incs=[(SCA, 1)])
                        sil_done = SCA.done

                        mo = mid_o + ic * 128 * RS[s] + lc0 * 128
                        wts_m = [sil_done]
                        if dn_barrier is not None and bi == 0:
                            wts_m.append(dn_barrier)
                        b.op("vector", lambda e, mo=mo, nidx=nidx, psu=psu, sb_=sb_:
                             e.tensor_mul(abf_t[:, mo:mo + nidx],
                                          sb_[:, 0:nidx], psu[:, 0:nidx]),
                             waits=wts_m, incs=[(VEC, 1)])
                        prev_ev[(bi, ic)] = VEC.done
                    bid_global += 1
                mid_done = prev_ev[(len(slot_batches[s]) - 1, 7)]

                assert RS[s] % 2 == 0
                prev_dn = {}
                for rc in range(RS[s]):
                    gch = ch_base[s] + rc
                    grp = gch // 2
                    ob = orow[grp % 2]
                    oco = (rc % 2) * D
                    for dc in range(4):
                        ps = psA if dc % 2 == 0 else psB

                        def _dmm(e, rc=rc, dc=dc, ps=ps, s=s):
                            last = None
                            for ic in range(8):
                                last = e.matmul(
                                    ps[:, :],
                                    abf_t[:, mid_o + ic * 128 * RS[s] + rc * 128:
                                          mid_o + ic * 128 * RS[s] + (rc + 1) * 128],
                                    wd_sb[:, ic * 2048 + dc * 512: ic * 2048 + (dc + 1) * 512],
                                    start=(ic == 0), stop=(ic == 7))
                            return last
                        wts = [mid_done, wd_ld]
                        if rc > 0 or dc > 1:
                            pk = (rc, dc - 2) if dc >= 2 else (rc - 1, 2 + dc)
                            wts.append(prev_dn[pk])
                        b.op("tensor", _dmm, waits=wts, incs=[(TEN, 1)])
                        dn_done = TEN.done
                        if rc == RS[s] - 1 and dc == 3:
                            last_dn_mm = dn_done

                        wts_e = [dn_done, wfill_done]
                        if dc == 0 and rc % 2 == 0 and grp >= 2:
                            wts_e.append((SSM, SSM.n - 16))
                        b.op("vector", lambda e, gch=gch, dc=dc, ps=ps, ob=ob, oco=oco:
                             e.tensor_scalar_mul(ob[:, oco + dc * 512: oco + (dc + 1) * 512],
                                                 ps[:, :],
                                                 wrow_sb[:, gch:gch + 1]),
                             waits=wts_e, incs=[(VEC, 1)])
                        prev_dn[(rc, dc)] = VEC.done

                    if rc % 2 == 1:
                        wts_s = [prev_dn[(rc - 1, 3)], prev_dn[(rc, 3)], yzero_done]
                        if prev_scatter is not None:
                            wts_s.append(prev_scatter)
                        g0 = gch - 1
                        b.op("gpsimd", lambda e, g0=g0, ob=ob: e.dma_scatter_add(
                            y_full[:, :],
                            ob[:, :].rearrange("p (a m) -> p a m", a=2),
                            sidx_sb[:, g0 * 8:(g0 + 2) * 8], 256, 256, 2048),
                            waits=wts_s, incs=[(SSM, 16)])
                        prev_scatter = SSM.done

            b.op("sync", lambda e: e.dma_start(out=y_main[:, :],
                                               in_=y_full[0:NTOK, :]),
                 waits=[prev_scatter], incs=[(LD, 16)])
            b.op("gpsimd", lambda e: e.collective_compute(
                "ReduceScatter", ALU.add, replica_groups=[list(range(NCORES))],
                ins=[y_main.opt()], outs=[y_red.opt()]),
                waits=[LD.done], incs=[(CC, 1)])
            rs_done = CC.done

        abf.reset(0)
        yb_o = abf.take(2 * D)
        sb_o = abf.take(2 * D)
        yt = [abf_t[:, yb_o + i * D: yb_o + (i + 1) * D] for i in range(2)]
        st2 = [abf_t[:, sb_o + i * D: sb_o + (i + 1) * D] for i in range(2)]
        af32.reset(0)
        xf_o = af32.take(2 * D)
        g_o = af32.take(D)
        xt = [af32_t[:, xf_o + i * D: xf_o + (i + 1) * D] for i in range(2)]
        gsl = af32_t[:, g_o:g_o + D]

        prev_vec5 = {}
        prev_sca5 = {}
        for tt in range(8):
            buf = tt % 2

            def _fl(e, tt=tt, buf=buf):
                e.dma_start(out=yt[buf],
                            in_=y_red[tt * 128:(tt + 1) * 128, :]).then_inc(LD.h, 16)
                e.dma_start(out=st2[buf],
                            in_=s_spill[tt * 128:(tt + 1) * 128, :]).then_inc(LD.h, 16)
                return e.dma_start(out=xt[buf],
                                   in_=x_sl[WIN + tt * 128: WIN + (tt + 1) * 128, :])
            LD.n += 32
            wts = [rs_done, s_done]
            if tt >= 2:
                wts += [prev_vec5[tt - 2], prev_sca5[tt - 2]]
            b.op("sync", _fl, waits=wts, incs=[(LD, 16)])
            fl_done = LD.done

            b.op("scalar", lambda e, buf=buf: e.activation(gsl, yt[buf], AF.Silu),
                 waits=[fl_done] + ([prev_vec5[tt - 1]] if tt >= 1 else []),
                 incs=[(SCA, 1)])
            prev_sca5[tt] = SCA.done

            b.op("vector", lambda e, buf=buf: e.tensor_mul(gsl, gsl, st2[buf]),
                 waits=[prev_sca5[tt], fl_done], incs=[(VEC, 1)])
            b.op("vector", lambda e, buf=buf: e.tensor_add(xt[buf], xt[buf], gsl),
                 waits=[VEC.done], incs=[(VEC, 1)])
            prev_vec5[tt] = VEC.done

            b.op("sync", lambda e, tt=tt, buf=buf: e.dma_start(
                out=out[tt * 128:(tt + 1) * 128, :], in_=xt[buf]),
                waits=[prev_vec5[tt]], incs=[(OS, 16)])

        if DEBUG_DUMPS:
            def _dbg(e):
                e.dma_start(out=dbg["d_xn"][:, :], in_=xn_loc[:, :]).then_inc(OS.h, 16)
                e.dma_start(out=dbg["d_ag"][:, :], in_=ag_out[:, :]).then_inc(OS.h, 16)
                e.dma_start(out=dbg["d_s"][:, :], in_=s_spill[:, :]).then_inc(OS.h, 16)
                e.dma_start(out=dbg["d_ym"][:, :], in_=y_main[:, :]).then_inc(OS.h, 16)
                return e.dma_start(out=dbg["d_yr"][:, :], in_=y_red[:, :])
            OS.n += 4 * 16
            b.op("sync", _dbg, waits=[rs_done, (OS, OS.n - 4 * 16)], incs=[(OS, 16)])
        b.op("sync", lambda e: e.nop(), waits=[(OS, OS.n)])

        with nc.Block() as block:
            b.emit(block)

    return nc



def kernel(**inputs):
    x = np.ascontiguousarray(np.asarray(inputs["L0_attn_x"])[0], np.float32)
    Wq = np.asarray(inputs["L0_attn_Wq"], np.float32)
    Wkv = np.asarray(inputs["L0_attn_Wkv"], np.float32)
    q_scale = np.asarray(inputs["L0_attn_q_scale"], np.float32)
    k_scale = np.asarray(inputs["L0_attn_k_scale"], np.float32)
    Wg = np.asarray(inputs["L0_attn_Wg"], np.float32)
    bg = np.asarray(inputs["L0_attn_bg"], np.float32)
    Wo = np.asarray(inputs["L0_attn_Wo"], np.float32)
    gate_w = np.asarray(inputs["L0_ffn_gate_weight"], np.float32)
    up_w = np.asarray(inputs["L0_ffn_up_weight"], np.float32)
    down_w = np.asarray(inputs["L0_ffn_down_weight"], np.float32)
    topk_idx = np.asarray(inputs["L0_ffn_topk_idx"])
    topk_weight = np.asarray(inputs["L0_ffn_topk_weight"], np.float32)

    experts, extw = _route(topk_idx, topk_weight)
    meta, cores = _plan(experts, extw)
    assert max(meta["RS"]) <= 14, "unexpectedly large expert group"
    assert meta["EWC"] <= 16

    nc = _build(meta)
    mybir.codegen_inst_isa_subclasses(nc)

    wqT = np.ascontiguousarray(Wq.T)
    wkvT = np.ascontiguousarray(Wkv.T)
    wgaT = np.ascontiguousarray(Wg.T)
    woT = np.ascontiguousarray(Wo.T)
    bgr = np.ascontiguousarray(bg.reshape(1, H))
    qsc = np.ascontiguousarray(q_scale.reshape(128, 1))
    ksc = np.ascontiguousarray(k_scale.reshape(128, 1))
    identb = np.eye(128, dtype=BF)
    onesc = np.ones((128, 1), BF)
    onesr = np.ones((1, 128), BF)

    qi = np.arange(128)[:, None]
    jj = np.arange(640)[None, :]
    base_ok = (jj >= qi) & (jj <= qi + WIN)
    masks0 = np.zeros((5, 128, 640), np.float32)
    for t in range(4):
        ok = base_ok & ((t * 128 + jj - WIN) >= 0)
        masks0[t] = np.where(ok, 0.0, MASKVAL)
    masks0[4] = np.where(base_ok, 0.0, MASKVAL)
    masks_other = np.broadcast_to(masks0[4], (5, 128, 640)).copy()
    masks_other = np.ascontiguousarray(masks_other)

    in_maps = []
    for c in range(NCORES):
        lo = c * TPC - WIN
        if lo < 0:
            x_slc = np.concatenate(
                [np.zeros((WIN, D), np.float32), x[0:TPC]], axis=0)
        else:
            x_slc = x[lo:c * TPC + TPC]
        x_slc = np.ascontiguousarray(x_slc)
        eo = cores[c]["expert_of"]
        wgS = np.ascontiguousarray(np.stack([gate_w[e].T for e in eo]))
        wuS = np.ascontiguousarray(np.stack([up_w[e].T for e in eo]))
        wdS = np.ascontiguousarray(np.stack([down_w[e].T for e in eo]))
        in_maps.append({
            "x_sl": x_slc, "wqT": wqT, "wkvT": wkvT, "wgaT": wgaT,
            "woT": woT, "bgr": bgr, "qsc": qsc, "ksc": ksc,
            "maskt": masks0 if c == 0 else masks_other,
            "identb": identb, "onesc": onesc, "onesr": onesr,
            "wgS": wgS, "wuS": wuS, "wdS": wdS,
            "gidx": cores[c]["gidx"], "sidx": cores[c]["sidx"],
            "wrow": cores[c]["wrow"], "extw": cores[c]["extw"],
        })

    res = run_bass_kernel_spmd(nc, in_maps, core_ids=list(range(NCORES)))
    shards = [res.results[c]["out"] for c in range(NCORES)]
    full = np.concatenate(shards, axis=0).reshape(1, NTOK, D)
    return full


# version 05
# speedup vs baseline: 1.0158x; 1.0158x over previous
import numpy as np
import ml_dtypes
import concourse.bass as bass
import concourse.mybir as mybir
from concourse.bass_utils import run_bass_kernel_spmd
from concourse.library_config import mlp as _mlp_lib

F32 = mybir.dt.float32
BF16 = mybir.dt.bfloat16
I16 = mybir.dt.int16
AF = mybir.ActivationFunctionType
ALU = mybir.AluOpType
AX = mybir.AxisListType
BF = ml_dtypes.bfloat16

NCORES = 8
NTOK = 8192
D = 2048
H = 8
DH = 128
WIN = 512
NE = 32
TPC = NTOK // NCORES
SL = TPC + WIN
NT = SL // 128
NSLOT = 4
MASKVAL = -30000.0
YROWS = NTOK + 128

ENGINES = ("sync", "tensor", "vector", "scalar", "gpsimd")
DEBUG_DUMPS = False
SKIP_ATTN = False
SKIP_MOE = False



def _pack_idx(vals, ncols):
    arr = np.zeros((128, ncols), np.int16)
    j = np.arange(len(vals))
    v = np.asarray(vals, np.int16)
    arr[j % 16, j // 16] = v
    arr[16 + j % 16, j // 16] = v
    return arr


def _route(topk_idx, topk_weight):
    ti = np.asarray(topk_idx).reshape(-1).astype(np.int64)
    tw = np.asarray(topk_weight).reshape(-1).astype(np.float32)
    nreg = NTOK * 4
    reg_e, reg_w = ti[:nreg], tw[:nreg]
    reg_t = np.repeat(np.arange(NTOK), 4)
    ext_e, ext_w = ti[nreg:], tw[nreg:]

    experts = []
    for e in range(NE):
        m = reg_e == e
        toks = reg_t[m]
        ws = reg_w[m]
        rows = [(NTOK - 1, 0.0, True)]
        rows += list(zip(toks.tolist(), ws.tolist(), [False] * len(toks)))
        groups, group_toks = [], []
        for r in rows:
            t = r[0]
            for ci in range(len(groups)):
                if len(groups[ci]) < 256 and t not in group_toks[ci]:
                    groups[ci].append(r)
                    group_toks[ci].add(t)
                    break
            else:
                groups.append([r])
                group_toks.append({t})
        chunks = []
        for g in groups:
            chunks.append(g[0:128])
            chunks.append(g[128:256])
        experts.append(chunks)
    extw = [ext_w[ext_e == e] for e in range(NE)]
    return experts, extw


def _plan(experts, extw):
    nch = np.array([len(c) for c in experts])
    order = np.argsort(-nch, kind="stable")
    expert_of = [[int(order[s * NCORES + c]) for s in range(NSLOT)]
                 for c in range(NCORES)]
    RS = [int(max(nch[order[s * NCORES + c]] for c in range(NCORES)))
          for s in range(NSLOT)]
    EWC = max(1, -(-int(max(len(w) for w in extw)) // 128))
    NCH = sum(RS)

    batches = []
    base = 0
    for s in range(NSLOT):
        c0 = 0
        while c0 < RS[s]:
            nb = min(4, RS[s] - c0)
            batches.append((s, base + c0, nb))
            c0 += nb
        base += RS[s]

    cores = []
    spec_pos = None
    for c in range(NCORES):
        gi = np.zeros(NCH * 128, np.int64)
        si = np.zeros(NCH * 128, np.int64)
        wr = np.zeros((128, NCH), np.float32)
        sp = []
        gch = 0
        for s in range(NSLOT):
            e = expert_of[c][s]
            chunks = experts[e]
            for lc in range(RS[s]):
                rows = chunks[lc] if lc < len(chunks) else []
                for j in range(128):
                    p = gch * 128 + j
                    if j < len(rows):
                        t, w, spc = rows[j]
                        gi[p] = t
                        si[p] = t
                        wr[j, gch] = w
                        if spc:
                            sp.append((gch, j))
                    else:
                        gi[p] = 0
                        si[p] = NTOK + j
                gch += 1
        assert len(sp) == NSLOT
        ew = np.zeros((128, NSLOT * EWC), np.float32)
        for s in range(NSLOT):
            e = expert_of[c][s]
            assert sp[s][1] == 0
            vals = extw[e]
            tmp = np.zeros(128 * EWC, np.float32)
            tmp[:len(vals)] = vals
            ew[:, s * EWC:(s + 1) * EWC] = tmp.reshape(EWC, 128).T
        if spec_pos is None:
            spec_pos = sp
        else:
            assert sp == spec_pos, "special positions must be core-uniform"
        cores.append(dict(expert_of=expert_of[c],
                          gidx=_pack_idx(gi, NCH * 8),
                          sidx=_pack_idx(si, NCH * 8),
                          wrow=wr, extw=ew))
    return dict(RS=RS, NCH=NCH, EWC=EWC, batches=batches, spec_pos=spec_pos), cores



class _Sem:
    __slots__ = ("h", "n", "name")

    def __init__(self, h, name):
        self.h = h
        self.n = 0
        self.name = name

    @property
    def done(self):
        return (self, self.n)


class _Builder:
    def __init__(self):
        self.ops = {e: [] for e in ENGINES}

    def op(self, eng, fn, waits=(), incs=()):
        for s, k in incs:
            s.n += k
        self.ops[eng].append((tuple(w for w in waits if w is not None),
                              fn, tuple(incs)))

    def emit(self, block):
        for eng in ENGINES:
            oplist = self.ops[eng]

            def body(e, _ops=oplist):
                for waits, fn, incs in _ops:
                    for s, v in waits:
                        if v > 0:
                            e.wait_ge(s.h, v)
                    ins = fn(e)
                    for s, k in incs:
                        ins.then_inc(s.h, k)
            getattr(block, eng)(body)


class _Span:
    def __init__(self, t, cols):
        self.t = t
        self.cols = cols
        self.pos = 0

    def take(self, n):
        assert self.pos + n <= self.cols, f"arena overflow {self.pos}+{n}"
        o = self.pos
        self.pos += n
        return o

    def mark(self):
        return self.pos

    def reset(self, pos):
        self.pos = pos



def _build(meta):
    RS = meta["RS"]
    NCH = meta["NCH"]
    EWC = meta["EWC"]
    batches = meta["batches"]
    spec_pos = meta["spec_pos"]
    RSMAX = max(RS)

    nc = bass.Bass("TRN2", target_bir_lowering=False, num_devices=NCORES)

    def din(name, shape, dt):
        return nc.dram_tensor(name, list(shape), dt, kind="ExternalInput").ap()

    x_sl = din("x_sl", (SL, D), F32)
    wqT = din("wqT", (D, H * DH), F32)
    wkvT = din("wkvT", (D, 2 * H * DH), F32)
    wgaT = din("wgaT", (D, H), F32)
    woT = din("woT", (H * DH, D), F32)
    bgr = din("bgr", (1, H), F32)
    qsc = din("qsc", (128, 1), F32)
    ksc = din("ksc", (128, 1), F32)
    maskt = din("maskt", (5, 128, 640), F32)
    identb = din("identb", (128, 128), BF16)
    onesc = din("onesc", (128, 1), BF16)
    onesr = din("onesr", (1, 128), BF16)
    wgS = din("wgS", (NSLOT, D, 1024), F32)
    wuS = din("wuS", (NSLOT, D, 1024), F32)
    wdS = din("wdS", (NSLOT, 1024, D), F32)
    gidx = din("gidx", (128, NCH * 8), I16)
    sidx = din("sidx", (128, NCH * 8), I16)
    wrow = din("wrow", (128, NCH), F32)
    extw = din("extw", (128, NSLOT * EWC), F32)
    yz = din("yz", (YROWS, D), BF16)

    out = nc.dram_tensor("out", [TPC, D], F32, kind="ExternalOutput").ap()
    dbg = {}
    if DEBUG_DUMPS:
        for nm, shp in (("d_xn", [SL, D]), ("d_ag", [NTOK, D]),
                        ("d_s", [TPC, D]), ("d_ym", [NTOK, D]),
                        ("d_yr", [TPC, D])):
            dbg[nm] = nc.dram_tensor(nm, shp, BF16, kind="ExternalOutput").ap()

    xn_loc = nc.dram_tensor("xn_loc", [SL, D], BF16).ap()
    ag_in = nc.dram_tensor("ag_in", [TPC, D], BF16).ap()
    ag_out = nc.dram_tensor("ag_out", [NTOK, D], BF16).ap()
    y_full = nc.dram_tensor("y_full", [YROWS, D], BF16).ap()
    y_main = nc.dram_tensor("y_main", [NTOK, D], BF16).ap()
    y_red = nc.dram_tensor("y_red", [TPC, D], BF16).ap()
    s_spill = nc.dram_tensor("s_spill", [TPC, D], BF16).ap()

    from contextlib import ExitStack
    stack = ExitStack()
    with stack:
        def sem(name):
            return _Sem(stack.enter_context(nc.semaphore(name)), name)

        def sbuf(name, shape, dt):
            return stack.enter_context(nc.sbuf_tensor(name, list(shape), dt))

        def psum(name, shape, dt):
            return stack.enter_context(nc.psum_tensor(name, list(shape), dt))

        LD = sem("LD")
        LD2 = sem("LD2")
        WSM = sem("WSM")
        GSM = sem("GSM")
        SSM = sem("SSM")
        CC = sem("CC")
        TEN = sem("TEN")
        VEC = sem("VEC")
        SCA = sem("SCA")
        XLW = sem("XLW")
        AGW = sem("AGW")
        TRS = sem("TRS")
        SSP = sem("SSP")
        OS = sem("OS")

        ident_sb = sbuf("ident_sb", (128, 128), BF16)
        ones128_sb = sbuf("ones128_sb", (128, 128), BF16)
        onesc_sb = sbuf("onesc_sb", (128, 1), BF16)
        onesr_sb = sbuf("onesr_sb", (1, 128), BF16)
        bgr_sb = sbuf("bgr_sb", (1, H), BF16)
        wga_sb = sbuf("wga_sb", (128, 16 * H), BF16)
        ccol_sb = sbuf("ccol_sb", (128, 1), F32)
        sc1_sb = sbuf("sc1_sb", (128, 4), F32)
        gidx_sb = sbuf("gidx_sb", (128, NCH * 8), I16)
        sidx_sb = sbuf("sidx_sb", (128, NCH * 8), I16)
        wrow_sb = sbuf("wrow_sb", (128, NCH), F32)
        cext_sb = sbuf("cext_sb", (128, NSLOT), F32)
        extw_sb = sbuf("extw_sb", (128, NSLOT * EWC), F32)
        extb_sb = sbuf("extb_sb", (128, NSLOT * EWC), BF16)
        qsc_sb = sbuf("qsc_sb", (128, 1), F32)
        ksc_sb = sbuf("ksc_sb", (128, 1), F32)

        ABF_COLS = 88100
        AF32_COLS = 6700
        abf_t = sbuf("abf", (128, ABF_COLS), BF16)
        af32_t = sbuf("af32", (128, AF32_COLS), F32)
        abf = _Span(abf_t, ABF_COLS)
        af32 = _Span(af32_t, AF32_COLS)

        psA = psum("psA", (128, 512), F32)
        psB = psum("psB", (128, 512), F32)
        psC = psum("psC", (128, 512), F32)
        psD = psum("psD", (128, 512), F32)
        psE = psum("psE", (128, 640), F32)
        psF1 = psum("psF1", (128, 128), BF16)
        psF2 = psum("psF2", (128, 128), BF16)

        b = _Builder()

        for dst, src in ((ident_sb, identb), (onesc_sb, onesc),
                         (onesr_sb, onesr), (gidx_sb, gidx), (sidx_sb, sidx),
                         (wrow_sb, wrow), (extw_sb, extw), (qsc_sb, qsc),
                         (ksc_sb, ksc)):
            b.op("sync", lambda e, d=dst, s=src: e.dma_start(out=d[:, :], in_=s[:, :]),
                 incs=[(LD, 16)])
        ph0_ld = LD.done

        b.op("gpsimd", lambda e: e.load_library(_mlp_lib))
        b.op("gpsimd", lambda e: e.dma_start(out=bgr_sb[:, :], in_=bgr[:, :]),
             incs=[(WSM, 16)])
        b.op("gpsimd", lambda e: e.dma_start(
            out=wga_sb[:, :].rearrange("p (c m) -> p c m", m=H),
            in_=wgaT.rearrange("(c p) m -> p c m", p=128)),
            incs=[(WSM, 16)])
        wga_done = WSM.done

        wreg_o = abf.take(8192)
        reg2_o = abf.take(6144)
        xnT_o = abf.take(16 * SL)
        q_o = abf.take(H * TPC)
        k_o = abf.take(H * SL)
        v_o = abf.take(12 * 1024)
        o_o = abf.take(H * TPC)

        wreg = abf_t[:, wreg_o:wreg_o + 8192]
        xnT = [abf_t[:, xnT_o + hc * SL: xnT_o + (hc + 1) * SL] for hc in range(16)]
        q_sb = abf_t[:, q_o:q_o + H * TPC]
        k_sb = abf_t[:, k_o:k_o + H * SL]
        v_sb = abf_t[:, v_o:v_o + 12 * 1024]
        o_sb = abf_t[:, o_o:o_o + H * TPC]

        xn2 = [abf_t[:, reg2_o + i * D: reg2_o + (i + 1) * D] for i in range(2)]
        squt = abf_t[:, reg2_o + 2 * D: reg2_o + 3 * D]
        qsq = [abf_t[:, reg2_o + i * 512: reg2_o + (i + 1) * 512] for i in range(2)]
        ksq = [abf_t[:, reg2_o + 1024 + i * 512: reg2_o + 1024 + (i + 1) * 512]
               for i in range(2)]
        bcT = abf_t[:, reg2_o + 2048: reg2_o + 2176]
        attbf = abf_t[:, reg2_o + 2176: reg2_o + 2816]
        attT = abf_t[:, reg2_o + 2816: reg2_o + 3456]
        s_half = [abf_t[:, reg2_o + 3456 + i * 1024: reg2_o + 3456 + (i + 1) * 1024]
                  for i in range(2)]

        x2_o = af32.take(2 * D)
        lst_o = af32.take(16)
        rq_o = af32.take(64)
        gt_o = af32.take(64)
        rk_o = af32.take(96)
        ms_o = af32.take(640)
        atf_o = af32.take(640)
        st_o = af32.take(16)
        sil_o = af32.take(1024)
        x2 = [af32_t[:, x2_o + i * D: x2_o + (i + 1) * D] for i in range(2)]
        lst = af32_t[:, lst_o:lst_o + 16]
        rq_tok = af32_t[:, rq_o:rq_o + 64]
        gate_tok = af32_t[:, gt_o:gt_o + 64]
        rk_tok = af32_t[:, rk_o:rk_o + 96]
        mask_sb = af32_t[:, x2_o:x2_o + 5 * 640]
        ms = af32_t[:, ms_o:ms_o + 640]
        attf = af32_t[:, atf_o:atf_o + 640]
        stt = af32_t[:, st_o:st_o + 16]
        sil = [af32_t[:, sil_o:sil_o + 512], af32_t[:, sil_o + 512:sil_o + 1024]]

        b.op("vector", lambda e: e.memset(ones128_sb[:, :], 1.0),
             incs=[(VEC, 1)])
        zz_done = VEC.done

        b.op("vector", lambda e: e.tensor_mul(sc1_sb[:, 0:1], qsc_sb[:, :],
                                              ksc_sb[:, :]),
             waits=[ph0_ld], incs=[(VEC, 1)])
        b.op("vector", lambda e: e.tensor_scalar_mul(ccol_sb[:, :],
                                                     sc1_sb[:, 0:1], 8.0),
             waits=[VEC.done], incs=[(VEC, 1)])
        ccol_done = VEC.done

        b.op("vector", lambda e: e.tensor_copy(extb_sb[:, :], extw_sb[:, :]),
             waits=[ph0_ld], incs=[(VEC, 1)])
        extb_done = VEC.done

        def _extmm(e):
            last = None
            for s in range(NSLOT):
                last = e.matmul(psC[0:1, s * EWC:(s + 1) * EWC],
                                onesc_sb[:, :],
                                extb_sb[:, s * EWC:(s + 1) * EWC])
            return last
        b.op("tensor", _extmm, waits=[extb_done], incs=[(TEN, 1)])
        extmm_done = TEN.done

        def _extred(e):
            last = None
            for s in range(NSLOT):
                last = e.reduce_sum(cext_sb[0:1, s:s + 1],
                                    psC[0:1, s * EWC:(s + 1) * EWC], axis=AX.X)
            return last
        b.op("vector", _extred, waits=[extmm_done], incs=[(VEC, 1)])

        def _extcp(e):
            last = None
            for s in range(NSLOT):
                gch, p = spec_pos[s]
                last = e.tensor_copy(wrow_sb[0:1, gch:gch + 1],
                                     cext_sb[0:1, s:s + 1])
            return last
        b.op("vector", _extcp, waits=[VEC.done], incs=[(VEC, 1)])
        wfill_done = VEC.done

        b.op("sync", lambda e: e.dma_start(out=y_full[:, :], in_=yz[:, :]),
             incs=[(LD, 16)])
        yzero_done = LD.done

        xld, xvec, xsca, xwr = [None] * NT, [None] * NT, [None] * NT, [None] * NT
        for i in range(NT):
            waits = [xvec[i - 2], xsca[i - 2]] if i >= 2 else []
            b.op("sync", lambda e, i=i: e.dma_start(
                out=x2[i % 2], in_=x_sl[i * 128:(i + 1) * 128, :]),
                waits=waits, incs=[(LD, 16)])
            xld[i] = LD.done

            sqc = 1 + 8 * (i % 2)
            b.op("scalar", lambda e, i=i, sqc=sqc: e.activation(
                squt, x2[i % 2], AF.Square, accum_out=lst[:, sqc:sqc + 1]),
                waits=[xld[i]], incs=[(SCA, 1)])
            ssq_done = SCA.done

            wv = [xld[i], ssq_done]
            if i >= 2:
                wv.append(xwr[i - 2])
            b.op("vector", lambda e, i=i: e.reduce_sum(lst[:, 0:1], x2[i % 2],
                                                       axis=AX.X),
                 waits=wv, incs=[(VEC, 1)])
            b.op("vector", lambda e: e.tensor_scalar_mul(lst[:, 2:3],
                                                         lst[:, 0:1], 1.0 / D),
                 waits=[VEC.done], incs=[(VEC, 1)])
            b.op("vector", lambda e, sqc=sqc: e.tensor_scalar_mul(
                lst[:, 3:4], lst[:, sqc:sqc + 1], 1.0 / D),
                waits=[VEC.done], incs=[(VEC, 1)])
            b.op("vector", lambda e: e.tensor_mul(lst[:, 4:5], lst[:, 2:3],
                                                  lst[:, 2:3]),
                 waits=[VEC.done], incs=[(VEC, 1)])
            b.op("vector", lambda e: e.tensor_sub(lst[:, 5:6], lst[:, 3:4],
                                                  lst[:, 4:5]),
                 waits=[VEC.done], incs=[(VEC, 1)])
            var_done = VEC.done

            b.op("vector", lambda e: e.tensor_scalar_add(
                lst[:, 8:9], lst[:, 5:6], 1e-5),
                waits=[var_done], incs=[(VEC, 1)])
            epsd = VEC.done
            b.op("scalar", lambda e: e.activation(
                lst[:, 6:7], lst[:, 8:9], AF.Sqrt),
                waits=[epsd], incs=[(SCA, 1)])
            xsca[i] = SCA.done

            b.op("vector", lambda e: e.reciprocal(lst[:, 7:8], lst[:, 6:7]),
                 waits=[xsca[i]], incs=[(VEC, 1)])
            rstd_done = VEC.done
            b.op("vector", lambda e, i=i: e.tensor_scalar(
                xn2[i % 2], x2[i % 2], lst[:, 2:3], lst[:, 7:8],
                op0=ALU.subtract, op1=ALU.mult),
                waits=[rstd_done], incs=[(VEC, 1)])
            xvec[i] = VEC.done

            def _lnw(e, i=i):
                ins = e.dma_start(out=xn_loc[i * 128:(i + 1) * 128, :],
                                  in_=xn2[i % 2])
                ins.then_inc(XLW.h, 16)
                if i >= 4:
                    ins = e.dma_start(out=ag_in[(i - 4) * 128:(i - 3) * 128, :],
                                      in_=xn2[i % 2])
                    ins.then_inc(AGW.h, 16)
                return ins
            XLW.n += 16
            if i >= 4:
                AGW.n += 16
            b.op("sync", _lnw, waits=[xvec[i]])
            xwr[i] = (XLW, XLW.n)
        ln_done = (VEC, VEC.n)
        ln_sca_done = (SCA, SCA.n)

        def _trd(e):
            last = None
            for hc in range(16):
                last = e.dma_start(out=xnT[hc],
                                   in_=xn_loc[:, hc * 128:(hc + 1) * 128],
                                   transpose=True)
                last.then_inc(TRS.h, 16)
            return last
        TRS.n += 16 * 16
        b.op("sync", _trd, waits=[(XLW, XLW.n)])
        xnT_done = TRS.done

        b.op("gpsimd", lambda e: e.collective_compute(
            "AllGather", ALU.bypass, replica_groups=[list(range(NCORES))],
            ins=[ag_in.opt()], outs=[ag_out.opt()]),
            waits=[(AGW, AGW.n)], incs=[(CC, 1)])
        ag_done = CC.done

        if SKIP_ATTN:
            s_done = (SSP, 0)
            attn_skip = True
        else:
            attn_skip = False
        if not attn_skip:
            prev_half_last = None
            ev_q = {}
            for half in range(2):
                wsrc = wqT.rearrange("(c p) m -> p c m", p=128)[:, :, half * 512:(half + 1) * 512]
                wts = [] if half == 0 else [prev_half_last]
                b.op("gpsimd", lambda e, s=wsrc: e.dma_start(
                    out=wreg[:, :].rearrange("p (c m) -> p c m", m=512), in_=s),
                    waits=wts, incs=[(WSM, 16)])
                wq_ld = WSM.done
                for hh in range(4):
                    h = half * 4 + hh
                    for bb in range(2):
                        u = h * 2 + bb
                        ps = psA if u % 2 == 0 else psB

                        def _qmm(e, hh=hh, bb=bb, ps=ps):
                            last = None
                            for hc in range(16):
                                last = e.matmul(
                                    ps[:, :],
                                    wreg[:, hc * 512 + hh * 128: hc * 512 + (hh + 1) * 128],
                                    xnT[hc][:, WIN + bb * 512: WIN + (bb + 1) * 512],
                                    start=(hc == 0), stop=(hc == 15))
                            return last
                        wts = [wq_ld, ccol_done, xnT_done]
                        if u >= 2:
                            wts.append(ev_q[u - 2])
                        b.op("tensor", _qmm, waits=wts, incs=[(TEN, 1)])
                        mm_done = TEN.done

                        b.op("vector", lambda e, h=h, bb=bb, ps=ps: e.tensor_scalar_mul(
                            q_sb[:, h * TPC + bb * 512: h * TPC + (bb + 1) * 512],
                            ps[:, :], ccol_sb[:, :]),
                            waits=[mm_done], incs=[(VEC, 1)])
                        evd = VEC.done

                        b.op("scalar", lambda e, bb=bb, ps=ps: e.activation(
                            qsq[bb], ps[:, :], AF.Square),
                            waits=[evd], incs=[(SCA, 1)])
                        sqd = SCA.done

                        def _ssmm(e, bb=bb):
                            last = None
                            for j in range(4):
                                last = e.matmul(psC[:, j:j + 1],
                                                qsq[bb][:, j * 128:(j + 1) * 128],
                                                onesc_sb[:, :])
                            return last
                        b.op("tensor", _ssmm, waits=[sqd], incs=[(TEN, 1)])
                        ssd = TEN.done

                        def _ssev(e, h=h, bb=bb):
                            last = None
                            for j in range(4):
                                t = bb * 4 + j
                                last = e.tensor_copy(
                                    rq_tok[:, t * 8 + h: t * 8 + h + 1],
                                    psC[:, j:j + 1])
                            return last
                        b.op("vector", _ssev, waits=[ssd], incs=[(VEC, 1)])
                        ev_q[u] = VEC.done
                prev_half_last = ev_q[half * 8 + 7]

            b.op("scalar", lambda e: e.activation(rq_tok[:, :], rq_tok[:, :], AF.Sqrt),
                 waits=[ev_q[15]], incs=[(SCA, 1)])

            b.op("vector", lambda e: e.tensor_scalar_max(
                rq_tok[:, :], rq_tok[:, :], 1e-12),
                waits=[SCA.done], incs=[(VEC, 1)])
            b.op("vector", lambda e: e.reciprocal(rq_tok[:, :], rq_tok[:, :]),
                 waits=[VEC.done], incs=[(VEC, 1)])
            rq_done = VEC.done

            prev_gate = None
            for t in range(8):
                def _gmm(e, t=t):
                    for hc in range(16):
                        e.matmul(psD[:, 0:8],
                                 xnT[hc][:, WIN + t * 128: WIN + (t + 1) * 128],
                                 wga_sb[:, hc * H:(hc + 1) * H],
                                 start=(hc == 0), stop=False)
                    return e.matmul(psD[:, 0:8], onesr_sb[:, :], bgr_sb[:, :],
                                    start=False, stop=True)
                wts = [xnT_done, wga_done, ph0_ld]
                if prev_gate is not None:
                    wts.append(prev_gate)
                b.op("tensor", _gmm, waits=wts, incs=[(TEN, 1)])
                b.op("scalar", lambda e, t=t: e.activation(
                    gate_tok[:, t * 8:(t + 1) * 8], psD[:, 0:8], AF.Sigmoid),
                    waits=[TEN.done], incs=[(SCA, 1)])
                prev_gate = SCA.done
            gates_done = prev_gate

            ev_k = {}
            for half in range(2):
                wsrc = wkvT.rearrange("(c p) m -> p c m", p=128)[:, :, half * 512:(half + 1) * 512]
                b.op("gpsimd", lambda e, s=wsrc: e.dma_start(
                    out=wreg[:, :].rearrange("p (c m) -> p c m", m=512), in_=s),
                    waits=[ev_q[15] if half == 0 else ev_k[11]],
                    incs=[(WSM, 16)])
                wk_ld = WSM.done
                for hh in range(4):
                    h = half * 4 + hh
                    for b3 in range(3):
                        u = h * 3 + b3
                        ps = psA if u % 2 == 0 else psB

                        def _kmm(e, hh=hh, b3=b3, ps=ps):
                            last = None
                            for hc in range(16):
                                last = e.matmul(
                                    ps[:, :],
                                    wreg[:, hc * 512 + hh * 128: hc * 512 + (hh + 1) * 128],
                                    xnT[hc][:, b3 * 512:(b3 + 1) * 512],
                                    start=(hc == 0), stop=(hc == 15))
                            return last
                        wts = [wk_ld]
                        if u >= 2:
                            wts.append(ev_k[u - 2])
                        b.op("tensor", _kmm, waits=wts, incs=[(TEN, 1)])
                        mm_done = TEN.done

                        b.op("vector", lambda e, h=h, b3=b3, ps=ps: e.tensor_copy(
                            k_sb[:, h * SL + b3 * 512: h * SL + (b3 + 1) * 512],
                            ps[:, :]),
                            waits=[mm_done], incs=[(VEC, 1)])
                        evd = VEC.done

                        b.op("scalar", lambda e, b3=b3, ps=ps: e.activation(
                            ksq[b3 % 2], ps[:, :], AF.Square),
                            waits=[evd], incs=[(SCA, 1)])
                        sqd = SCA.done

                        def _ssmm(e, b3=b3):
                            last = None
                            for j in range(4):
                                last = e.matmul(psC[:, j:j + 1],
                                                ksq[b3 % 2][:, j * 128:(j + 1) * 128],
                                                onesc_sb[:, :])
                            return last
                        b.op("tensor", _ssmm, waits=[sqd], incs=[(TEN, 1)])
                        ssd = TEN.done

                        def _ssev(e, h=h, b3=b3):
                            last = None
                            for j in range(4):
                                ch = b3 * 4 + j
                                last = e.tensor_copy(
                                    rk_tok[:, ch * 8 + h: ch * 8 + h + 1],
                                    psC[:, j:j + 1])
                            return last
                        b.op("vector", _ssev, waits=[ssd], incs=[(VEC, 1)])
                        ev_k[u] = VEC.done
            k_done = ev_k[23]

            ev_v = {}
            for half in range(2):
                wsrc = wkvT.rearrange("(c p) m -> p c m", p=128)[:, :, 1024 + half * 512: 1024 + (half + 1) * 512]
                b.op("gpsimd", lambda e, s=wsrc: e.dma_start(
                    out=wreg[:, :].rearrange("p (c m) -> p c m", m=512), in_=s),
                    waits=[k_done if half == 0 else ev_v[11]],
                    incs=[(WSM, 16)])
                wv_ld = WSM.done
                for ch in range(12):
                    u = half * 12 + ch
                    ps = psA if u % 2 == 0 else psB

                    def _vmm(e, ch=ch, ps=ps):
                        last = None
                        for hc in range(16):
                            last = e.matmul(
                                ps[:, :],
                                xnT[hc][:, ch * 128:(ch + 1) * 128],
                                wreg[:, hc * 512:(hc + 1) * 512],
                                start=(hc == 0), stop=(hc == 15))
                        return last
                    wts = [wv_ld]
                    if u >= 2:
                        wts.append(ev_v[u - 2])
                    b.op("tensor", _vmm, waits=wts, incs=[(TEN, 1)])
                    b.op("vector", lambda e, ch=ch, half=half, ps=ps: e.tensor_copy(
                        v_sb[:, ch * 1024 + half * 512: ch * 1024 + (half + 1) * 512],
                        ps[:, :]),
                        waits=[TEN.done], incs=[(VEC, 1)])
                    ev_v[u] = VEC.done
            v_done = ev_v[23]

            b.op("scalar", lambda e: e.activation(rk_tok[:, :], rk_tok[:, :], AF.Sqrt),
                 waits=[k_done], incs=[(SCA, 1)])

            b.op("vector", lambda e: e.tensor_scalar_max(
                rk_tok[:, :], rk_tok[:, :], 1e-12),
                waits=[SCA.done], incs=[(VEC, 1)])
            b.op("vector", lambda e: e.reciprocal(rk_tok[:, :], rk_tok[:, :]),
                 waits=[VEC.done], incs=[(VEC, 1)])
            rk_done = VEC.done

            prev_kh = {}
            for h in range(H):
                for ch in range(12):
                    u = h * 12 + ch
                    psf = psF1 if u % 2 == 0 else psF2

                    wts = [rk_done, zz_done]
                    if u >= 1:
                        wts.append(prev_kh[u - 1][0])
                    b.op("vector", lambda e, h=h, ch=ch: e.tensor_scalar_mul(
                        bcT, ones128_sb[:, :],
                        rk_tok[:, (ch * 8 + h):(ch * 8 + h) + 1]),
                        waits=wts, incs=[(VEC, 1)])
                    bct_done = VEC.done

                    wts = [bct_done, ph0_ld]
                    if u >= 2:
                        wts.append(prev_kh[u - 2][1])
                    if u == 0:
                        wts.append(gates_done)
                    b.op("tensor", lambda e, psf=psf: e.transpose(
                        psf[:, :], bcT, ident_sb[:, :]),
                        waits=wts, incs=[(TEN, 1)])
                    tr_done = TEN.done

                    b.op("vector", lambda e, h=h, ch=ch, psf=psf: e.tensor_mul(
                        k_sb[:, h * SL + ch * 128: h * SL + (ch + 1) * 128],
                        k_sb[:, h * SL + ch * 128: h * SL + (ch + 1) * 128],
                        psf[:, :]),
                        waits=[tr_done], incs=[(VEC, 1)])
                    prev_kh[u] = (tr_done, VEC.done)
            kh_done = prev_kh[H * 12 - 1][1]

            b.op("sync", lambda e: e.dma_start(
                out=mask_sb[:, :].rearrange("p (c m) -> p c m", m=640),
                in_=maskt.rearrange("c p m -> p c m")),
                waits=[ln_done, ln_sca_done], incs=[(LD, 16)])
            mask_done = LD.done

            prev_mask_add = None
            prev_exp = None
            prev_oev = None
            prev_t_ev = {}
            for h in range(H):
                for t in range(8):
                    u = h * 8 + t

                    def _simm(e, h=h, t=t):
                        e.matmul(psE[:, 0:512],
                                 q_sb[:, h * TPC + t * 128: h * TPC + (t + 1) * 128],
                                 k_sb[:, h * SL + t * 128: h * SL + t * 128 + 512],
                                 start=True, stop=True)
                        return e.matmul(
                            psE[:, 512:640],
                            q_sb[:, h * TPC + t * 128: h * TPC + (t + 1) * 128],
                            k_sb[:, h * SL + t * 128 + 512: h * SL + t * 128 + 640],
                            start=True, stop=True)
                    wts = [kh_done, mask_done]
                    if prev_mask_add is not None:
                        wts.append(prev_mask_add)
                    b.op("tensor", _simm, waits=wts, incs=[(TEN, 1)])
                    sim_done = TEN.done

                    mslot = t if t < 4 else 4

                    wts = [sim_done, rq_done]
                    if prev_exp is not None:
                        wts.append(prev_exp)
                    b.op("vector", lambda e, mslot=mslot: e.tensor_add(
                        ms, psE[:, :],
                        mask_sb[:, mslot * 640:(mslot + 1) * 640]),
                        waits=wts, incs=[(VEC, 1)])
                    b.op("vector", lambda e: e.reduce_max(
                        stt[:, 0:1], ms, axis=AX.X, negate=True),
                        waits=[VEC.done], incs=[(VEC, 1)])
                    b.op("vector", lambda e, h=h, t=t: e.tensor_mul(
                        stt[:, 1:2], stt[:, 0:1],
                        rq_tok[:, t * 8 + h: t * 8 + h + 1]),
                        waits=[VEC.done], incs=[(VEC, 1)])
                    prev_mask_add = VEC.done

                    b.op("scalar", lambda e, h=h, t=t: e.activation(
                        attf, ms, AF.Exp, bias=stt[:, 1:2],
                        scale=rq_tok[:, t * 8 + h: t * 8 + h + 1],
                        accum_out=stt[:, 2:3]),
                        waits=[prev_mask_add], incs=[(SCA, 1)])
                    prev_exp = SCA.done
                    exp_done = prev_exp

                    b.op("vector", lambda e: e.reciprocal(stt[:, 3:4],
                                                          stt[:, 2:3]),
                         waits=[exp_done], incs=[(VEC, 1)])
                    b.op("vector", lambda e, h=h, t=t: e.tensor_mul(
                        stt[:, 4:5], stt[:, 3:4],
                        gate_tok[:, t * 8 + h: t * 8 + h + 1]),
                        waits=[VEC.done], incs=[(VEC, 1)])
                    scl1_done = VEC.done
                    b.op("vector", lambda e: e.tensor_scalar_mul(
                        attbf, attf, stt[:, 4:5]),
                        waits=[scl1_done], incs=[(VEC, 1)])
                    attbf_done = VEC.done

                    tev = {}
                    for kk in range(5):
                        psf = psF1 if kk % 2 == 0 else psF2
                        wts = [attbf_done]
                        if kk >= 2:
                            wts.append(tev[kk - 2])
                        elif u > 0:
                            wts.append(prev_t_ev[(u - 1, 4 if kk == 0 else 3)])
                        b.op("tensor", lambda e, kk=kk, psf=psf: e.transpose(
                            psf[:, :], attbf[:, kk * 128:(kk + 1) * 128],
                            ident_sb[:, :]),
                            waits=wts, incs=[(TEN, 1)])
                        b.op("vector", lambda e, kk=kk, psf=psf: e.tensor_copy(
                            attT[:, kk * 128:(kk + 1) * 128], psf[:, :]),
                            waits=[TEN.done], incs=[(VEC, 1)])
                        tev[kk] = VEC.done
                        prev_t_ev[(u, kk)] = VEC.done

                    def _avmm(e, h=h, t=t):
                        last = None
                        for kk in range(5):
                            last = e.matmul(
                                psD[:, 0:128],
                                v_sb[:, (t + kk) * 1024 + h * 128:
                                     (t + kk) * 1024 + (h + 1) * 128],
                                attT[:, kk * 128:(kk + 1) * 128],
                                start=(kk == 0), stop=(kk == 4))
                        return last
                    wts = [tev[4], v_done]
                    if u == 0:
                        wts.append(gates_done)
                    if prev_oev is not None:
                        wts.append(prev_oev)
                    b.op("tensor", _avmm, waits=wts, incs=[(TEN, 1)])
                    b.op("vector", lambda e, h=h, t=t: e.tensor_copy(
                        o_sb[:, h * TPC + t * 128: h * TPC + (t + 1) * 128],
                        psD[:, 0:128]),
                        waits=[TEN.done], incs=[(VEC, 1)])
                    prev_oev = VEC.done
            attn_done = prev_oev

            prev_sev = {}
            for half in range(2):
                wsrc = woT.rearrange("(c p) m -> p c m", p=128)[:, :, half * 1024:(half + 1) * 1024]
                b.op("gpsimd", lambda e, s=wsrc: e.dma_start(
                    out=wreg[:, :].rearrange("p (c m) -> p c m", m=1024), in_=s),
                    waits=[v_done if half == 0 else prev_sev[(0, 15)]],
                    incs=[(WSM, 16)])
                wo_ld = WSM.done
                for t in range(8):
                    for dci in range(2):
                        u = t * 2 + dci
                        ps = psA if u % 2 == 0 else psB

                        def _smm(e, t=t, dci=dci, ps=ps):
                            last = None
                            for ic in range(8):
                                last = e.matmul(
                                    ps[:, :],
                                    o_sb[:, ic * TPC + t * 128: ic * TPC + (t + 1) * 128],
                                    wreg[:, ic * 1024 + dci * 512: ic * 1024 + (dci + 1) * 512],
                                    start=(ic == 0), stop=(ic == 7))
                            return last
                        wts = [wo_ld, attn_done]
                        if u >= 2:
                            wts.append(prev_sev[(half, u - 2)])
                        b.op("tensor", _smm, waits=wts, incs=[(TEN, 1)])
                        b.op("vector", lambda e, t=t, dci=dci, ps=ps: e.tensor_copy(
                            s_half[t % 2][:, dci * 512:(dci + 1) * 512], ps[:, :]),
                            waits=[TEN.done, (SSP, max(0, SSP.n - 16))],
                            incs=[(VEC, 1)])
                        prev_sev[(half, u)] = VEC.done
                    b.op("sync", lambda e, t=t, half=half: e.dma_start(
                        out=s_spill[t * 128:(t + 1) * 128,
                                    half * 1024:(half + 1) * 1024],
                        in_=s_half[t % 2]),
                        waits=[prev_sev[(half, t * 2 + 1)]], incs=[(SSP, 16)])
            s_done = SSP.done

        if SKIP_MOE:
            rs_done = (CC, CC.n)
            moe_skip = True
        else:
            moe_skip = False
        abf.reset(0)
        if not moe_skip:
            wg_o = abf.take(16 * 1024)
            wu_o = abf.take(16 * 1024)
            wd_o = abf.take(16 * 1024)
            xs_o = abf.take(2 * 16 * 512)
            mid_o = abf.take(8 * 128 * RSMAX)
            or_o = abf.take(2 * 2 * D)
            wg_sb = abf_t[:, wg_o:wg_o + 16 * 1024]
            wu_sb = abf_t[:, wu_o:wu_o + 16 * 1024]
            wd_sb = abf_t[:, wd_o:wd_o + 16 * 1024]
            xsT = [abf_t[:, xs_o + i * 8192: xs_o + (i + 1) * 8192] for i in range(2)]
            orow = [abf_t[:, or_o + i * 2 * D: or_o + (i + 1) * 2 * D] for i in range(2)]

            p2_bar = [(TEN, TEN.n), (VEC, VEC.n), (SCA, SCA.n), (SSP, SSP.n),
                  (TRS, TRS.n), (LD, LD.n)]

            slot_batches = [[bt for bt in batches if bt[0] == s] for s in range(NSLOT)]
            gather_ten = {}
            prev_scatter = None
            last_gu_mm = None
            last_dn_mm = None
            bid_global = 0
            ch_base = [0]
            for s in range(1, NSLOT):
                ch_base.append(ch_base[-1] + RS[s - 1])

            for s in range(NSLOT):
                wgsrc = wgS[s].rearrange("(c p) m -> p c m", p=128)
                wusrc = wuS[s].rearrange("(c p) m -> p c m", p=128)
                wdsrc = wdS[s].rearrange("(c p) m -> p c m", p=128)
                wts_w = list(p2_bar) if s == 0 else [last_gu_mm]
                b.op("gpsimd", lambda e, src=wgsrc: e.dma_start(
                    out=wg_sb[:, :].rearrange("p (c m) -> p c m", m=1024), in_=src),
                    waits=wts_w, incs=[(WSM, 16)])
                b.op("gpsimd", lambda e, src=wusrc: e.dma_start(
                    out=wu_sb[:, :].rearrange("p (c m) -> p c m", m=1024), in_=src),
                    incs=[(WSM, 16)])
                wgu_ld = WSM.done
                wts_d = [] if s == 0 else [last_dn_mm]
                b.op("gpsimd", lambda e, src=wdsrc: e.dma_start(
                    out=wd_sb[:, :].rearrange("p (c m) -> p c m", m=1024), in_=src),
                    waits=wts_d, incs=[(WSM, 16)])
                wd_ld = WSM.done

                dn_barrier = last_dn_mm
                prev_ev = {}
                for bi, (s_, gch0, nb) in enumerate(slot_batches[s]):
                    nidx = nb * 128
                    buf = bid_global % 2
                    wts_g = [ag_done, ph0_ld]
                    if s == 0 and bi == 0:
                        wts_g += p2_bar
                    if bid_global >= 2:
                        wts_g.append(gather_ten[bid_global - 2])
                    b.op("gpsimd", lambda e, buf=buf, gch0=gch0, nidx=nidx: e.dma_gather(
                        xsT[buf][:, 0:16 * nidx].rearrange("p (f i) -> p f i", f=16),
                        ag_out[:, :],
                        gidx_sb[:, gch0 * 8: gch0 * 8 + (nidx + 15) // 16],
                        nidx, nidx, 2048, transpose=True),
                        waits=wts_g, incs=[(GSM, 16)])
                    g_done = GSM.done

                    lc0 = gch0 - ch_base[s]
                    for ic in range(8):
                        psg = psA if ic % 2 == 0 else psC
                        psu = psB if ic % 2 == 0 else psD

                        def _gumm(e, buf=buf, ic=ic, nidx=nidx, psg=psg, psu=psu):
                            for hc in range(16):
                                e.matmul(psg[:, 0:nidx],
                                         wg_sb[:, hc * 1024 + ic * 128: hc * 1024 + (ic + 1) * 128],
                                         xsT[buf][:, hc * nidx:(hc + 1) * nidx],
                                         start=(hc == 0), stop=(hc == 15))
                            last = None
                            for hc in range(16):
                                last = e.matmul(psu[:, 0:nidx],
                                                wu_sb[:, hc * 1024 + ic * 128: hc * 1024 + (ic + 1) * 128],
                                                xsT[buf][:, hc * nidx:(hc + 1) * nidx],
                                                start=(hc == 0), stop=(hc == 15))
                            return last
                        wts = [g_done, wgu_ld]
                        if ic >= 2:
                            wts.append(prev_ev[(bi, ic - 2)])
                        elif bi > 0:
                            wts.append(prev_ev[(bi - 1, 6 + ic)])
                        b.op("tensor", _gumm, waits=wts, incs=[(TEN, 1)])
                        gu_done = TEN.done
                        if ic == 7:
                            gather_ten[bid_global] = gu_done
                            last_gu_mm = gu_done

                        sb_ = sil[ic % 2]
                        b.op("scalar", lambda e, nidx=nidx, psg=psg, sb_=sb_: e.activation(
                            sb_[:, 0:nidx], psg[:, 0:nidx], AF.Silu),
                            waits=[gu_done], incs=[(SCA, 1)])
                        sil_done = SCA.done

                        mo = mid_o + ic * 128 * RS[s] + lc0 * 128
                        wts_m = [sil_done]
                        if dn_barrier is not None and bi == 0:
                            wts_m.append(dn_barrier)
                        b.op("vector", lambda e, mo=mo, nidx=nidx, psu=psu, sb_=sb_:
                             e.tensor_mul(abf_t[:, mo:mo + nidx],
                                          sb_[:, 0:nidx], psu[:, 0:nidx]),
                             waits=wts_m, incs=[(VEC, 1)])
                        prev_ev[(bi, ic)] = VEC.done
                    bid_global += 1
                mid_done_of = {}
                for bi2, (s2, gch02, nb2) in enumerate(slot_batches[s]):
                    for lc2 in range(gch02 - ch_base[s], gch02 - ch_base[s] + nb2):
                        mid_done_of[lc2] = prev_ev[(bi2, 7)]
                mid_done = prev_ev[(len(slot_batches[s]) - 1, 7)]

                assert RS[s] % 2 == 0
                prev_dn = {}
                for rc in range(RS[s]):
                    gch = ch_base[s] + rc
                    grp = gch // 2
                    ob = orow[grp % 2]
                    oco = (rc % 2) * D
                    for dc in range(4):
                        ps = psA if dc % 2 == 0 else psB

                        def _dmm(e, rc=rc, dc=dc, ps=ps, s=s):
                            last = None
                            for ic in range(8):
                                last = e.matmul(
                                    ps[:, :],
                                    abf_t[:, mid_o + ic * 128 * RS[s] + rc * 128:
                                          mid_o + ic * 128 * RS[s] + (rc + 1) * 128],
                                    wd_sb[:, ic * 2048 + dc * 512: ic * 2048 + (dc + 1) * 512],
                                    start=(ic == 0), stop=(ic == 7))
                            return last
                        wts = [mid_done, wd_ld]
                        if rc > 0 or dc > 1:
                            pk = (rc, dc - 2) if dc >= 2 else (rc - 1, 2 + dc)
                            wts.append(prev_dn[pk])
                        b.op("tensor", _dmm, waits=wts, incs=[(TEN, 1)])
                        dn_done = TEN.done
                        if rc == RS[s] - 1 and dc == 3:
                            last_dn_mm = dn_done

                        wts_e = [dn_done, wfill_done]
                        if dc == 0 and rc % 2 == 0 and grp >= 2:
                            wts_e.append((SSM, SSM.n - 16))
                        b.op("vector", lambda e, gch=gch, dc=dc, ps=ps, ob=ob, oco=oco:
                             e.tensor_scalar_mul(ob[:, oco + dc * 512: oco + (dc + 1) * 512],
                                                 ps[:, :],
                                                 wrow_sb[:, gch:gch + 1]),
                             waits=wts_e, incs=[(VEC, 1)])
                        prev_dn[(rc, dc)] = VEC.done

                    if rc % 2 == 1:
                        wts_s = [prev_dn[(rc - 1, 3)], prev_dn[(rc, 3)], yzero_done]
                        if prev_scatter is not None:
                            wts_s.append(prev_scatter)
                        g0 = gch - 1
                        b.op("gpsimd", lambda e, g0=g0, ob=ob: e.dma_scatter_add(
                            y_full[:, :],
                            ob[:, :].rearrange("p (a m) -> p a m", a=2),
                            sidx_sb[:, g0 * 8:(g0 + 2) * 8], 256, 256, 2048),
                            waits=wts_s, incs=[(SSM, 16)])
                        prev_scatter = SSM.done

            b.op("sync", lambda e: e.dma_start(out=y_main[:, :],
                                               in_=y_full[0:NTOK, :]),
                 waits=[prev_scatter], incs=[(LD, 16)])
            b.op("gpsimd", lambda e: e.collective_compute(
                "ReduceScatter", ALU.add, replica_groups=[list(range(NCORES))],
                ins=[y_main.opt()], outs=[y_red.opt()]),
                waits=[LD.done], incs=[(CC, 1)])
            rs_done = CC.done

        abf.reset(0)
        yb_o = abf.take(2 * D)
        sb_o = abf.take(2 * D)
        yt = [abf_t[:, yb_o + i * D: yb_o + (i + 1) * D] for i in range(2)]
        st2 = [abf_t[:, sb_o + i * D: sb_o + (i + 1) * D] for i in range(2)]
        af32.reset(0)
        xf_o = af32.take(2 * D)
        g_o = af32.take(D)
        xt = [af32_t[:, xf_o + i * D: xf_o + (i + 1) * D] for i in range(2)]
        gsl = af32_t[:, g_o:g_o + D]

        prev_vec5 = {}
        prev_sca5 = {}
        for tt in range(8):
            buf = tt % 2

            def _fl(e, tt=tt, buf=buf):
                return e.dma_start(out=yt[buf],
                                   in_=y_red[tt * 128:(tt + 1) * 128, :])
            def _fl2(e, tt=tt, buf=buf):
                e.dma_start(out=st2[buf],
                            in_=s_spill[tt * 128:(tt + 1) * 128, :]).then_inc(LD2.h, 16)
                return e.dma_start(out=xt[buf],
                                   in_=x_sl[WIN + tt * 128: WIN + (tt + 1) * 128, :])
            LD2.n += 16
            wts = [rs_done, s_done]
            if tt >= 2:
                wts += [prev_vec5[tt - 2], prev_sca5[tt - 2]]
            b.op("sync", _fl, waits=wts, incs=[(LD, 16)])
            b.op("scalar", _fl2, waits=wts, incs=[(LD2, 16)])
            fl_done = LD.done
            fl2_done = LD2.done

            b.op("scalar", lambda e, buf=buf: e.activation(gsl, yt[buf], AF.Silu),
                 waits=[fl_done, fl2_done] + ([prev_vec5[tt - 1]] if tt >= 1 else []),
                 incs=[(SCA, 1)])
            prev_sca5[tt] = SCA.done

            b.op("vector", lambda e, buf=buf: e.tensor_mul(gsl, gsl, st2[buf]),
                 waits=[prev_sca5[tt], fl_done, fl2_done], incs=[(VEC, 1)])
            b.op("vector", lambda e, buf=buf: e.tensor_add(xt[buf], xt[buf], gsl),
                 waits=[VEC.done], incs=[(VEC, 1)])
            prev_vec5[tt] = VEC.done

            b.op("sync", lambda e, tt=tt, buf=buf: e.dma_start(
                out=out[tt * 128:(tt + 1) * 128, :], in_=xt[buf]),
                waits=[prev_vec5[tt]], incs=[(OS, 16)])

        if DEBUG_DUMPS:
            def _dbg(e):
                e.dma_start(out=dbg["d_xn"][:, :], in_=xn_loc[:, :]).then_inc(OS.h, 16)
                e.dma_start(out=dbg["d_ag"][:, :], in_=ag_out[:, :]).then_inc(OS.h, 16)
                e.dma_start(out=dbg["d_s"][:, :], in_=s_spill[:, :]).then_inc(OS.h, 16)
                e.dma_start(out=dbg["d_ym"][:, :], in_=y_main[:, :]).then_inc(OS.h, 16)
                return e.dma_start(out=dbg["d_yr"][:, :], in_=y_red[:, :])
            OS.n += 4 * 16
            b.op("sync", _dbg, waits=[rs_done, (OS, OS.n - 4 * 16)], incs=[(OS, 16)])
        b.op("sync", lambda e: e.nop(), waits=[(OS, OS.n)])

        with nc.Block() as block:
            b.emit(block)

    return nc



def kernel(**inputs):
    x = np.ascontiguousarray(np.asarray(inputs["L0_attn_x"])[0], np.float32)
    Wq = np.asarray(inputs["L0_attn_Wq"], np.float32)
    Wkv = np.asarray(inputs["L0_attn_Wkv"], np.float32)
    q_scale = np.asarray(inputs["L0_attn_q_scale"], np.float32)
    k_scale = np.asarray(inputs["L0_attn_k_scale"], np.float32)
    Wg = np.asarray(inputs["L0_attn_Wg"], np.float32)
    bg = np.asarray(inputs["L0_attn_bg"], np.float32)
    Wo = np.asarray(inputs["L0_attn_Wo"], np.float32)
    gate_w = np.asarray(inputs["L0_ffn_gate_weight"], np.float32)
    up_w = np.asarray(inputs["L0_ffn_up_weight"], np.float32)
    down_w = np.asarray(inputs["L0_ffn_down_weight"], np.float32)
    topk_idx = np.asarray(inputs["L0_ffn_topk_idx"])
    topk_weight = np.asarray(inputs["L0_ffn_topk_weight"], np.float32)

    experts, extw = _route(topk_idx, topk_weight)
    meta, cores = _plan(experts, extw)
    assert max(meta["RS"]) <= 14, "unexpectedly large expert group"
    assert meta["EWC"] <= 16

    nc = _build(meta)
    mybir.codegen_inst_isa_subclasses(nc)

    wqT = np.ascontiguousarray(Wq.T)
    wkvT = np.ascontiguousarray(Wkv.T)
    wgaT = np.ascontiguousarray(Wg.T)
    woT = np.ascontiguousarray(Wo.T)
    bgr = np.ascontiguousarray(bg.reshape(1, H))
    qsc = np.ascontiguousarray(q_scale.reshape(128, 1))
    ksc = np.ascontiguousarray(k_scale.reshape(128, 1))
    identb = np.eye(128, dtype=BF)
    onesc = np.ones((128, 1), BF)
    onesr = np.ones((1, 128), BF)

    qi = np.arange(128)[:, None]
    jj = np.arange(640)[None, :]
    base_ok = (jj >= qi) & (jj <= qi + WIN)
    masks0 = np.zeros((5, 128, 640), np.float32)
    for t in range(4):
        ok = base_ok & ((t * 128 + jj - WIN) >= 0)
        masks0[t] = np.where(ok, 0.0, MASKVAL)
    masks0[4] = np.where(base_ok, 0.0, MASKVAL)
    masks_other = np.broadcast_to(masks0[4], (5, 128, 640)).copy()
    masks_other = np.ascontiguousarray(masks_other)

    yzero_host = np.zeros((YROWS, D), BF)
    in_maps = []
    for c in range(NCORES):
        lo = c * TPC - WIN
        if lo < 0:
            x_slc = np.concatenate(
                [np.zeros((WIN, D), np.float32), x[0:TPC]], axis=0)
        else:
            x_slc = x[lo:c * TPC + TPC]
        x_slc = np.ascontiguousarray(x_slc)
        eo = cores[c]["expert_of"]
        wgS = np.ascontiguousarray(np.stack([gate_w[e].T for e in eo]))
        wuS = np.ascontiguousarray(np.stack([up_w[e].T for e in eo]))
        wdS = np.ascontiguousarray(np.stack([down_w[e].T for e in eo]))
        in_maps.append({
            "yz": yzero_host,
            "x_sl": x_slc, "wqT": wqT, "wkvT": wkvT, "wgaT": wgaT,
            "woT": woT, "bgr": bgr, "qsc": qsc, "ksc": ksc,
            "maskt": masks0 if c == 0 else masks_other,
            "identb": identb, "onesc": onesc, "onesr": onesr,
            "wgS": wgS, "wuS": wuS, "wdS": wdS,
            "gidx": cores[c]["gidx"], "sidx": cores[c]["sidx"],
            "wrow": cores[c]["wrow"], "extw": cores[c]["extw"],
        })

    res = run_bass_kernel_spmd(nc, in_maps, core_ids=list(range(NCORES)))
    shards = [res.results[c]["out"] for c in range(NCORES)]
    full = np.concatenate(shards, axis=0).reshape(1, NTOK, D)
    return full


# version 06
# speedup vs baseline: 1.0365x; 1.0203x over previous
import numpy as np
import ml_dtypes
import concourse.bass as bass
import concourse.mybir as mybir
from concourse.bass_utils import run_bass_kernel_spmd
from concourse.library_config import mlp as _mlp_lib

F32 = mybir.dt.float32
BF16 = mybir.dt.bfloat16
I16 = mybir.dt.int16
AF = mybir.ActivationFunctionType
ALU = mybir.AluOpType
AX = mybir.AxisListType
BF = ml_dtypes.bfloat16

NCORES = 8
NTOK = 8192
D = 2048
H = 8
DH = 128
WIN = 512
NE = 32
TPC = NTOK // NCORES
SL = TPC + WIN
NT = SL // 128
NSLOT = 4
MASKVAL = -30000.0
YROWS = NTOK + 128

ENGINES = ("sync", "tensor", "vector", "scalar", "gpsimd")
DEBUG_DUMPS = False
SKIP_ATTN = False
SKIP_MOE = False



def _pack_idx(vals, ncols):
    arr = np.zeros((128, ncols), np.int16)
    j = np.arange(len(vals))
    v = np.asarray(vals, np.int16)
    arr[j % 16, j // 16] = v
    arr[16 + j % 16, j // 16] = v
    return arr


def _route(topk_idx, topk_weight):
    ti = np.asarray(topk_idx).reshape(-1).astype(np.int64)
    tw = np.asarray(topk_weight).reshape(-1).astype(np.float32)
    nreg = NTOK * 4
    reg_e, reg_w = ti[:nreg], tw[:nreg]
    reg_t = np.repeat(np.arange(NTOK), 4)
    ext_e, ext_w = ti[nreg:], tw[nreg:]

    experts = []
    for e in range(NE):
        m = reg_e == e
        toks = reg_t[m]
        ws = reg_w[m]
        rows = [(NTOK - 1, 0.0, True)]
        rows += list(zip(toks.tolist(), ws.tolist(), [False] * len(toks)))
        groups, group_toks = [], []
        for r in rows:
            t = r[0]
            for ci in range(len(groups)):
                if len(groups[ci]) < 256 and t not in group_toks[ci]:
                    groups[ci].append(r)
                    group_toks[ci].add(t)
                    break
            else:
                groups.append([r])
                group_toks.append({t})
        chunks = []
        for g in groups:
            chunks.append(g[0:128])
            chunks.append(g[128:256])
        experts.append(chunks)
    extw = [ext_w[ext_e == e] for e in range(NE)]
    return experts, extw


def _plan(experts, extw):
    nch = np.array([len(c) for c in experts])
    order = np.argsort(-nch, kind="stable")
    expert_of = [[int(order[s * NCORES + c]) for s in range(NSLOT)]
                 for c in range(NCORES)]
    RS = [int(max(nch[order[s * NCORES + c]] for c in range(NCORES)))
          for s in range(NSLOT)]
    EWC = max(1, -(-int(max(len(w) for w in extw)) // 128))
    NCH = sum(RS)

    batches = []
    base = 0
    for s in range(NSLOT):
        c0 = 0
        while c0 < RS[s]:
            nb = min(4, RS[s] - c0)
            batches.append((s, base + c0, nb))
            c0 += nb
        base += RS[s]

    cores = []
    spec_pos = None
    for c in range(NCORES):
        gi = np.zeros(NCH * 128, np.int64)
        si = np.zeros(NCH * 128, np.int64)
        wr = np.zeros((128, NCH), np.float32)
        sp = []
        gch = 0
        for s in range(NSLOT):
            e = expert_of[c][s]
            chunks = experts[e]
            for lc in range(RS[s]):
                rows = chunks[lc] if lc < len(chunks) else []
                for j in range(128):
                    p = gch * 128 + j
                    if j < len(rows):
                        t, w, spc = rows[j]
                        gi[p] = t
                        si[p] = t
                        wr[j, gch] = w
                        if spc:
                            sp.append((gch, j))
                    else:
                        gi[p] = 0
                        si[p] = NTOK + j
                gch += 1
        assert len(sp) == NSLOT
        ew = np.zeros((128, NSLOT * EWC), np.float32)
        for s in range(NSLOT):
            e = expert_of[c][s]
            assert sp[s][1] == 0
            vals = extw[e]
            tmp = np.zeros(128 * EWC, np.float32)
            tmp[:len(vals)] = vals
            ew[:, s * EWC:(s + 1) * EWC] = tmp.reshape(EWC, 128).T
        if spec_pos is None:
            spec_pos = sp
        else:
            assert sp == spec_pos, "special positions must be core-uniform"
        cores.append(dict(expert_of=expert_of[c],
                          gidx=_pack_idx(gi, NCH * 8),
                          sidx=_pack_idx(si, NCH * 8),
                          wrow=wr, extw=ew))
    return dict(RS=RS, NCH=NCH, EWC=EWC, batches=batches, spec_pos=spec_pos), cores



class _Sem:
    __slots__ = ("h", "n", "name")

    def __init__(self, h, name):
        self.h = h
        self.n = 0
        self.name = name

    @property
    def done(self):
        return (self, self.n)


class _Builder:
    def __init__(self):
        self.ops = {e: [] for e in ENGINES}

    def op(self, eng, fn, waits=(), incs=()):
        for s, k in incs:
            s.n += k
        self.ops[eng].append((tuple(w for w in waits if w is not None),
                              fn, tuple(incs)))

    def emit(self, block):
        for eng in ENGINES:
            oplist = self.ops[eng]

            def body(e, _ops=oplist):
                for waits, fn, incs in _ops:
                    for s, v in waits:
                        if v > 0:
                            e.wait_ge(s.h, v)
                    ins = fn(e)
                    for s, k in incs:
                        ins.then_inc(s.h, k)
            getattr(block, eng)(body)


class _Span:
    def __init__(self, t, cols):
        self.t = t
        self.cols = cols
        self.pos = 0

    def take(self, n):
        assert self.pos + n <= self.cols, f"arena overflow {self.pos}+{n}"
        o = self.pos
        self.pos += n
        return o

    def mark(self):
        return self.pos

    def reset(self, pos):
        self.pos = pos



def _build(meta):
    RS = meta["RS"]
    NCH = meta["NCH"]
    EWC = meta["EWC"]
    batches = meta["batches"]
    spec_pos = meta["spec_pos"]
    RSMAX = max(RS)

    nc = bass.Bass("TRN2", target_bir_lowering=False, num_devices=NCORES)

    def din(name, shape, dt):
        return nc.dram_tensor(name, list(shape), dt, kind="ExternalInput").ap()

    x_sl = din("x_sl", (SL, D), F32)
    wqT = din("wqT", (D, H * DH), F32)
    wkvT = din("wkvT", (D, 2 * H * DH), F32)
    wgaT = din("wgaT", (D, H), F32)
    woT = din("woT", (H * DH, D), F32)
    bgr = din("bgr", (1, H), F32)
    qsc = din("qsc", (128, 1), F32)
    ksc = din("ksc", (128, 1), F32)
    maskt = din("maskt", (5, 128, 640), F32)
    identb = din("identb", (128, 128), BF16)
    onesc = din("onesc", (128, 1), BF16)
    onesr = din("onesr", (1, 128), BF16)
    wgS = din("wgS", (NSLOT, D, 1024), F32)
    wuS = din("wuS", (NSLOT, D, 1024), F32)
    wdS = din("wdS", (NSLOT, 1024, D), F32)
    gidx = din("gidx", (128, NCH * 8), I16)
    sidx = din("sidx", (128, NCH * 8), I16)
    wrow = din("wrow", (128, NCH), F32)
    extw = din("extw", (128, NSLOT * EWC), F32)
    yz = din("yz", (YROWS, D), BF16)

    out = nc.dram_tensor("out", [TPC, D], F32, kind="ExternalOutput").ap()
    dbg = {}
    if DEBUG_DUMPS:
        for nm, shp in (("d_xn", [SL, D]), ("d_ag", [NTOK, D]),
                        ("d_s", [TPC, D]), ("d_ym", [NTOK, D]),
                        ("d_yr", [TPC, D])):
            dbg[nm] = nc.dram_tensor(nm, shp, BF16, kind="ExternalOutput").ap()

    xn_loc = nc.dram_tensor("xn_loc", [SL, D], BF16).ap()
    ag_in = nc.dram_tensor("ag_in", [TPC, D], BF16).ap()
    ag_out = nc.dram_tensor("ag_out", [NTOK, D], BF16).ap()
    y_full = nc.dram_tensor("y_full", [YROWS, D], BF16).ap()
    y_main = nc.dram_tensor("y_main", [NTOK, D], BF16).ap()
    y_red = nc.dram_tensor("y_red", [TPC, D], BF16).ap()
    s_spill = nc.dram_tensor("s_spill", [TPC, D], BF16).ap()

    from contextlib import ExitStack
    stack = ExitStack()
    with stack:
        def sem(name):
            return _Sem(stack.enter_context(nc.semaphore(name)), name)

        def sbuf(name, shape, dt):
            return stack.enter_context(nc.sbuf_tensor(name, list(shape), dt))

        def psum(name, shape, dt):
            return stack.enter_context(nc.psum_tensor(name, list(shape), dt))

        LD = sem("LD")
        LD2 = sem("LD2")
        WSM = sem("WSM")
        GSM = sem("GSM")
        SSM = sem("SSM")
        CC = sem("CC")
        TEN = sem("TEN")
        VEC = sem("VEC")
        SCA = sem("SCA")
        XLW = sem("XLW")
        AGW = sem("AGW")
        TRS = sem("TRS")
        SSP = sem("SSP")
        OS = sem("OS")

        ident_sb = sbuf("ident_sb", (128, 128), BF16)
        ones128_sb = sbuf("ones128_sb", (128, 128), BF16)
        onesc_sb = sbuf("onesc_sb", (128, 1), BF16)
        onesr_sb = sbuf("onesr_sb", (1, 128), BF16)
        bgr_sb = sbuf("bgr_sb", (1, H), BF16)
        wga_sb = sbuf("wga_sb", (128, 16 * H), BF16)
        ccol_sb = sbuf("ccol_sb", (128, 1), F32)
        sc1_sb = sbuf("sc1_sb", (128, 4), F32)
        gidx_sb = sbuf("gidx_sb", (128, NCH * 8), I16)
        sidx_sb = sbuf("sidx_sb", (128, NCH * 8), I16)
        wrow_sb = sbuf("wrow_sb", (128, NCH), F32)
        cext_sb = sbuf("cext_sb", (128, NSLOT), F32)
        extw_sb = sbuf("extw_sb", (128, NSLOT * EWC), F32)
        extb_sb = sbuf("extb_sb", (128, NSLOT * EWC), BF16)
        qsc_sb = sbuf("qsc_sb", (128, 1), F32)
        ksc_sb = sbuf("ksc_sb", (128, 1), F32)

        ABF_COLS = 88100
        AF32_COLS = 8000
        abf_t = sbuf("abf", (128, ABF_COLS), BF16)
        af32_t = sbuf("af32", (128, AF32_COLS), F32)
        abf = _Span(abf_t, ABF_COLS)
        af32 = _Span(af32_t, AF32_COLS)

        psA = psum("psA", (128, 512), F32)
        psB = psum("psB", (128, 512), F32)
        psC = psum("psC", (128, 512), F32)
        psD = psum("psD", (128, 512), F32)
        psE = psum("psE", (128, 640), F32)
        psF1 = psum("psF1", (128, 128), BF16)
        psF2 = psum("psF2", (128, 128), BF16)

        b = _Builder()

        for dst, src in ((ident_sb, identb), (onesc_sb, onesc),
                         (onesr_sb, onesr), (gidx_sb, gidx), (sidx_sb, sidx),
                         (wrow_sb, wrow), (extw_sb, extw), (qsc_sb, qsc),
                         (ksc_sb, ksc)):
            b.op("sync", lambda e, d=dst, s=src: e.dma_start(out=d[:, :], in_=s[:, :]),
                 incs=[(LD, 16)])
        ph0_ld = LD.done

        b.op("gpsimd", lambda e: e.load_library(_mlp_lib))
        b.op("gpsimd", lambda e: e.dma_start(out=bgr_sb[:, :], in_=bgr[:, :]),
             incs=[(WSM, 16)])
        b.op("gpsimd", lambda e: e.dma_start(
            out=wga_sb[:, :].rearrange("p (c m) -> p c m", m=H),
            in_=wgaT.rearrange("(c p) m -> p c m", p=128)),
            incs=[(WSM, 16)])
        wga_done = WSM.done

        wreg_o = abf.take(8192)
        reg2_o = abf.take(6144)
        xnT_o = abf.take(16 * SL)
        q_o = abf.take(H * TPC)
        k_o = abf.take(H * SL)
        v_o = abf.take(12 * 1024)
        o_o = abf.take(H * TPC)

        wreg = abf_t[:, wreg_o:wreg_o + 8192]
        xnT = [abf_t[:, xnT_o + hc * SL: xnT_o + (hc + 1) * SL] for hc in range(16)]
        q_sb = abf_t[:, q_o:q_o + H * TPC]
        k_sb = abf_t[:, k_o:k_o + H * SL]
        v_sb = abf_t[:, v_o:v_o + 12 * 1024]
        o_sb = abf_t[:, o_o:o_o + H * TPC]

        xn2 = [abf_t[:, reg2_o + i * D: reg2_o + (i + 1) * D] for i in range(2)]
        squt = abf_t[:, reg2_o + 2 * D: reg2_o + 3 * D]
        qsq = [abf_t[:, reg2_o + i * 512: reg2_o + (i + 1) * 512] for i in range(2)]
        ksq = [abf_t[:, reg2_o + 1024 + i * 512: reg2_o + 1024 + (i + 1) * 512]
               for i in range(2)]
        bcT = abf_t[:, reg2_o + 2048: reg2_o + 2176]
        attbf = [abf_t[:, reg2_o + 2176 + i * 640: reg2_o + 2176 + (i + 1) * 640]
                 for i in range(2)]
        attT = [abf_t[:, reg2_o + 3456 + i * 640: reg2_o + 3456 + (i + 1) * 640]
                for i in range(2)]
        s_half = [abf_t[:, reg2_o + i * 1024: reg2_o + (i + 1) * 1024]
                  for i in range(2)]

        x2_o = af32.take(2 * D)
        lst_o = af32.take(16)
        rq_o = af32.take(64)
        gt_o = af32.take(64)
        rk_o = af32.take(96)
        ms_o = af32.take(1280)
        atf_o = af32.take(1280)
        st_o = af32.take(32)
        sil_o = af32.take(1024)
        x2 = [af32_t[:, x2_o + i * D: x2_o + (i + 1) * D] for i in range(2)]
        lst = af32_t[:, lst_o:lst_o + 16]
        rq_tok = af32_t[:, rq_o:rq_o + 64]
        gate_tok = af32_t[:, gt_o:gt_o + 64]
        rk_tok = af32_t[:, rk_o:rk_o + 96]
        mask_sb = af32_t[:, x2_o:x2_o + 5 * 640]
        ms = [af32_t[:, ms_o + i * 640: ms_o + (i + 1) * 640] for i in range(2)]
        attf = [af32_t[:, atf_o + i * 640: atf_o + (i + 1) * 640] for i in range(2)]
        stt = [af32_t[:, st_o + i * 16: st_o + (i + 1) * 16] for i in range(2)]
        sil = [af32_t[:, sil_o:sil_o + 512], af32_t[:, sil_o + 512:sil_o + 1024]]

        b.op("vector", lambda e: e.memset(ones128_sb[:, :], 1.0),
             incs=[(VEC, 1)])
        zz_done = VEC.done

        b.op("vector", lambda e: e.tensor_mul(sc1_sb[:, 0:1], qsc_sb[:, :],
                                              ksc_sb[:, :]),
             waits=[ph0_ld], incs=[(VEC, 1)])
        b.op("vector", lambda e: e.tensor_scalar_mul(ccol_sb[:, :],
                                                     sc1_sb[:, 0:1], 8.0),
             waits=[VEC.done], incs=[(VEC, 1)])
        ccol_done = VEC.done

        b.op("vector", lambda e: e.tensor_copy(extb_sb[:, :], extw_sb[:, :]),
             waits=[ph0_ld], incs=[(VEC, 1)])
        extb_done = VEC.done

        def _extmm(e):
            last = None
            for s in range(NSLOT):
                last = e.matmul(psC[0:1, s * EWC:(s + 1) * EWC],
                                onesc_sb[:, :],
                                extb_sb[:, s * EWC:(s + 1) * EWC])
            return last
        b.op("tensor", _extmm, waits=[extb_done], incs=[(TEN, 1)])
        extmm_done = TEN.done

        def _extred(e):
            last = None
            for s in range(NSLOT):
                last = e.reduce_sum(cext_sb[0:1, s:s + 1],
                                    psC[0:1, s * EWC:(s + 1) * EWC], axis=AX.X)
            return last
        b.op("vector", _extred, waits=[extmm_done], incs=[(VEC, 1)])

        def _extcp(e):
            last = None
            for s in range(NSLOT):
                gch, p = spec_pos[s]
                last = e.tensor_copy(wrow_sb[0:1, gch:gch + 1],
                                     cext_sb[0:1, s:s + 1])
            return last
        b.op("vector", _extcp, waits=[VEC.done], incs=[(VEC, 1)])
        wfill_done = VEC.done

        b.op("sync", lambda e: e.dma_start(out=y_full[:, :], in_=yz[:, :]),
             incs=[(LD, 16)])
        yzero_done = LD.done

        xld, xvec, xsca, xwr = [None] * NT, [None] * NT, [None] * NT, [None] * NT
        for i in range(NT):
            waits = [xvec[i - 2], xsca[i - 2]] if i >= 2 else []
            b.op("sync", lambda e, i=i: e.dma_start(
                out=x2[i % 2], in_=x_sl[i * 128:(i + 1) * 128, :]),
                waits=waits, incs=[(LD, 16)])
            xld[i] = LD.done

            sqc = 1 + 8 * (i % 2)
            b.op("scalar", lambda e, i=i, sqc=sqc: e.activation(
                squt, x2[i % 2], AF.Square, accum_out=lst[:, sqc:sqc + 1]),
                waits=[xld[i]], incs=[(SCA, 1)])
            ssq_done = SCA.done

            wv = [xld[i], ssq_done]
            if i >= 2:
                wv.append(xwr[i - 2])
            b.op("vector", lambda e, i=i: e.reduce_sum(lst[:, 0:1], x2[i % 2],
                                                       axis=AX.X),
                 waits=wv, incs=[(VEC, 1)])
            b.op("vector", lambda e: e.tensor_scalar_mul(lst[:, 2:3],
                                                         lst[:, 0:1], 1.0 / D),
                 waits=[VEC.done], incs=[(VEC, 1)])
            b.op("vector", lambda e, sqc=sqc: e.tensor_scalar_mul(
                lst[:, 3:4], lst[:, sqc:sqc + 1], 1.0 / D),
                waits=[VEC.done], incs=[(VEC, 1)])
            b.op("vector", lambda e: e.tensor_mul(lst[:, 4:5], lst[:, 2:3],
                                                  lst[:, 2:3]),
                 waits=[VEC.done], incs=[(VEC, 1)])
            b.op("vector", lambda e: e.tensor_sub(lst[:, 5:6], lst[:, 3:4],
                                                  lst[:, 4:5]),
                 waits=[VEC.done], incs=[(VEC, 1)])
            var_done = VEC.done

            b.op("vector", lambda e: e.tensor_scalar_add(
                lst[:, 8:9], lst[:, 5:6], 1e-5),
                waits=[var_done], incs=[(VEC, 1)])
            epsd = VEC.done
            b.op("scalar", lambda e: e.activation(
                lst[:, 6:7], lst[:, 8:9], AF.Sqrt),
                waits=[epsd], incs=[(SCA, 1)])
            xsca[i] = SCA.done

            b.op("vector", lambda e: e.reciprocal(lst[:, 7:8], lst[:, 6:7]),
                 waits=[xsca[i]], incs=[(VEC, 1)])
            rstd_done = VEC.done
            b.op("vector", lambda e, i=i: e.tensor_scalar(
                xn2[i % 2], x2[i % 2], lst[:, 2:3], lst[:, 7:8],
                op0=ALU.subtract, op1=ALU.mult),
                waits=[rstd_done], incs=[(VEC, 1)])
            xvec[i] = VEC.done

            def _lnw(e, i=i):
                ins = e.dma_start(out=xn_loc[i * 128:(i + 1) * 128, :],
                                  in_=xn2[i % 2])
                ins.then_inc(XLW.h, 16)
                if i >= 4:
                    ins = e.dma_start(out=ag_in[(i - 4) * 128:(i - 3) * 128, :],
                                      in_=xn2[i % 2])
                    ins.then_inc(AGW.h, 16)
                return ins
            XLW.n += 16
            if i >= 4:
                AGW.n += 16
            b.op("sync", _lnw, waits=[xvec[i]])
            xwr[i] = (XLW, XLW.n)
        ln_done = (VEC, VEC.n)
        ln_sca_done = (SCA, SCA.n)

        def _trd(e):
            last = None
            for hc in range(16):
                last = e.dma_start(out=xnT[hc],
                                   in_=xn_loc[:, hc * 128:(hc + 1) * 128],
                                   transpose=True)
                last.then_inc(TRS.h, 16)
            return last
        TRS.n += 16 * 16
        b.op("sync", _trd, waits=[(XLW, XLW.n)])
        xnT_done = TRS.done

        b.op("gpsimd", lambda e: e.collective_compute(
            "AllGather", ALU.bypass, replica_groups=[list(range(NCORES))],
            ins=[ag_in.opt()], outs=[ag_out.opt()]),
            waits=[(AGW, AGW.n)], incs=[(CC, 1)])
        ag_done = CC.done

        if SKIP_ATTN:
            s_done = (SSP, 0)
            attn_skip = True
        else:
            attn_skip = False
        if not attn_skip:
            prev_half_last = None
            ev_q = {}
            for half in range(2):
                wsrc = wqT.rearrange("(c p) m -> p c m", p=128)[:, :, half * 512:(half + 1) * 512]
                wts = [] if half == 0 else [prev_half_last]
                b.op("gpsimd", lambda e, s=wsrc: e.dma_start(
                    out=wreg[:, :].rearrange("p (c m) -> p c m", m=512), in_=s),
                    waits=wts, incs=[(WSM, 16)])
                wq_ld = WSM.done
                for hh in range(4):
                    h = half * 4 + hh
                    for bb in range(2):
                        u = h * 2 + bb
                        ps = psA if u % 2 == 0 else psB

                        def _qmm(e, hh=hh, bb=bb, ps=ps):
                            last = None
                            for hc in range(16):
                                last = e.matmul(
                                    ps[:, :],
                                    wreg[:, hc * 512 + hh * 128: hc * 512 + (hh + 1) * 128],
                                    xnT[hc][:, WIN + bb * 512: WIN + (bb + 1) * 512],
                                    start=(hc == 0), stop=(hc == 15))
                            return last
                        wts = [wq_ld, ccol_done, xnT_done]
                        if u >= 2:
                            wts.append(ev_q[u - 2])
                        b.op("tensor", _qmm, waits=wts, incs=[(TEN, 1)])
                        mm_done = TEN.done

                        b.op("vector", lambda e, h=h, bb=bb, ps=ps: e.tensor_scalar_mul(
                            q_sb[:, h * TPC + bb * 512: h * TPC + (bb + 1) * 512],
                            ps[:, :], ccol_sb[:, :]),
                            waits=[mm_done], incs=[(VEC, 1)])
                        evd = VEC.done

                        b.op("scalar", lambda e, bb=bb, ps=ps: e.activation(
                            qsq[bb], ps[:, :], AF.Square),
                            waits=[evd], incs=[(SCA, 1)])
                        sqd = SCA.done

                        def _ssmm(e, bb=bb):
                            last = None
                            for j in range(4):
                                last = e.matmul(psC[:, j:j + 1],
                                                qsq[bb][:, j * 128:(j + 1) * 128],
                                                onesc_sb[:, :])
                            return last
                        b.op("tensor", _ssmm, waits=[sqd], incs=[(TEN, 1)])
                        ssd = TEN.done

                        def _ssev(e, h=h, bb=bb):
                            last = None
                            for j in range(4):
                                t = bb * 4 + j
                                last = e.tensor_copy(
                                    rq_tok[:, t * 8 + h: t * 8 + h + 1],
                                    psC[:, j:j + 1])
                            return last
                        b.op("vector", _ssev, waits=[ssd], incs=[(VEC, 1)])
                        ev_q[u] = VEC.done
                prev_half_last = ev_q[half * 8 + 7]

            b.op("scalar", lambda e: e.activation(rq_tok[:, :], rq_tok[:, :], AF.Sqrt),
                 waits=[ev_q[15]], incs=[(SCA, 1)])

            b.op("vector", lambda e: e.tensor_scalar_max(
                rq_tok[:, :], rq_tok[:, :], 1e-12),
                waits=[SCA.done], incs=[(VEC, 1)])
            b.op("vector", lambda e: e.reciprocal(rq_tok[:, :], rq_tok[:, :]),
                 waits=[VEC.done], incs=[(VEC, 1)])
            rq_done = VEC.done

            prev_gate = None
            for t in range(8):
                def _gmm(e, t=t):
                    for hc in range(16):
                        e.matmul(psD[:, 0:8],
                                 xnT[hc][:, WIN + t * 128: WIN + (t + 1) * 128],
                                 wga_sb[:, hc * H:(hc + 1) * H],
                                 start=(hc == 0), stop=False)
                    return e.matmul(psD[:, 0:8], onesr_sb[:, :], bgr_sb[:, :],
                                    start=False, stop=True)
                wts = [xnT_done, wga_done, ph0_ld]
                if prev_gate is not None:
                    wts.append(prev_gate)
                b.op("tensor", _gmm, waits=wts, incs=[(TEN, 1)])
                b.op("scalar", lambda e, t=t: e.activation(
                    gate_tok[:, t * 8:(t + 1) * 8], psD[:, 0:8], AF.Sigmoid),
                    waits=[TEN.done], incs=[(SCA, 1)])
                prev_gate = SCA.done
            gates_done = prev_gate

            ev_k = {}
            for half in range(2):
                wsrc = wkvT.rearrange("(c p) m -> p c m", p=128)[:, :, half * 512:(half + 1) * 512]
                b.op("gpsimd", lambda e, s=wsrc: e.dma_start(
                    out=wreg[:, :].rearrange("p (c m) -> p c m", m=512), in_=s),
                    waits=[ev_q[15] if half == 0 else ev_k[11]],
                    incs=[(WSM, 16)])
                wk_ld = WSM.done
                for hh in range(4):
                    h = half * 4 + hh
                    for b3 in range(3):
                        u = h * 3 + b3
                        ps = psA if u % 2 == 0 else psB

                        def _kmm(e, hh=hh, b3=b3, ps=ps):
                            last = None
                            for hc in range(16):
                                last = e.matmul(
                                    ps[:, :],
                                    wreg[:, hc * 512 + hh * 128: hc * 512 + (hh + 1) * 128],
                                    xnT[hc][:, b3 * 512:(b3 + 1) * 512],
                                    start=(hc == 0), stop=(hc == 15))
                            return last
                        wts = [wk_ld]
                        if u >= 2:
                            wts.append(ev_k[u - 2])
                        b.op("tensor", _kmm, waits=wts, incs=[(TEN, 1)])
                        mm_done = TEN.done

                        b.op("vector", lambda e, h=h, b3=b3, ps=ps: e.tensor_copy(
                            k_sb[:, h * SL + b3 * 512: h * SL + (b3 + 1) * 512],
                            ps[:, :]),
                            waits=[mm_done], incs=[(VEC, 1)])
                        evd = VEC.done

                        b.op("scalar", lambda e, b3=b3, ps=ps: e.activation(
                            ksq[b3 % 2], ps[:, :], AF.Square),
                            waits=[evd], incs=[(SCA, 1)])
                        sqd = SCA.done

                        def _ssmm(e, b3=b3):
                            last = None
                            for j in range(4):
                                last = e.matmul(psC[:, j:j + 1],
                                                ksq[b3 % 2][:, j * 128:(j + 1) * 128],
                                                onesc_sb[:, :])
                            return last
                        b.op("tensor", _ssmm, waits=[sqd], incs=[(TEN, 1)])
                        ssd = TEN.done

                        def _ssev(e, h=h, b3=b3):
                            last = None
                            for j in range(4):
                                ch = b3 * 4 + j
                                last = e.tensor_copy(
                                    rk_tok[:, ch * 8 + h: ch * 8 + h + 1],
                                    psC[:, j:j + 1])
                            return last
                        b.op("vector", _ssev, waits=[ssd], incs=[(VEC, 1)])
                        ev_k[u] = VEC.done
            k_done = ev_k[23]

            ev_v = {}
            for half in range(2):
                wsrc = wkvT.rearrange("(c p) m -> p c m", p=128)[:, :, 1024 + half * 512: 1024 + (half + 1) * 512]
                b.op("gpsimd", lambda e, s=wsrc: e.dma_start(
                    out=wreg[:, :].rearrange("p (c m) -> p c m", m=512), in_=s),
                    waits=[k_done if half == 0 else ev_v[11]],
                    incs=[(WSM, 16)])
                wv_ld = WSM.done
                for ch in range(12):
                    u = half * 12 + ch
                    ps = psA if u % 2 == 0 else psB

                    def _vmm(e, ch=ch, ps=ps):
                        last = None
                        for hc in range(16):
                            last = e.matmul(
                                ps[:, :],
                                xnT[hc][:, ch * 128:(ch + 1) * 128],
                                wreg[:, hc * 512:(hc + 1) * 512],
                                start=(hc == 0), stop=(hc == 15))
                        return last
                    wts = [wv_ld]
                    if u >= 2:
                        wts.append(ev_v[u - 2])
                    b.op("tensor", _vmm, waits=wts, incs=[(TEN, 1)])
                    b.op("vector", lambda e, ch=ch, half=half, ps=ps: e.tensor_copy(
                        v_sb[:, ch * 1024 + half * 512: ch * 1024 + (half + 1) * 512],
                        ps[:, :]),
                        waits=[TEN.done], incs=[(VEC, 1)])
                    ev_v[u] = VEC.done
            v_done = ev_v[23]

            b.op("scalar", lambda e: e.activation(rk_tok[:, :], rk_tok[:, :], AF.Sqrt),
                 waits=[k_done], incs=[(SCA, 1)])

            b.op("vector", lambda e: e.tensor_scalar_max(
                rk_tok[:, :], rk_tok[:, :], 1e-12),
                waits=[SCA.done], incs=[(VEC, 1)])
            b.op("vector", lambda e: e.reciprocal(rk_tok[:, :], rk_tok[:, :]),
                 waits=[VEC.done], incs=[(VEC, 1)])
            rk_done = VEC.done

            prev_kh = {}
            for h in range(H):
                for ch in range(12):
                    u = h * 12 + ch
                    psf = psF1 if u % 2 == 0 else psF2

                    wts = [rk_done, zz_done]
                    if u >= 1:
                        wts.append(prev_kh[u - 1][0])
                    b.op("vector", lambda e, h=h, ch=ch: e.tensor_scalar_mul(
                        bcT, ones128_sb[:, :],
                        rk_tok[:, (ch * 8 + h):(ch * 8 + h) + 1]),
                        waits=wts, incs=[(VEC, 1)])
                    bct_done = VEC.done

                    wts = [bct_done, ph0_ld]
                    if u >= 2:
                        wts.append(prev_kh[u - 2][1])
                    if u == 0:
                        wts.append(gates_done)
                    b.op("tensor", lambda e, psf=psf: e.transpose(
                        psf[:, :], bcT, ident_sb[:, :]),
                        waits=wts, incs=[(TEN, 1)])
                    tr_done = TEN.done

                    b.op("vector", lambda e, h=h, ch=ch, psf=psf: e.tensor_mul(
                        k_sb[:, h * SL + ch * 128: h * SL + (ch + 1) * 128],
                        k_sb[:, h * SL + ch * 128: h * SL + (ch + 1) * 128],
                        psf[:, :]),
                        waits=[tr_done], incs=[(VEC, 1)])
                    prev_kh[u] = (tr_done, VEC.done)
            kh_done = prev_kh[H * 12 - 1][1]

            b.op("sync", lambda e: e.dma_start(
                out=mask_sb[:, :].rearrange("p (c m) -> p c m", m=640),
                in_=maskt.rearrange("c p m -> p c m")),
                waits=[ln_done, ln_sca_done], incs=[(LD, 16)])
            mask_done = LD.done

            s_phase_guard = v_done
            prev_madd = {}
            prev_exp2 = {}
            prev_oev = None
            prev_t_ev = {}
            for h in range(H):
                for t in range(8):
                    u = h * 8 + t
                    pp = u % 2
                    msP, attfP, sttP = ms[pp], attf[pp], stt[pp]
                    attbfP, attTP = attbf[pp], attT[pp]

                    if pp == 0:
                        def _simm(e, h=h, t=t):
                            e.matmul(psE[:, 0:512],
                                     q_sb[:, h * TPC + t * 128: h * TPC + (t + 1) * 128],
                                     k_sb[:, h * SL + t * 128: h * SL + t * 128 + 512],
                                     start=True, stop=True)
                            return e.matmul(
                                psE[:, 512:640],
                                q_sb[:, h * TPC + t * 128: h * TPC + (t + 1) * 128],
                                k_sb[:, h * SL + t * 128 + 512: h * SL + t * 128 + 640],
                                start=True, stop=True)
                    else:
                        def _simm(e, h=h, t=t):
                            e.matmul(psA[:, :],
                                     q_sb[:, h * TPC + t * 128: h * TPC + (t + 1) * 128],
                                     k_sb[:, h * SL + t * 128: h * SL + t * 128 + 512],
                                     start=True, stop=True)
                            return e.matmul(
                                psB[:, 0:128],
                                q_sb[:, h * TPC + t * 128: h * TPC + (t + 1) * 128],
                                k_sb[:, h * SL + t * 128 + 512: h * SL + t * 128 + 640],
                                start=True, stop=True)
                    wts = [kh_done, mask_done]
                    if u == 0 or u == 1:
                        wts.append(s_phase_guard)
                    if u - 2 in prev_madd:
                        wts.append(prev_madd[u - 2])
                    b.op("tensor", _simm, waits=wts, incs=[(TEN, 1)])
                    sim_done = TEN.done

                    mslot = t if t < 4 else 4
                    wts = [sim_done, rq_done]
                    if u - 2 in prev_exp2:
                        wts.append(prev_exp2[u - 2])
                    if pp == 0:
                        b.op("vector", lambda e, mslot=mslot, msP=msP: e.tensor_add(
                            msP, psE[:, :],
                            mask_sb[:, mslot * 640:(mslot + 1) * 640]),
                            waits=wts, incs=[(VEC, 1)])
                    else:
                        b.op("vector", lambda e, mslot=mslot, msP=msP: e.tensor_add(
                            msP[:, 0:512], psA[:, :],
                            mask_sb[:, mslot * 640: mslot * 640 + 512]),
                            waits=wts, incs=[(VEC, 1)])
                        b.op("vector", lambda e, mslot=mslot, msP=msP: e.tensor_add(
                            msP[:, 512:640], psB[:, 0:128],
                            mask_sb[:, mslot * 640 + 512:(mslot + 1) * 640]),
                            incs=[(VEC, 1)])
                    prev_madd[u] = VEC.done
                    b.op("vector", lambda e, msP=msP, sttP=sttP: e.reduce_max(
                        sttP[:, 0:1], msP, axis=AX.X, negate=True),
                        waits=[VEC.done], incs=[(VEC, 1)])
                    b.op("vector", lambda e, h=h, t=t, sttP=sttP: e.tensor_mul(
                        sttP[:, 1:2], sttP[:, 0:1],
                        rq_tok[:, t * 8 + h: t * 8 + h + 1]),
                        waits=[VEC.done], incs=[(VEC, 1)])
                    bias_done = VEC.done

                    b.op("scalar", lambda e, h=h, t=t, msP=msP, attfP=attfP,
                         sttP=sttP: e.activation(
                             attfP, msP, AF.Exp, bias=sttP[:, 1:2],
                             scale=rq_tok[:, t * 8 + h: t * 8 + h + 1],
                             accum_out=sttP[:, 2:3]),
                         waits=[bias_done], incs=[(SCA, 1)])
                    prev_exp2[u] = SCA.done
                    exp_done = SCA.done

                    b.op("vector", lambda e, sttP=sttP: e.reciprocal(
                        sttP[:, 3:4], sttP[:, 2:3]),
                        waits=[exp_done], incs=[(VEC, 1)])
                    b.op("vector", lambda e, h=h, t=t, attfP=attfP, sttP=sttP,
                         attbfP=attbfP: e.tensor_scalar(
                             attbfP, attfP, sttP[:, 3:4],
                             gate_tok[:, t * 8 + h: t * 8 + h + 1],
                             op0=ALU.mult, op1=ALU.mult),
                         waits=[VEC.done], incs=[(VEC, 1)])
                    attbf_done = VEC.done

                    tev = {}
                    for kk in range(5):
                        psf = psF1 if kk % 2 == 0 else psF2
                        wts = [attbf_done]
                        if kk >= 2:
                            wts.append(tev[kk - 2])
                        elif u > 0:
                            wts.append(prev_t_ev[(u - 1, 4 if kk == 0 else 3)])
                        b.op("tensor", lambda e, kk=kk, psf=psf, attbfP=attbfP:
                             e.transpose(psf[:, :],
                                         attbfP[:, kk * 128:(kk + 1) * 128],
                                         ident_sb[:, :]),
                             waits=wts, incs=[(TEN, 1)])
                        b.op("vector", lambda e, kk=kk, psf=psf, attTP=attTP:
                             e.tensor_copy(attTP[:, kk * 128:(kk + 1) * 128],
                                           psf[:, :]),
                             waits=[TEN.done], incs=[(VEC, 1)])
                        tev[kk] = VEC.done
                        prev_t_ev[(u, kk)] = VEC.done

                    def _avmm(e, h=h, t=t, attTP=attTP):
                        last = None
                        for kk in range(5):
                            last = e.matmul(
                                psD[:, 0:128],
                                v_sb[:, (t + kk) * 1024 + h * 128:
                                     (t + kk) * 1024 + (h + 1) * 128],
                                attTP[:, kk * 128:(kk + 1) * 128],
                                start=(kk == 0), stop=(kk == 4))
                        return last
                    wts = [tev[4], v_done]
                    if u == 0:
                        wts.append(gates_done)
                    if prev_oev is not None:
                        wts.append(prev_oev)
                    b.op("tensor", _avmm, waits=wts, incs=[(TEN, 1)])
                    b.op("vector", lambda e, h=h, t=t: e.tensor_copy(
                        o_sb[:, h * TPC + t * 128: h * TPC + (t + 1) * 128],
                        psD[:, 0:128]),
                        waits=[TEN.done], incs=[(VEC, 1)])
                    prev_oev = VEC.done
            attn_done = prev_oev

            prev_sev = {}
            for half in range(2):
                wsrc = woT.rearrange("(c p) m -> p c m", p=128)[:, :, half * 1024:(half + 1) * 1024]
                b.op("gpsimd", lambda e, s=wsrc: e.dma_start(
                    out=wreg[:, :].rearrange("p (c m) -> p c m", m=1024), in_=s),
                    waits=[v_done if half == 0 else prev_sev[(0, 15)]],
                    incs=[(WSM, 16)])
                wo_ld = WSM.done
                for t in range(8):
                    for dci in range(2):
                        u = t * 2 + dci
                        ps = psA if u % 2 == 0 else psB

                        def _smm(e, t=t, dci=dci, ps=ps):
                            last = None
                            for ic in range(8):
                                last = e.matmul(
                                    ps[:, :],
                                    o_sb[:, ic * TPC + t * 128: ic * TPC + (t + 1) * 128],
                                    wreg[:, ic * 1024 + dci * 512: ic * 1024 + (dci + 1) * 512],
                                    start=(ic == 0), stop=(ic == 7))
                            return last
                        wts = [wo_ld, attn_done]
                        if u >= 2:
                            wts.append(prev_sev[(half, u - 2)])
                        b.op("tensor", _smm, waits=wts, incs=[(TEN, 1)])
                        b.op("vector", lambda e, t=t, dci=dci, ps=ps: e.tensor_copy(
                            s_half[t % 2][:, dci * 512:(dci + 1) * 512], ps[:, :]),
                            waits=[TEN.done, (SSP, max(0, SSP.n - 16))],
                            incs=[(VEC, 1)])
                        prev_sev[(half, u)] = VEC.done
                    b.op("sync", lambda e, t=t, half=half: e.dma_start(
                        out=s_spill[t * 128:(t + 1) * 128,
                                    half * 1024:(half + 1) * 1024],
                        in_=s_half[t % 2]),
                        waits=[prev_sev[(half, t * 2 + 1)]], incs=[(SSP, 16)])
            s_done = SSP.done

        if SKIP_MOE:
            rs_done = (CC, CC.n)
            moe_skip = True
        else:
            moe_skip = False
        abf.reset(0)
        if not moe_skip:
            wg_o = abf.take(16 * 1024)
            wu_o = abf.take(16 * 1024)
            wd_o = abf.take(16 * 1024)
            xs_o = abf.take(2 * 16 * 512)
            mid_o = abf.take(8 * 128 * RSMAX)
            or_o = abf.take(2 * 2 * D)
            wg_sb = abf_t[:, wg_o:wg_o + 16 * 1024]
            wu_sb = abf_t[:, wu_o:wu_o + 16 * 1024]
            wd_sb = abf_t[:, wd_o:wd_o + 16 * 1024]
            xsT = [abf_t[:, xs_o + i * 8192: xs_o + (i + 1) * 8192] for i in range(2)]
            orow = [abf_t[:, or_o + i * 2 * D: or_o + (i + 1) * 2 * D] for i in range(2)]

            p2_bar = [(TEN, TEN.n), (VEC, VEC.n), (SCA, SCA.n), (SSP, SSP.n),
                  (TRS, TRS.n), (LD, LD.n)]

            slot_batches = [[bt for bt in batches if bt[0] == s] for s in range(NSLOT)]
            gather_ten = {}
            prev_scatter = None
            last_gu_mm = None
            last_dn_mm = None
            bid_global = 0
            ch_base = [0]
            for s in range(1, NSLOT):
                ch_base.append(ch_base[-1] + RS[s - 1])

            for s in range(NSLOT):
                wgsrc = wgS[s].rearrange("(c p) m -> p c m", p=128)
                wusrc = wuS[s].rearrange("(c p) m -> p c m", p=128)
                wdsrc = wdS[s].rearrange("(c p) m -> p c m", p=128)
                wts_w = list(p2_bar) if s == 0 else [last_gu_mm]
                b.op("gpsimd", lambda e, src=wgsrc: e.dma_start(
                    out=wg_sb[:, :].rearrange("p (c m) -> p c m", m=1024), in_=src),
                    waits=wts_w, incs=[(WSM, 16)])
                b.op("gpsimd", lambda e, src=wusrc: e.dma_start(
                    out=wu_sb[:, :].rearrange("p (c m) -> p c m", m=1024), in_=src),
                    incs=[(WSM, 16)])
                wgu_ld = WSM.done
                wts_d = [] if s == 0 else [last_dn_mm]
                b.op("gpsimd", lambda e, src=wdsrc: e.dma_start(
                    out=wd_sb[:, :].rearrange("p (c m) -> p c m", m=1024), in_=src),
                    waits=wts_d, incs=[(WSM, 16)])
                wd_ld = WSM.done

                dn_barrier = last_dn_mm
                prev_ev = {}
                for bi, (s_, gch0, nb) in enumerate(slot_batches[s]):
                    nidx = nb * 128
                    buf = bid_global % 2
                    wts_g = [ag_done, ph0_ld]
                    if s == 0 and bi == 0:
                        wts_g += p2_bar
                    if bid_global >= 2:
                        wts_g.append(gather_ten[bid_global - 2])
                    b.op("gpsimd", lambda e, buf=buf, gch0=gch0, nidx=nidx: e.dma_gather(
                        xsT[buf][:, 0:16 * nidx].rearrange("p (f i) -> p f i", f=16),
                        ag_out[:, :],
                        gidx_sb[:, gch0 * 8: gch0 * 8 + (nidx + 15) // 16],
                        nidx, nidx, 2048, transpose=True),
                        waits=wts_g, incs=[(GSM, 16)])
                    g_done = GSM.done

                    lc0 = gch0 - ch_base[s]
                    for ic in range(8):
                        psg = psA if ic % 2 == 0 else psC
                        psu = psB if ic % 2 == 0 else psD

                        def _gumm(e, buf=buf, ic=ic, nidx=nidx, psg=psg, psu=psu):
                            for hc in range(16):
                                e.matmul(psg[:, 0:nidx],
                                         wg_sb[:, hc * 1024 + ic * 128: hc * 1024 + (ic + 1) * 128],
                                         xsT[buf][:, hc * nidx:(hc + 1) * nidx],
                                         start=(hc == 0), stop=(hc == 15))
                            last = None
                            for hc in range(16):
                                last = e.matmul(psu[:, 0:nidx],
                                                wu_sb[:, hc * 1024 + ic * 128: hc * 1024 + (ic + 1) * 128],
                                                xsT[buf][:, hc * nidx:(hc + 1) * nidx],
                                                start=(hc == 0), stop=(hc == 15))
                            return last
                        wts = [g_done, wgu_ld]
                        if ic >= 2:
                            wts.append(prev_ev[(bi, ic - 2)])
                        elif bi > 0:
                            wts.append(prev_ev[(bi - 1, 6 + ic)])
                        b.op("tensor", _gumm, waits=wts, incs=[(TEN, 1)])
                        gu_done = TEN.done
                        if ic == 7:
                            gather_ten[bid_global] = gu_done
                            last_gu_mm = gu_done

                        sb_ = sil[ic % 2]
                        b.op("scalar", lambda e, nidx=nidx, psg=psg, sb_=sb_: e.activation(
                            sb_[:, 0:nidx], psg[:, 0:nidx], AF.Silu),
                            waits=[gu_done], incs=[(SCA, 1)])
                        sil_done = SCA.done

                        mo = mid_o + ic * 128 * RS[s] + lc0 * 128
                        wts_m = [sil_done]
                        if dn_barrier is not None and bi == 0:
                            wts_m.append(dn_barrier)
                        b.op("vector", lambda e, mo=mo, nidx=nidx, psu=psu, sb_=sb_:
                             e.tensor_mul(abf_t[:, mo:mo + nidx],
                                          sb_[:, 0:nidx], psu[:, 0:nidx]),
                             waits=wts_m, incs=[(VEC, 1)])
                        prev_ev[(bi, ic)] = VEC.done
                    bid_global += 1
                mid_done_of = {}
                for bi2, (s2, gch02, nb2) in enumerate(slot_batches[s]):
                    for lc2 in range(gch02 - ch_base[s], gch02 - ch_base[s] + nb2):
                        mid_done_of[lc2] = prev_ev[(bi2, 7)]
                mid_done = prev_ev[(len(slot_batches[s]) - 1, 7)]

                assert RS[s] % 2 == 0
                prev_dn = {}
                for rc in range(RS[s]):
                    gch = ch_base[s] + rc
                    grp = gch // 2
                    ob = orow[grp % 2]
                    oco = (rc % 2) * D
                    for dc in range(4):
                        ps = psA if dc % 2 == 0 else psB

                        def _dmm(e, rc=rc, dc=dc, ps=ps, s=s):
                            last = None
                            for ic in range(8):
                                last = e.matmul(
                                    ps[:, :],
                                    abf_t[:, mid_o + ic * 128 * RS[s] + rc * 128:
                                          mid_o + ic * 128 * RS[s] + (rc + 1) * 128],
                                    wd_sb[:, ic * 2048 + dc * 512: ic * 2048 + (dc + 1) * 512],
                                    start=(ic == 0), stop=(ic == 7))
                            return last
                        wts = [mid_done, wd_ld]
                        if rc > 0 or dc > 1:
                            pk = (rc, dc - 2) if dc >= 2 else (rc - 1, 2 + dc)
                            wts.append(prev_dn[pk])
                        b.op("tensor", _dmm, waits=wts, incs=[(TEN, 1)])
                        dn_done = TEN.done
                        if rc == RS[s] - 1 and dc == 3:
                            last_dn_mm = dn_done

                        wts_e = [dn_done, wfill_done]
                        if dc == 0 and rc % 2 == 0 and grp >= 2:
                            wts_e.append((SSM, SSM.n - 16))
                        b.op("vector", lambda e, gch=gch, dc=dc, ps=ps, ob=ob, oco=oco:
                             e.tensor_scalar_mul(ob[:, oco + dc * 512: oco + (dc + 1) * 512],
                                                 ps[:, :],
                                                 wrow_sb[:, gch:gch + 1]),
                             waits=wts_e, incs=[(VEC, 1)])
                        prev_dn[(rc, dc)] = VEC.done

                    if rc % 2 == 1:
                        wts_s = [prev_dn[(rc - 1, 3)], prev_dn[(rc, 3)], yzero_done]
                        if prev_scatter is not None:
                            wts_s.append(prev_scatter)
                        g0 = gch - 1
                        b.op("gpsimd", lambda e, g0=g0, ob=ob: e.dma_scatter_add(
                            y_full[:, :],
                            ob[:, :].rearrange("p (a m) -> p a m", a=2),
                            sidx_sb[:, g0 * 8:(g0 + 2) * 8], 256, 256, 2048),
                            waits=wts_s, incs=[(SSM, 16)])
                        prev_scatter = SSM.done

            b.op("sync", lambda e: e.dma_start(out=y_main[:, :],
                                               in_=y_full[0:NTOK, :]),
                 waits=[prev_scatter], incs=[(LD, 16)])
            b.op("gpsimd", lambda e: e.collective_compute(
                "ReduceScatter", ALU.add, replica_groups=[list(range(NCORES))],
                ins=[y_main.opt()], outs=[y_red.opt()]),
                waits=[LD.done], incs=[(CC, 1)])
            rs_done = CC.done

        abf.reset(0)
        yb_o = abf.take(2 * D)
        sb_o = abf.take(2 * D)
        yt = [abf_t[:, yb_o + i * D: yb_o + (i + 1) * D] for i in range(2)]
        st2 = [abf_t[:, sb_o + i * D: sb_o + (i + 1) * D] for i in range(2)]
        af32.reset(0)
        xf_o = af32.take(2 * D)
        g_o = af32.take(D)
        xt = [af32_t[:, xf_o + i * D: xf_o + (i + 1) * D] for i in range(2)]
        gsl = af32_t[:, g_o:g_o + D]

        prev_vec5 = {}
        prev_sca5 = {}
        for tt in range(8):
            buf = tt % 2

            def _fl(e, tt=tt, buf=buf):
                return e.dma_start(out=yt[buf],
                                   in_=y_red[tt * 128:(tt + 1) * 128, :])
            def _fl2(e, tt=tt, buf=buf):
                e.dma_start(out=st2[buf],
                            in_=s_spill[tt * 128:(tt + 1) * 128, :]).then_inc(LD2.h, 16)
                return e.dma_start(out=xt[buf],
                                   in_=x_sl[WIN + tt * 128: WIN + (tt + 1) * 128, :])
            LD2.n += 16
            wts = [rs_done, s_done]
            if tt >= 2:
                wts += [prev_vec5[tt - 2], prev_sca5[tt - 2]]
            b.op("sync", _fl, waits=wts, incs=[(LD, 16)])
            b.op("scalar", _fl2, waits=wts, incs=[(LD2, 16)])
            fl_done = LD.done
            fl2_done = LD2.done

            b.op("scalar", lambda e, buf=buf: e.activation(gsl, yt[buf], AF.Silu),
                 waits=[fl_done, fl2_done] + ([prev_vec5[tt - 1]] if tt >= 1 else []),
                 incs=[(SCA, 1)])
            prev_sca5[tt] = SCA.done

            b.op("vector", lambda e, buf=buf: e.tensor_mul(gsl, gsl, st2[buf]),
                 waits=[prev_sca5[tt], fl_done, fl2_done], incs=[(VEC, 1)])
            b.op("vector", lambda e, buf=buf: e.tensor_add(xt[buf], xt[buf], gsl),
                 waits=[VEC.done], incs=[(VEC, 1)])
            prev_vec5[tt] = VEC.done

            b.op("sync", lambda e, tt=tt, buf=buf: e.dma_start(
                out=out[tt * 128:(tt + 1) * 128, :], in_=xt[buf]),
                waits=[prev_vec5[tt]], incs=[(OS, 16)])

        if DEBUG_DUMPS:
            def _dbg(e):
                e.dma_start(out=dbg["d_xn"][:, :], in_=xn_loc[:, :]).then_inc(OS.h, 16)
                e.dma_start(out=dbg["d_ag"][:, :], in_=ag_out[:, :]).then_inc(OS.h, 16)
                e.dma_start(out=dbg["d_s"][:, :], in_=s_spill[:, :]).then_inc(OS.h, 16)
                e.dma_start(out=dbg["d_ym"][:, :], in_=y_main[:, :]).then_inc(OS.h, 16)
                return e.dma_start(out=dbg["d_yr"][:, :], in_=y_red[:, :])
            OS.n += 4 * 16
            b.op("sync", _dbg, waits=[rs_done, (OS, OS.n - 4 * 16)], incs=[(OS, 16)])
        b.op("sync", lambda e: e.nop(), waits=[(OS, OS.n)])

        with nc.Block() as block:
            b.emit(block)

    return nc



def kernel(**inputs):
    x = np.ascontiguousarray(np.asarray(inputs["L0_attn_x"])[0], np.float32)
    Wq = np.asarray(inputs["L0_attn_Wq"], np.float32)
    Wkv = np.asarray(inputs["L0_attn_Wkv"], np.float32)
    q_scale = np.asarray(inputs["L0_attn_q_scale"], np.float32)
    k_scale = np.asarray(inputs["L0_attn_k_scale"], np.float32)
    Wg = np.asarray(inputs["L0_attn_Wg"], np.float32)
    bg = np.asarray(inputs["L0_attn_bg"], np.float32)
    Wo = np.asarray(inputs["L0_attn_Wo"], np.float32)
    gate_w = np.asarray(inputs["L0_ffn_gate_weight"], np.float32)
    up_w = np.asarray(inputs["L0_ffn_up_weight"], np.float32)
    down_w = np.asarray(inputs["L0_ffn_down_weight"], np.float32)
    topk_idx = np.asarray(inputs["L0_ffn_topk_idx"])
    topk_weight = np.asarray(inputs["L0_ffn_topk_weight"], np.float32)

    experts, extw = _route(topk_idx, topk_weight)
    meta, cores = _plan(experts, extw)
    assert max(meta["RS"]) <= 14, "unexpectedly large expert group"
    assert meta["EWC"] <= 16

    nc = _build(meta)
    mybir.codegen_inst_isa_subclasses(nc)

    wqT = np.ascontiguousarray(Wq.T)
    wkvT = np.ascontiguousarray(Wkv.T)
    wgaT = np.ascontiguousarray(Wg.T)
    woT = np.ascontiguousarray(Wo.T)
    bgr = np.ascontiguousarray(bg.reshape(1, H))
    qsc = np.ascontiguousarray(q_scale.reshape(128, 1))
    ksc = np.ascontiguousarray(k_scale.reshape(128, 1))
    identb = np.eye(128, dtype=BF)
    onesc = np.ones((128, 1), BF)
    onesr = np.ones((1, 128), BF)

    qi = np.arange(128)[:, None]
    jj = np.arange(640)[None, :]
    base_ok = (jj >= qi) & (jj <= qi + WIN)
    masks0 = np.zeros((5, 128, 640), np.float32)
    for t in range(4):
        ok = base_ok & ((t * 128 + jj - WIN) >= 0)
        masks0[t] = np.where(ok, 0.0, MASKVAL)
    masks0[4] = np.where(base_ok, 0.0, MASKVAL)
    masks_other = np.broadcast_to(masks0[4], (5, 128, 640)).copy()
    masks_other = np.ascontiguousarray(masks_other)

    yzero_host = np.zeros((YROWS, D), BF)
    in_maps = []
    for c in range(NCORES):
        lo = c * TPC - WIN
        if lo < 0:
            x_slc = np.concatenate(
                [np.zeros((WIN, D), np.float32), x[0:TPC]], axis=0)
        else:
            x_slc = x[lo:c * TPC + TPC]
        x_slc = np.ascontiguousarray(x_slc)
        eo = cores[c]["expert_of"]
        wgS = np.ascontiguousarray(np.stack([gate_w[e].T for e in eo]))
        wuS = np.ascontiguousarray(np.stack([up_w[e].T for e in eo]))
        wdS = np.ascontiguousarray(np.stack([down_w[e].T for e in eo]))
        in_maps.append({
            "yz": yzero_host,
            "x_sl": x_slc, "wqT": wqT, "wkvT": wkvT, "wgaT": wgaT,
            "woT": woT, "bgr": bgr, "qsc": qsc, "ksc": ksc,
            "maskt": masks0 if c == 0 else masks_other,
            "identb": identb, "onesc": onesc, "onesr": onesr,
            "wgS": wgS, "wuS": wuS, "wdS": wdS,
            "gidx": cores[c]["gidx"], "sidx": cores[c]["sidx"],
            "wrow": cores[c]["wrow"], "extw": cores[c]["extw"],
        })

    res = run_bass_kernel_spmd(nc, in_maps, core_ids=list(range(NCORES)))
    shards = [res.results[c]["out"] for c in range(NCORES)]
    full = np.concatenate(shards, axis=0).reshape(1, NTOK, D)
    return full


# version 07
# speedup vs baseline: 1.1049x; 1.0660x over previous
import numpy as np
import ml_dtypes
import concourse.bass as bass
import concourse.mybir as mybir
from concourse.bass_utils import run_bass_kernel_spmd
from concourse.library_config import mlp as _mlp_lib

F32 = mybir.dt.float32
BF16 = mybir.dt.bfloat16
I16 = mybir.dt.int16
AF = mybir.ActivationFunctionType
ALU = mybir.AluOpType
AX = mybir.AxisListType
BF = ml_dtypes.bfloat16

NCORES = 8
NTOK = 8192
D = 2048
H = 8
DH = 128
WIN = 512
NE = 32
TPC = NTOK // NCORES
SL = TPC + WIN
NT = SL // 128
NSLOT = 4
MASKVAL = -30000.0
YROWS = NTOK + 128

ENGINES = ("sync", "tensor", "vector", "scalar", "gpsimd")
DEBUG_DUMPS = False
SKIP_ATTN = False
SKIP_MOE = False



def _pack_idx(vals, ncols):
    arr = np.zeros((128, ncols), np.int16)
    j = np.arange(len(vals))
    v = np.asarray(vals, np.int16)
    arr[j % 16, j // 16] = v
    arr[16 + j % 16, j // 16] = v
    return arr


def _route(topk_idx, topk_weight):
    ti = np.asarray(topk_idx).reshape(-1).astype(np.int64)
    tw = np.asarray(topk_weight).reshape(-1).astype(np.float32)
    nreg = NTOK * 4
    reg_e, reg_w = ti[:nreg], tw[:nreg]
    reg_t = np.repeat(np.arange(NTOK), 4)
    ext_e, ext_w = ti[nreg:], tw[nreg:]

    experts = []
    for e in range(NE):
        m = reg_e == e
        toks = reg_t[m]
        ws = reg_w[m]
        rows = [(NTOK - 1, 0.0, True)]
        rows += list(zip(toks.tolist(), ws.tolist(), [False] * len(toks)))
        groups, group_toks = [], []
        for r in rows:
            t = r[0]
            for ci in range(len(groups)):
                if len(groups[ci]) < 256 and t not in group_toks[ci]:
                    groups[ci].append(r)
                    group_toks[ci].add(t)
                    break
            else:
                groups.append([r])
                group_toks.append({t})
        chunks = []
        for g in groups:
            chunks.append(g[0:128])
            chunks.append(g[128:256])
        experts.append(chunks)
    extw = [ext_w[ext_e == e] for e in range(NE)]
    return experts, extw


def _plan(experts, extw):
    nch = np.array([len(c) for c in experts])
    order = np.argsort(-nch, kind="stable")
    expert_of = [[int(order[s * NCORES + c]) for s in range(NSLOT)]
                 for c in range(NCORES)]
    RS = [int(max(nch[order[s * NCORES + c]] for c in range(NCORES)))
          for s in range(NSLOT)]
    EWC = max(1, -(-int(max(len(w) for w in extw)) // 128))
    NCH = sum(RS)

    batches = []
    base = 0
    for s in range(NSLOT):
        c0 = 0
        while c0 < RS[s]:
            nb = min(4, RS[s] - c0)
            batches.append((s, base + c0, nb))
            c0 += nb
        base += RS[s]

    cores = []
    spec_pos = None
    for c in range(NCORES):
        gi = np.zeros(NCH * 128, np.int64)
        si = np.zeros(NCH * 128, np.int64)
        wr = np.zeros((128, NCH), np.float32)
        sp = []
        gch = 0
        for s in range(NSLOT):
            e = expert_of[c][s]
            chunks = experts[e]
            for lc in range(RS[s]):
                rows = chunks[lc] if lc < len(chunks) else []
                for j in range(128):
                    p = gch * 128 + j
                    if j < len(rows):
                        t, w, spc = rows[j]
                        gi[p] = t
                        si[p] = t
                        wr[j, gch] = w
                        if spc:
                            sp.append((gch, j))
                    else:
                        gi[p] = 0
                        si[p] = NTOK + j
                gch += 1
        for g2 in range(0, NCH, 2):
            seg = si[g2 * 128:(g2 + 2) * 128]
            used = set(seg[seg < NTOK].tolist())
            cand = 0
            for kk in range(256):
                if seg[kk] >= NTOK:
                    while cand in used:
                        cand += 1
                    seg[kk] = cand
                    used.add(cand)
        assert len(sp) == NSLOT
        ew = np.zeros((128, NSLOT * EWC), np.float32)
        for s in range(NSLOT):
            e = expert_of[c][s]
            assert sp[s][1] == 0
            vals = extw[e]
            tmp = np.zeros(128 * EWC, np.float32)
            tmp[:len(vals)] = vals
            ew[:, s * EWC:(s + 1) * EWC] = tmp.reshape(EWC, 128).T
        if spec_pos is None:
            spec_pos = sp
        else:
            assert sp == spec_pos, "special positions must be core-uniform"
        cores.append(dict(expert_of=expert_of[c],
                          gidx=_pack_idx(gi, NCH * 8),
                          sidx=_pack_idx(si, NCH * 8),
                          wrow=wr, extw=ew))
    return dict(RS=RS, NCH=NCH, EWC=EWC, batches=batches, spec_pos=spec_pos), cores



class _Sem:
    __slots__ = ("h", "n", "name")

    def __init__(self, h, name):
        self.h = h
        self.n = 0
        self.name = name

    @property
    def done(self):
        return (self, self.n)


class _Builder:
    def __init__(self):
        self.ops = {e: [] for e in ENGINES}

    def op(self, eng, fn, waits=(), incs=()):
        for s, k in incs:
            s.n += k
        self.ops[eng].append((tuple(w for w in waits if w is not None),
                              fn, tuple(incs)))

    def emit(self, block):
        for eng in ENGINES:
            oplist = self.ops[eng]

            def body(e, _ops=oplist):
                for waits, fn, incs in _ops:
                    for s, v in waits:
                        if v > 0:
                            e.wait_ge(s.h, v)
                    ins = fn(e)
                    for s, k in incs:
                        ins.then_inc(s.h, k)
            getattr(block, eng)(body)


class _Span:
    def __init__(self, t, cols):
        self.t = t
        self.cols = cols
        self.pos = 0

    def take(self, n):
        assert self.pos + n <= self.cols, f"arena overflow {self.pos}+{n}"
        o = self.pos
        self.pos += n
        return o

    def mark(self):
        return self.pos

    def reset(self, pos):
        self.pos = pos



def _build(meta):
    RS = meta["RS"]
    NCH = meta["NCH"]
    EWC = meta["EWC"]
    batches = meta["batches"]
    spec_pos = meta["spec_pos"]
    RSMAX = max(RS)

    nc = bass.Bass("TRN2", target_bir_lowering=False, num_devices=NCORES)

    def din(name, shape, dt):
        return nc.dram_tensor(name, list(shape), dt, kind="ExternalInput").ap()

    x_sl = din("x_sl", (SL, D), F32)
    wqT = din("wqT", (D, H * DH), F32)
    wkvT = din("wkvT", (D, 2 * H * DH), F32)
    wgaT = din("wgaT", (D, H), F32)
    woT = din("woT", (H * DH, D), F32)
    bgr = din("bgr", (1, H), F32)
    qsc = din("qsc", (128, 1), F32)
    ksc = din("ksc", (128, 1), F32)
    maskt = din("maskt", (5, 128, 640), F32)
    identb = din("identb", (128, 128), BF16)
    onesc = din("onesc", (128, 1), BF16)
    onesr = din("onesr", (1, 128), BF16)
    wgS = din("wgS", (NSLOT, D, 1024), F32)
    wuS = din("wuS", (NSLOT, D, 1024), F32)
    wdS = din("wdS", (NSLOT, 1024, D), F32)
    gidx = din("gidx", (128, NCH * 8), I16)
    sidx = din("sidx", (128, NCH * 8), I16)
    wrow = din("wrow", (128, NCH), F32)
    extw = din("extw", (128, NSLOT * EWC), F32)

    out = nc.dram_tensor("out", [TPC, D], F32, kind="ExternalOutput").ap()
    dbg = {}
    if DEBUG_DUMPS:
        for nm, shp in (("d_xn", [SL, D]), ("d_ag", [NTOK, D]),
                        ("d_s", [TPC, D]), ("d_ym", [NTOK, D]),
                        ("d_yr", [TPC, D])):
            dbg[nm] = nc.dram_tensor(nm, shp, BF16, kind="ExternalOutput").ap()

    xn_loc = nc.dram_tensor("xn_loc", [SL, D], BF16).ap()
    ag_in = nc.dram_tensor("ag_in", [TPC, D], BF16).ap()
    ag_out = nc.dram_tensor("ag_out", [NTOK, D], BF16).ap()
    y_main = nc.dram_tensor("y_main", [NTOK, D], BF16).ap()
    y_red = nc.dram_tensor("y_red", [TPC, D], BF16).ap()
    s_spill = nc.dram_tensor("s_spill", [TPC, D], BF16).ap()

    from contextlib import ExitStack
    stack = ExitStack()
    with stack:
        def sem(name):
            return _Sem(stack.enter_context(nc.semaphore(name)), name)

        def sbuf(name, shape, dt):
            return stack.enter_context(nc.sbuf_tensor(name, list(shape), dt))

        def psum(name, shape, dt):
            return stack.enter_context(nc.psum_tensor(name, list(shape), dt))

        LD = sem("LD")
        LD2 = sem("LD2")
        WSM = sem("WSM")
        GSM = sem("GSM")
        SSM = sem("SSM")
        CC = sem("CC")
        TEN = sem("TEN")
        VEC = sem("VEC")
        SCA = sem("SCA")
        XLW = sem("XLW")
        AGW = sem("AGW")
        TRS = sem("TRS")
        SSP = sem("SSP")
        OS = sem("OS")
        ZW = sem("ZW")

        ident_sb = sbuf("ident_sb", (128, 128), BF16)
        ones128_sb = sbuf("ones128_sb", (128, 128), BF16)
        onesc_sb = sbuf("onesc_sb", (128, 1), BF16)
        onesr_sb = sbuf("onesr_sb", (1, 128), BF16)
        bgr_sb = sbuf("bgr_sb", (1, H), BF16)
        wga_sb = sbuf("wga_sb", (128, 16 * H), BF16)
        ccol_sb = sbuf("ccol_sb", (128, 1), F32)
        sc1_sb = sbuf("sc1_sb", (128, 4), F32)
        gidx_sb = sbuf("gidx_sb", (128, NCH * 8), I16)
        sidx_sb = sbuf("sidx_sb", (128, NCH * 8), I16)
        wrow_sb = sbuf("wrow_sb", (128, NCH), F32)
        cext_sb = sbuf("cext_sb", (128, NSLOT), F32)
        extw_sb = sbuf("extw_sb", (128, NSLOT * EWC), F32)
        extb_sb = sbuf("extb_sb", (128, NSLOT * EWC), BF16)
        qsc_sb = sbuf("qsc_sb", (128, 1), F32)
        ksc_sb = sbuf("ksc_sb", (128, 1), F32)

        ABF_COLS = 88100
        AF32_COLS = 8000
        abf_t = sbuf("abf", (128, ABF_COLS), BF16)
        af32_t = sbuf("af32", (128, AF32_COLS), F32)
        abf = _Span(abf_t, ABF_COLS)
        af32 = _Span(af32_t, AF32_COLS)

        psA = psum("psA", (128, 512), F32)
        psB = psum("psB", (128, 512), F32)
        psC = psum("psC", (128, 512), F32)
        psD = psum("psD", (128, 512), F32)
        psE = psum("psE", (128, 640), F32)
        psF1 = psum("psF1", (128, 128), BF16)
        psF2 = psum("psF2", (128, 128), BF16)

        b = _Builder()

        for dst, src in ((ident_sb, identb), (onesc_sb, onesc),
                         (onesr_sb, onesr), (gidx_sb, gidx), (sidx_sb, sidx),
                         (wrow_sb, wrow), (extw_sb, extw), (qsc_sb, qsc),
                         (ksc_sb, ksc)):
            b.op("sync", lambda e, d=dst, s=src: e.dma_start(out=d[:, :], in_=s[:, :]),
                 incs=[(LD, 16)])
        ph0_ld = LD.done

        b.op("gpsimd", lambda e: e.load_library(_mlp_lib))
        b.op("gpsimd", lambda e: e.dma_start(out=bgr_sb[:, :], in_=bgr[:, :]),
             incs=[(WSM, 16)])
        b.op("gpsimd", lambda e: e.dma_start(
            out=wga_sb[:, :].rearrange("p (c m) -> p c m", m=H),
            in_=wgaT.rearrange("(c p) m -> p c m", p=128)),
            incs=[(WSM, 16)])
        wga_done = WSM.done

        wreg_o = abf.take(8192)
        reg2_o = abf.take(6144)
        xnT_o = abf.take(16 * SL)
        q_o = abf.take(H * TPC)
        k_o = abf.take(H * SL)
        v_o = abf.take(12 * 1024)
        o_o = abf.take(H * TPC)

        wreg = abf_t[:, wreg_o:wreg_o + 8192]
        xnT = [abf_t[:, xnT_o + hc * SL: xnT_o + (hc + 1) * SL] for hc in range(16)]
        q_sb = abf_t[:, q_o:q_o + H * TPC]
        k_sb = abf_t[:, k_o:k_o + H * SL]
        v_sb = abf_t[:, v_o:v_o + 12 * 1024]
        o_sb = abf_t[:, o_o:o_o + H * TPC]

        xn2 = [abf_t[:, reg2_o + i * D: reg2_o + (i + 1) * D] for i in range(2)]
        squt = abf_t[:, reg2_o + 2 * D: reg2_o + 3 * D]
        qsq = [abf_t[:, reg2_o + i * 512: reg2_o + (i + 1) * 512] for i in range(2)]
        ksq = [abf_t[:, reg2_o + 1024 + i * 512: reg2_o + 1024 + (i + 1) * 512]
               for i in range(2)]
        bcT = abf_t[:, reg2_o + 2048: reg2_o + 2176]
        attbf = [abf_t[:, reg2_o + 2176 + i * 640: reg2_o + 2176 + (i + 1) * 640]
                 for i in range(2)]
        attT = [abf_t[:, reg2_o + 3456 + i * 640: reg2_o + 3456 + (i + 1) * 640]
                for i in range(2)]
        s_half = [abf_t[:, reg2_o + i * 1024: reg2_o + (i + 1) * 1024]
                  for i in range(2)]

        x2_o = af32.take(2 * D)
        lst_o = af32.take(16)
        rq_o = af32.take(64)
        gt_o = af32.take(64)
        rk_o = af32.take(96)
        ms_o = af32.take(1280)
        atf_o = af32.take(1280)
        st_o = af32.take(32)
        sil_o = af32.take(1024)
        x2 = [af32_t[:, x2_o + i * D: x2_o + (i + 1) * D] for i in range(2)]
        lst = af32_t[:, lst_o:lst_o + 16]
        rq_tok = af32_t[:, rq_o:rq_o + 64]
        gate_tok = af32_t[:, gt_o:gt_o + 64]
        rk_tok = af32_t[:, rk_o:rk_o + 96]
        mask_sb = af32_t[:, x2_o:x2_o + 5 * 640]
        ms = [af32_t[:, ms_o + i * 640: ms_o + (i + 1) * 640] for i in range(2)]
        attf = [af32_t[:, atf_o + i * 640: atf_o + (i + 1) * 640] for i in range(2)]
        stt = [af32_t[:, st_o + i * 16: st_o + (i + 1) * 16] for i in range(2)]
        sil = [af32_t[:, sil_o:sil_o + 512], af32_t[:, sil_o + 512:sil_o + 1024]]

        b.op("vector", lambda e: e.memset(ones128_sb[:, :], 1.0),
             incs=[(VEC, 1)])
        zz_done = VEC.done

        b.op("vector", lambda e: e.tensor_mul(sc1_sb[:, 0:1], qsc_sb[:, :],
                                              ksc_sb[:, :]),
             waits=[ph0_ld], incs=[(VEC, 1)])
        b.op("vector", lambda e: e.tensor_scalar_mul(ccol_sb[:, :],
                                                     sc1_sb[:, 0:1], 8.0),
             waits=[VEC.done], incs=[(VEC, 1)])
        ccol_done = VEC.done

        b.op("vector", lambda e: e.tensor_copy(extb_sb[:, :], extw_sb[:, :]),
             waits=[ph0_ld], incs=[(VEC, 1)])
        extb_done = VEC.done

        def _extmm(e):
            last = None
            for s in range(NSLOT):
                last = e.matmul(psC[0:1, s * EWC:(s + 1) * EWC],
                                onesc_sb[:, :],
                                extb_sb[:, s * EWC:(s + 1) * EWC])
            return last
        b.op("tensor", _extmm, waits=[extb_done], incs=[(TEN, 1)])
        extmm_done = TEN.done

        def _extred(e):
            last = None
            for s in range(NSLOT):
                last = e.reduce_sum(cext_sb[0:1, s:s + 1],
                                    psC[0:1, s * EWC:(s + 1) * EWC], axis=AX.X)
            return last
        b.op("vector", _extred, waits=[extmm_done], incs=[(VEC, 1)])

        def _extcp(e):
            last = None
            for s in range(NSLOT):
                gch, p = spec_pos[s]
                last = e.tensor_copy(wrow_sb[0:1, gch:gch + 1],
                                     cext_sb[0:1, s:s + 1])
            return last
        b.op("vector", _extcp, waits=[VEC.done], incs=[(VEC, 1)])
        wfill_done = VEC.done

        zv = abf_t[:, o_o:o_o + D]
        b.op("vector", lambda e: e.memset(zv, 0.0), incs=[(VEC, 1)])
        zv_done = VEC.done
        for zi in range(NTOK // 128):
            b.op("gpsimd", lambda e, zi=zi: e.dma_start(
                out=y_main[zi * 128:(zi + 1) * 128, :], in_=zv),
                waits=([zv_done] if zi == 0 else []), incs=[(ZW, 16)])
        yzero_done = ZW.done

        xld, xvec, xsca, xwr = [None] * NT, [None] * NT, [None] * NT, [None] * NT
        for i in range(NT):
            waits = [xvec[i - 2], xsca[i - 2]] if i >= 2 else []
            b.op("sync", lambda e, i=i: e.dma_start(
                out=x2[i % 2], in_=x_sl[i * 128:(i + 1) * 128, :]),
                waits=waits, incs=[(LD, 16)])
            xld[i] = LD.done

            sqc = 1 + 8 * (i % 2)
            b.op("scalar", lambda e, i=i, sqc=sqc: e.activation(
                squt, x2[i % 2], AF.Square, accum_out=lst[:, sqc:sqc + 1]),
                waits=[xld[i]], incs=[(SCA, 1)])
            ssq_done = SCA.done

            wv = [xld[i], ssq_done]
            if i >= 2:
                wv.append(xwr[i - 2])
            b.op("vector", lambda e, i=i: e.reduce_sum(lst[:, 0:1], x2[i % 2],
                                                       axis=AX.X),
                 waits=wv, incs=[(VEC, 1)])
            b.op("vector", lambda e: e.tensor_scalar_mul(lst[:, 2:3],
                                                         lst[:, 0:1], 1.0 / D),
                 waits=[VEC.done], incs=[(VEC, 1)])
            b.op("vector", lambda e, sqc=sqc: e.tensor_scalar_mul(
                lst[:, 3:4], lst[:, sqc:sqc + 1], 1.0 / D),
                waits=[VEC.done], incs=[(VEC, 1)])
            b.op("vector", lambda e: e.tensor_mul(lst[:, 4:5], lst[:, 2:3],
                                                  lst[:, 2:3]),
                 waits=[VEC.done], incs=[(VEC, 1)])
            b.op("vector", lambda e: e.tensor_sub(lst[:, 5:6], lst[:, 3:4],
                                                  lst[:, 4:5]),
                 waits=[VEC.done], incs=[(VEC, 1)])
            var_done = VEC.done

            b.op("vector", lambda e: e.tensor_scalar_add(
                lst[:, 8:9], lst[:, 5:6], 1e-5),
                waits=[var_done], incs=[(VEC, 1)])
            epsd = VEC.done
            b.op("scalar", lambda e: e.activation(
                lst[:, 6:7], lst[:, 8:9], AF.Sqrt),
                waits=[epsd], incs=[(SCA, 1)])
            xsca[i] = SCA.done

            b.op("vector", lambda e: e.reciprocal(lst[:, 7:8], lst[:, 6:7]),
                 waits=[xsca[i]], incs=[(VEC, 1)])
            rstd_done = VEC.done
            b.op("vector", lambda e, i=i: e.tensor_scalar(
                xn2[i % 2], x2[i % 2], lst[:, 2:3], lst[:, 7:8],
                op0=ALU.subtract, op1=ALU.mult),
                waits=[rstd_done], incs=[(VEC, 1)])
            xvec[i] = VEC.done

            def _lnw(e, i=i):
                ins = e.dma_start(out=xn_loc[i * 128:(i + 1) * 128, :],
                                  in_=xn2[i % 2])
                ins.then_inc(XLW.h, 16)
                if i >= 4:
                    ins = e.dma_start(out=ag_in[(i - 4) * 128:(i - 3) * 128, :],
                                      in_=xn2[i % 2])
                    ins.then_inc(AGW.h, 16)
                return ins
            XLW.n += 16
            if i >= 4:
                AGW.n += 16
            b.op("sync", _lnw, waits=[xvec[i]])
            xwr[i] = (XLW, XLW.n)
        ln_done = (VEC, VEC.n)
        ln_sca_done = (SCA, SCA.n)

        def _trd(e):
            last = None
            for hc in range(16):
                last = e.dma_start(out=xnT[hc],
                                   in_=xn_loc[:, hc * 128:(hc + 1) * 128],
                                   transpose=True)
                last.then_inc(TRS.h, 16)
            return last
        TRS.n += 16 * 16
        b.op("sync", _trd, waits=[(XLW, XLW.n)])
        xnT_done = TRS.done

        b.op("gpsimd", lambda e: e.collective_compute(
            "AllGather", ALU.bypass, replica_groups=[list(range(NCORES))],
            ins=[ag_in.opt()], outs=[ag_out.opt()]),
            waits=[(AGW, AGW.n)], incs=[(CC, 1)])
        ag_done = CC.done

        if SKIP_ATTN:
            s_done = (SSP, 0)
            attn_skip = True
        else:
            attn_skip = False
        if not attn_skip:
            prev_half_last = None
            ev_q = {}
            for half in range(2):
                wsrc = wqT.rearrange("(c p) m -> p c m", p=128)[:, :, half * 512:(half + 1) * 512]
                wts = [] if half == 0 else [prev_half_last]
                b.op("gpsimd", lambda e, s=wsrc: e.dma_start(
                    out=wreg[:, :].rearrange("p (c m) -> p c m", m=512), in_=s),
                    waits=wts, incs=[(WSM, 16)])
                wq_ld = WSM.done
                for hh in range(4):
                    h = half * 4 + hh
                    for bb in range(2):
                        u = h * 2 + bb
                        ps = psA if u % 2 == 0 else psB

                        def _qmm(e, hh=hh, bb=bb, ps=ps):
                            last = None
                            for hc in range(16):
                                last = e.matmul(
                                    ps[:, :],
                                    wreg[:, hc * 512 + hh * 128: hc * 512 + (hh + 1) * 128],
                                    xnT[hc][:, WIN + bb * 512: WIN + (bb + 1) * 512],
                                    start=(hc == 0), stop=(hc == 15))
                            return last
                        wts = [wq_ld, ccol_done, xnT_done]
                        if u >= 2:
                            wts.append(ev_q[u - 2])
                        b.op("tensor", _qmm, waits=wts, incs=[(TEN, 1)])
                        mm_done = TEN.done

                        b.op("vector", lambda e, h=h, bb=bb, ps=ps: e.tensor_scalar_mul(
                            q_sb[:, h * TPC + bb * 512: h * TPC + (bb + 1) * 512],
                            ps[:, :], ccol_sb[:, :]),
                            waits=[mm_done], incs=[(VEC, 1)])
                        evd = VEC.done

                        b.op("scalar", lambda e, bb=bb, ps=ps: e.activation(
                            qsq[bb], ps[:, :], AF.Square),
                            waits=[evd], incs=[(SCA, 1)])
                        sqd = SCA.done

                        def _ssmm(e, bb=bb):
                            last = None
                            for j in range(4):
                                last = e.matmul(psC[:, j:j + 1],
                                                qsq[bb][:, j * 128:(j + 1) * 128],
                                                onesc_sb[:, :])
                            return last
                        b.op("tensor", _ssmm, waits=[sqd], incs=[(TEN, 1)])
                        ssd = TEN.done

                        def _ssev(e, h=h, bb=bb):
                            last = None
                            for j in range(4):
                                t = bb * 4 + j
                                last = e.tensor_copy(
                                    rq_tok[:, t * 8 + h: t * 8 + h + 1],
                                    psC[:, j:j + 1])
                            return last
                        b.op("vector", _ssev, waits=[ssd], incs=[(VEC, 1)])
                        ev_q[u] = VEC.done
                prev_half_last = ev_q[half * 8 + 7]

            b.op("scalar", lambda e: e.activation(rq_tok[:, :], rq_tok[:, :], AF.Sqrt),
                 waits=[ev_q[15]], incs=[(SCA, 1)])

            b.op("vector", lambda e: e.tensor_scalar_max(
                rq_tok[:, :], rq_tok[:, :], 1e-12),
                waits=[SCA.done], incs=[(VEC, 1)])
            b.op("vector", lambda e: e.reciprocal(rq_tok[:, :], rq_tok[:, :]),
                 waits=[VEC.done], incs=[(VEC, 1)])
            rq_done = VEC.done

            prev_gate = None
            for t in range(8):
                def _gmm(e, t=t):
                    for hc in range(16):
                        e.matmul(psD[:, 0:8],
                                 xnT[hc][:, WIN + t * 128: WIN + (t + 1) * 128],
                                 wga_sb[:, hc * H:(hc + 1) * H],
                                 start=(hc == 0), stop=False)
                    return e.matmul(psD[:, 0:8], onesr_sb[:, :], bgr_sb[:, :],
                                    start=False, stop=True)
                wts = [xnT_done, wga_done, ph0_ld]
                if prev_gate is not None:
                    wts.append(prev_gate)
                b.op("tensor", _gmm, waits=wts, incs=[(TEN, 1)])
                b.op("scalar", lambda e, t=t: e.activation(
                    gate_tok[:, t * 8:(t + 1) * 8], psD[:, 0:8], AF.Sigmoid),
                    waits=[TEN.done], incs=[(SCA, 1)])
                prev_gate = SCA.done
            gates_done = prev_gate

            ev_k = {}
            for half in range(2):
                wsrc = wkvT.rearrange("(c p) m -> p c m", p=128)[:, :, half * 512:(half + 1) * 512]
                b.op("gpsimd", lambda e, s=wsrc: e.dma_start(
                    out=wreg[:, :].rearrange("p (c m) -> p c m", m=512), in_=s),
                    waits=[ev_q[15] if half == 0 else ev_k[11]],
                    incs=[(WSM, 16)])
                wk_ld = WSM.done
                for hh in range(4):
                    h = half * 4 + hh
                    for b3 in range(3):
                        u = h * 3 + b3
                        ps = psA if u % 2 == 0 else psB

                        def _kmm(e, hh=hh, b3=b3, ps=ps):
                            last = None
                            for hc in range(16):
                                last = e.matmul(
                                    ps[:, :],
                                    wreg[:, hc * 512 + hh * 128: hc * 512 + (hh + 1) * 128],
                                    xnT[hc][:, b3 * 512:(b3 + 1) * 512],
                                    start=(hc == 0), stop=(hc == 15))
                            return last
                        wts = [wk_ld]
                        if u >= 2:
                            wts.append(ev_k[u - 2])
                        b.op("tensor", _kmm, waits=wts, incs=[(TEN, 1)])
                        mm_done = TEN.done

                        b.op("vector", lambda e, h=h, b3=b3, ps=ps: e.tensor_copy(
                            k_sb[:, h * SL + b3 * 512: h * SL + (b3 + 1) * 512],
                            ps[:, :]),
                            waits=[mm_done], incs=[(VEC, 1)])
                        evd = VEC.done

                        b.op("scalar", lambda e, b3=b3, ps=ps: e.activation(
                            ksq[b3 % 2], ps[:, :], AF.Square),
                            waits=[evd], incs=[(SCA, 1)])
                        sqd = SCA.done

                        def _ssmm(e, b3=b3):
                            last = None
                            for j in range(4):
                                last = e.matmul(psC[:, j:j + 1],
                                                ksq[b3 % 2][:, j * 128:(j + 1) * 128],
                                                onesc_sb[:, :])
                            return last
                        b.op("tensor", _ssmm, waits=[sqd], incs=[(TEN, 1)])
                        ssd = TEN.done

                        def _ssev(e, h=h, b3=b3):
                            last = None
                            for j in range(4):
                                ch = b3 * 4 + j
                                last = e.tensor_copy(
                                    rk_tok[:, ch * 8 + h: ch * 8 + h + 1],
                                    psC[:, j:j + 1])
                            return last
                        b.op("vector", _ssev, waits=[ssd], incs=[(VEC, 1)])
                        ev_k[u] = VEC.done
            k_done = ev_k[23]

            ev_v = {}
            for half in range(2):
                wsrc = wkvT.rearrange("(c p) m -> p c m", p=128)[:, :, 1024 + half * 512: 1024 + (half + 1) * 512]
                b.op("gpsimd", lambda e, s=wsrc: e.dma_start(
                    out=wreg[:, :].rearrange("p (c m) -> p c m", m=512), in_=s),
                    waits=[k_done if half == 0 else ev_v[11]],
                    incs=[(WSM, 16)])
                wv_ld = WSM.done
                for ch in range(12):
                    u = half * 12 + ch
                    ps = psA if u % 2 == 0 else psB

                    def _vmm(e, ch=ch, ps=ps):
                        last = None
                        for hc in range(16):
                            last = e.matmul(
                                ps[:, :],
                                xnT[hc][:, ch * 128:(ch + 1) * 128],
                                wreg[:, hc * 512:(hc + 1) * 512],
                                start=(hc == 0), stop=(hc == 15))
                        return last
                    wts = [wv_ld]
                    if u >= 2:
                        wts.append(ev_v[u - 2])
                    b.op("tensor", _vmm, waits=wts, incs=[(TEN, 1)])
                    b.op("vector", lambda e, ch=ch, half=half, ps=ps: e.tensor_copy(
                        v_sb[:, ch * 1024 + half * 512: ch * 1024 + (half + 1) * 512],
                        ps[:, :]),
                        waits=[TEN.done], incs=[(VEC, 1)])
                    ev_v[u] = VEC.done
            v_done = ev_v[23]

            b.op("scalar", lambda e: e.activation(rk_tok[:, :], rk_tok[:, :], AF.Sqrt),
                 waits=[k_done], incs=[(SCA, 1)])

            b.op("vector", lambda e: e.tensor_scalar_max(
                rk_tok[:, :], rk_tok[:, :], 1e-12),
                waits=[SCA.done], incs=[(VEC, 1)])
            b.op("vector", lambda e: e.reciprocal(rk_tok[:, :], rk_tok[:, :]),
                 waits=[VEC.done], incs=[(VEC, 1)])
            rk_done = VEC.done

            prev_kh = {}
            for h in range(H):
                for ch in range(12):
                    u = h * 12 + ch
                    psf = psF1 if u % 2 == 0 else psF2

                    wts = [rk_done, zz_done]
                    if u >= 1:
                        wts.append(prev_kh[u - 1][0])
                    b.op("vector", lambda e, h=h, ch=ch: e.tensor_scalar_mul(
                        bcT, ones128_sb[:, :],
                        rk_tok[:, (ch * 8 + h):(ch * 8 + h) + 1]),
                        waits=wts, incs=[(VEC, 1)])
                    bct_done = VEC.done

                    wts = [bct_done, ph0_ld]
                    if u >= 2:
                        wts.append(prev_kh[u - 2][1])
                    if u == 0:
                        wts.append(gates_done)
                    b.op("tensor", lambda e, psf=psf: e.transpose(
                        psf[:, :], bcT, ident_sb[:, :]),
                        waits=wts, incs=[(TEN, 1)])
                    tr_done = TEN.done

                    b.op("vector", lambda e, h=h, ch=ch, psf=psf: e.tensor_mul(
                        k_sb[:, h * SL + ch * 128: h * SL + (ch + 1) * 128],
                        k_sb[:, h * SL + ch * 128: h * SL + (ch + 1) * 128],
                        psf[:, :]),
                        waits=[tr_done], incs=[(VEC, 1)])
                    prev_kh[u] = (tr_done, VEC.done)
            kh_done = prev_kh[H * 12 - 1][1]

            b.op("sync", lambda e: e.dma_start(
                out=mask_sb[:, :].rearrange("p (c m) -> p c m", m=640),
                in_=maskt.rearrange("c p m -> p c m")),
                waits=[ln_done, ln_sca_done], incs=[(LD, 16)])
            mask_done = LD.done

            s_phase_guard = v_done
            prev_madd = {}
            prev_exp2 = {}
            prev_oev = None
            prev_t_ev = {}
            for h in range(H):
                for t in range(8):
                    u = h * 8 + t
                    pp = u % 2
                    msP, attfP, sttP = ms[pp], attf[pp], stt[pp]
                    attbfP, attTP = attbf[pp], attT[pp]

                    if pp == 0:
                        def _simm(e, h=h, t=t):
                            e.matmul(psE[:, 0:512],
                                     q_sb[:, h * TPC + t * 128: h * TPC + (t + 1) * 128],
                                     k_sb[:, h * SL + t * 128: h * SL + t * 128 + 512],
                                     start=True, stop=True)
                            return e.matmul(
                                psE[:, 512:640],
                                q_sb[:, h * TPC + t * 128: h * TPC + (t + 1) * 128],
                                k_sb[:, h * SL + t * 128 + 512: h * SL + t * 128 + 640],
                                start=True, stop=True)
                    else:
                        def _simm(e, h=h, t=t):
                            e.matmul(psA[:, :],
                                     q_sb[:, h * TPC + t * 128: h * TPC + (t + 1) * 128],
                                     k_sb[:, h * SL + t * 128: h * SL + t * 128 + 512],
                                     start=True, stop=True)
                            return e.matmul(
                                psB[:, 0:128],
                                q_sb[:, h * TPC + t * 128: h * TPC + (t + 1) * 128],
                                k_sb[:, h * SL + t * 128 + 512: h * SL + t * 128 + 640],
                                start=True, stop=True)
                    wts = [kh_done, mask_done]
                    if u == 0 or u == 1:
                        wts.append(s_phase_guard)
                    if u - 2 in prev_madd:
                        wts.append(prev_madd[u - 2])
                    b.op("tensor", _simm, waits=wts, incs=[(TEN, 1)])
                    sim_done = TEN.done

                    mslot = t if t < 4 else 4
                    wts = [sim_done, rq_done]
                    if u - 2 in prev_exp2:
                        wts.append(prev_exp2[u - 2])
                    if pp == 0:
                        b.op("vector", lambda e, mslot=mslot, msP=msP: e.tensor_add(
                            msP, psE[:, :],
                            mask_sb[:, mslot * 640:(mslot + 1) * 640]),
                            waits=wts, incs=[(VEC, 1)])
                    else:
                        b.op("vector", lambda e, mslot=mslot, msP=msP: e.tensor_add(
                            msP[:, 0:512], psA[:, :],
                            mask_sb[:, mslot * 640: mslot * 640 + 512]),
                            waits=wts, incs=[(VEC, 1)])
                        b.op("vector", lambda e, mslot=mslot, msP=msP: e.tensor_add(
                            msP[:, 512:640], psB[:, 0:128],
                            mask_sb[:, mslot * 640 + 512:(mslot + 1) * 640]),
                            incs=[(VEC, 1)])
                    prev_madd[u] = VEC.done
                    b.op("vector", lambda e, msP=msP, sttP=sttP: e.reduce_max(
                        sttP[:, 0:1], msP, axis=AX.X, negate=True),
                        waits=[VEC.done], incs=[(VEC, 1)])
                    b.op("vector", lambda e, h=h, t=t, sttP=sttP: e.tensor_mul(
                        sttP[:, 1:2], sttP[:, 0:1],
                        rq_tok[:, t * 8 + h: t * 8 + h + 1]),
                        waits=[VEC.done], incs=[(VEC, 1)])
                    bias_done = VEC.done

                    b.op("scalar", lambda e, h=h, t=t, msP=msP, attfP=attfP,
                         sttP=sttP: e.activation(
                             attfP, msP, AF.Exp, bias=sttP[:, 1:2],
                             scale=rq_tok[:, t * 8 + h: t * 8 + h + 1],
                             accum_out=sttP[:, 2:3]),
                         waits=[bias_done], incs=[(SCA, 1)])
                    prev_exp2[u] = SCA.done
                    exp_done = SCA.done

                    b.op("vector", lambda e, sttP=sttP: e.reciprocal(
                        sttP[:, 3:4], sttP[:, 2:3]),
                        waits=[exp_done], incs=[(VEC, 1)])
                    b.op("vector", lambda e, h=h, t=t, attfP=attfP, sttP=sttP,
                         attbfP=attbfP: e.tensor_scalar(
                             attbfP, attfP, sttP[:, 3:4],
                             gate_tok[:, t * 8 + h: t * 8 + h + 1],
                             op0=ALU.mult, op1=ALU.mult),
                         waits=[VEC.done], incs=[(VEC, 1)])
                    attbf_done = VEC.done

                    tev = {}
                    for kk in range(5):
                        psf = psF1 if kk % 2 == 0 else psF2
                        wts = [attbf_done]
                        if kk >= 2:
                            wts.append(tev[kk - 2])
                        elif u > 0:
                            wts.append(prev_t_ev[(u - 1, 4 if kk == 0 else 3)])
                        b.op("tensor", lambda e, kk=kk, psf=psf, attbfP=attbfP:
                             e.transpose(psf[:, :],
                                         attbfP[:, kk * 128:(kk + 1) * 128],
                                         ident_sb[:, :]),
                             waits=wts, incs=[(TEN, 1)])
                        b.op("vector", lambda e, kk=kk, psf=psf, attTP=attTP:
                             e.tensor_copy(attTP[:, kk * 128:(kk + 1) * 128],
                                           psf[:, :]),
                             waits=[TEN.done], incs=[(VEC, 1)])
                        tev[kk] = VEC.done
                        prev_t_ev[(u, kk)] = VEC.done

                    def _avmm(e, h=h, t=t, attTP=attTP):
                        last = None
                        for kk in range(5):
                            last = e.matmul(
                                psD[:, 0:128],
                                v_sb[:, (t + kk) * 1024 + h * 128:
                                     (t + kk) * 1024 + (h + 1) * 128],
                                attTP[:, kk * 128:(kk + 1) * 128],
                                start=(kk == 0), stop=(kk == 4))
                        return last
                    wts = [tev[4], v_done]
                    if u == 0:
                        wts.append(gates_done)
                    if prev_oev is not None:
                        wts.append(prev_oev)
                    b.op("tensor", _avmm, waits=wts, incs=[(TEN, 1)])
                    wts_oev = [TEN.done] + ([yzero_done] if prev_oev is None
                                            else [])
                    b.op("vector", lambda e, h=h, t=t: e.tensor_copy(
                        o_sb[:, h * TPC + t * 128: h * TPC + (t + 1) * 128],
                        psD[:, 0:128]),
                        waits=wts_oev, incs=[(VEC, 1)])
                    prev_oev = VEC.done
            attn_done = prev_oev

            prev_sev = {}
            for half in range(2):
                wsrc = woT.rearrange("(c p) m -> p c m", p=128)[:, :, half * 1024:(half + 1) * 1024]
                b.op("gpsimd", lambda e, s=wsrc: e.dma_start(
                    out=wreg[:, :].rearrange("p (c m) -> p c m", m=1024), in_=s),
                    waits=[v_done if half == 0 else prev_sev[(0, 15)]],
                    incs=[(WSM, 16)])
                wo_ld = WSM.done
                for t in range(8):
                    for dci in range(2):
                        u = t * 2 + dci
                        ps = psA if u % 2 == 0 else psB

                        def _smm(e, t=t, dci=dci, ps=ps):
                            last = None
                            for ic in range(8):
                                last = e.matmul(
                                    ps[:, :],
                                    o_sb[:, ic * TPC + t * 128: ic * TPC + (t + 1) * 128],
                                    wreg[:, ic * 1024 + dci * 512: ic * 1024 + (dci + 1) * 512],
                                    start=(ic == 0), stop=(ic == 7))
                            return last
                        wts = [wo_ld, attn_done]
                        if u >= 2:
                            wts.append(prev_sev[(half, u - 2)])
                        b.op("tensor", _smm, waits=wts, incs=[(TEN, 1)])
                        b.op("vector", lambda e, t=t, dci=dci, ps=ps: e.tensor_copy(
                            s_half[t % 2][:, dci * 512:(dci + 1) * 512], ps[:, :]),
                            waits=[TEN.done, (SSP, max(0, SSP.n - 16))],
                            incs=[(VEC, 1)])
                        prev_sev[(half, u)] = VEC.done
                    b.op("sync", lambda e, t=t, half=half: e.dma_start(
                        out=s_spill[t * 128:(t + 1) * 128,
                                    half * 1024:(half + 1) * 1024],
                        in_=s_half[t % 2]),
                        waits=[prev_sev[(half, t * 2 + 1)]], incs=[(SSP, 16)])
            s_done = SSP.done

        if SKIP_MOE:
            rs_done = (CC, CC.n)
            moe_skip = True
        else:
            moe_skip = False
        abf.reset(0)
        if not moe_skip:
            wg_o = abf.take(16 * 1024)
            wu_o = abf.take(16 * 1024)
            wd_o = abf.take(16 * 1024)
            xs_o = abf.take(2 * 16 * 512)
            mid_o = abf.take(8 * 128 * RSMAX)
            or_o = abf.take(2 * 2 * D)
            wg_sb = abf_t[:, wg_o:wg_o + 16 * 1024]
            wu_sb = abf_t[:, wu_o:wu_o + 16 * 1024]
            wd_sb = abf_t[:, wd_o:wd_o + 16 * 1024]
            xsT = [abf_t[:, xs_o + i * 8192: xs_o + (i + 1) * 8192] for i in range(2)]
            orow = [abf_t[:, or_o + i * 2 * D: or_o + (i + 1) * 2 * D] for i in range(2)]

            p2_bar = [(TEN, TEN.n), (VEC, VEC.n), (SCA, SCA.n), (SSP, SSP.n),
                  (TRS, TRS.n), (LD, LD.n)]

            slot_batches = [[bt for bt in batches if bt[0] == s] for s in range(NSLOT)]
            gather_ten = {}
            prev_scatter = None
            last_gu_mm = None
            last_dn_mm = None
            bid_global = 0
            ch_base = [0]
            for s in range(1, NSLOT):
                ch_base.append(ch_base[-1] + RS[s - 1])

            for s in range(NSLOT):
                wgsrc = wgS[s].rearrange("(c p) m -> p c m", p=128)
                wusrc = wuS[s].rearrange("(c p) m -> p c m", p=128)
                wdsrc = wdS[s].rearrange("(c p) m -> p c m", p=128)
                wts_w = list(p2_bar) if s == 0 else [last_gu_mm]
                b.op("gpsimd", lambda e, src=wgsrc: e.dma_start(
                    out=wg_sb[:, :].rearrange("p (c m) -> p c m", m=1024), in_=src),
                    waits=wts_w, incs=[(WSM, 16)])
                b.op("gpsimd", lambda e, src=wusrc: e.dma_start(
                    out=wu_sb[:, :].rearrange("p (c m) -> p c m", m=1024), in_=src),
                    incs=[(WSM, 16)])
                wgu_ld = WSM.done
                wts_d = [] if s == 0 else [last_dn_mm]
                b.op("gpsimd", lambda e, src=wdsrc: e.dma_start(
                    out=wd_sb[:, :].rearrange("p (c m) -> p c m", m=1024), in_=src),
                    waits=wts_d, incs=[(WSM, 16)])
                wd_ld = WSM.done

                dn_barrier = last_dn_mm
                prev_ev = {}
                for bi, (s_, gch0, nb) in enumerate(slot_batches[s]):
                    nidx = nb * 128
                    buf = bid_global % 2
                    wts_g = [ag_done, ph0_ld]
                    if s == 0 and bi == 0:
                        wts_g += p2_bar
                    if bid_global >= 2:
                        wts_g.append(gather_ten[bid_global - 2])
                    b.op("gpsimd", lambda e, buf=buf, gch0=gch0, nidx=nidx: e.dma_gather(
                        xsT[buf][:, 0:16 * nidx].rearrange("p (f i) -> p f i", f=16),
                        ag_out[:, :],
                        gidx_sb[:, gch0 * 8: gch0 * 8 + (nidx + 15) // 16],
                        nidx, nidx, 2048, transpose=True),
                        waits=wts_g, incs=[(GSM, 16)])
                    g_done = GSM.done

                    lc0 = gch0 - ch_base[s]
                    for ic in range(8):
                        psg = psA if ic % 2 == 0 else psC
                        psu = psB if ic % 2 == 0 else psD

                        def _gumm(e, buf=buf, ic=ic, nidx=nidx, psg=psg, psu=psu):
                            for hc in range(16):
                                e.matmul(psg[:, 0:nidx],
                                         wg_sb[:, hc * 1024 + ic * 128: hc * 1024 + (ic + 1) * 128],
                                         xsT[buf][:, hc * nidx:(hc + 1) * nidx],
                                         start=(hc == 0), stop=(hc == 15))
                            last = None
                            for hc in range(16):
                                last = e.matmul(psu[:, 0:nidx],
                                                wu_sb[:, hc * 1024 + ic * 128: hc * 1024 + (ic + 1) * 128],
                                                xsT[buf][:, hc * nidx:(hc + 1) * nidx],
                                                start=(hc == 0), stop=(hc == 15))
                            return last
                        wts = [g_done, wgu_ld]
                        if ic >= 2:
                            wts.append(prev_ev[(bi, ic - 2)])
                        elif bi > 0:
                            wts.append(prev_ev[(bi - 1, 6 + ic)])
                        b.op("tensor", _gumm, waits=wts, incs=[(TEN, 1)])
                        gu_done = TEN.done
                        if ic == 7:
                            gather_ten[bid_global] = gu_done
                            last_gu_mm = gu_done

                        sb_ = sil[ic % 2]
                        b.op("scalar", lambda e, nidx=nidx, psg=psg, sb_=sb_: e.activation(
                            sb_[:, 0:nidx], psg[:, 0:nidx], AF.Silu),
                            waits=[gu_done], incs=[(SCA, 1)])
                        sil_done = SCA.done

                        mo = mid_o + ic * 128 * RS[s] + lc0 * 128
                        wts_m = [sil_done]
                        if dn_barrier is not None and bi == 0:
                            wts_m.append(dn_barrier)
                        b.op("vector", lambda e, mo=mo, nidx=nidx, psu=psu, sb_=sb_:
                             e.tensor_mul(abf_t[:, mo:mo + nidx],
                                          sb_[:, 0:nidx], psu[:, 0:nidx]),
                             waits=wts_m, incs=[(VEC, 1)])
                        prev_ev[(bi, ic)] = VEC.done
                    bid_global += 1
                mid_done_of = {}
                for bi2, (s2, gch02, nb2) in enumerate(slot_batches[s]):
                    for lc2 in range(gch02 - ch_base[s], gch02 - ch_base[s] + nb2):
                        mid_done_of[lc2] = prev_ev[(bi2, 7)]
                mid_done = prev_ev[(len(slot_batches[s]) - 1, 7)]

                assert RS[s] % 2 == 0
                prev_dn = {}
                for rc in range(RS[s]):
                    gch = ch_base[s] + rc
                    grp = gch // 2
                    ob = orow[grp % 2]
                    oco = (rc % 2) * D
                    for dc in range(4):
                        ps = psA if dc % 2 == 0 else psB

                        def _dmm(e, rc=rc, dc=dc, ps=ps, s=s):
                            last = None
                            for ic in range(8):
                                last = e.matmul(
                                    ps[:, :],
                                    abf_t[:, mid_o + ic * 128 * RS[s] + rc * 128:
                                          mid_o + ic * 128 * RS[s] + (rc + 1) * 128],
                                    wd_sb[:, ic * 2048 + dc * 512: ic * 2048 + (dc + 1) * 512],
                                    start=(ic == 0), stop=(ic == 7))
                            return last
                        wts = [mid_done, wd_ld]
                        if rc > 0 or dc > 1:
                            pk = (rc, dc - 2) if dc >= 2 else (rc - 1, 2 + dc)
                            wts.append(prev_dn[pk])
                        b.op("tensor", _dmm, waits=wts, incs=[(TEN, 1)])
                        dn_done = TEN.done
                        if rc == RS[s] - 1 and dc == 3:
                            last_dn_mm = dn_done

                        wts_e = [dn_done, wfill_done]
                        if dc == 0 and rc % 2 == 0 and grp >= 2:
                            wts_e.append((SSM, SSM.n - 16))
                        b.op("vector", lambda e, gch=gch, dc=dc, ps=ps, ob=ob, oco=oco:
                             e.tensor_scalar_mul(ob[:, oco + dc * 512: oco + (dc + 1) * 512],
                                                 ps[:, :],
                                                 wrow_sb[:, gch:gch + 1]),
                             waits=wts_e, incs=[(VEC, 1)])
                        prev_dn[(rc, dc)] = VEC.done

                    if rc % 2 == 1:
                        wts_s = [prev_dn[(rc - 1, 3)], prev_dn[(rc, 3)], yzero_done]
                        if prev_scatter is not None:
                            wts_s.append(prev_scatter)
                        g0 = gch - 1
                        b.op("gpsimd", lambda e, g0=g0, ob=ob: e.dma_scatter_add(
                            y_main[:, :],
                            ob[:, :].rearrange("p (a m) -> p a m", a=2),
                            sidx_sb[:, g0 * 8:(g0 + 2) * 8], 256, 256, 2048),
                            waits=wts_s, incs=[(SSM, 16)])
                        prev_scatter = SSM.done

            b.op("gpsimd", lambda e: e.collective_compute(
                "ReduceScatter", ALU.add, replica_groups=[list(range(NCORES))],
                ins=[y_main.opt()], outs=[y_red.opt()]),
                waits=[prev_scatter], incs=[(CC, 1)])
            rs_done = CC.done

        abf.reset(0)
        yb_o = abf.take(2 * D)
        sb_o = abf.take(2 * D)
        yt = [abf_t[:, yb_o + i * D: yb_o + (i + 1) * D] for i in range(2)]
        st2 = [abf_t[:, sb_o + i * D: sb_o + (i + 1) * D] for i in range(2)]
        af32.reset(0)
        xf_o = af32.take(2 * D)
        g_o = af32.take(D)
        xt = [af32_t[:, xf_o + i * D: xf_o + (i + 1) * D] for i in range(2)]
        gsl = af32_t[:, g_o:g_o + D]

        prev_vec5 = {}
        prev_sca5 = {}
        for tt in range(8):
            buf = tt % 2

            def _fl(e, tt=tt, buf=buf):
                return e.dma_start(out=yt[buf],
                                   in_=y_red[tt * 128:(tt + 1) * 128, :])
            def _fl2(e, tt=tt, buf=buf):
                e.dma_start(out=st2[buf],
                            in_=s_spill[tt * 128:(tt + 1) * 128, :]).then_inc(LD2.h, 16)
                return e.dma_start(out=xt[buf],
                                   in_=x_sl[WIN + tt * 128: WIN + (tt + 1) * 128, :])
            LD2.n += 16
            wts = [rs_done, s_done]
            if tt >= 2:
                wts += [prev_vec5[tt - 2], prev_sca5[tt - 2]]
            b.op("sync", _fl, waits=wts, incs=[(LD, 16)])
            b.op("scalar", _fl2, waits=wts, incs=[(LD2, 16)])
            fl_done = LD.done
            fl2_done = LD2.done

            b.op("scalar", lambda e, buf=buf: e.activation(gsl, yt[buf], AF.Silu),
                 waits=[fl_done, fl2_done] + ([prev_vec5[tt - 1]] if tt >= 1 else []),
                 incs=[(SCA, 1)])
            prev_sca5[tt] = SCA.done

            b.op("vector", lambda e, buf=buf: e.tensor_mul(gsl, gsl, st2[buf]),
                 waits=[prev_sca5[tt], fl_done, fl2_done], incs=[(VEC, 1)])
            b.op("vector", lambda e, buf=buf: e.tensor_add(xt[buf], xt[buf], gsl),
                 waits=[VEC.done], incs=[(VEC, 1)])
            prev_vec5[tt] = VEC.done

            b.op("sync", lambda e, tt=tt, buf=buf: e.dma_start(
                out=out[tt * 128:(tt + 1) * 128, :], in_=xt[buf]),
                waits=[prev_vec5[tt]], incs=[(OS, 16)])

        if DEBUG_DUMPS:
            def _dbg(e):
                e.dma_start(out=dbg["d_xn"][:, :], in_=xn_loc[:, :]).then_inc(OS.h, 16)
                e.dma_start(out=dbg["d_ag"][:, :], in_=ag_out[:, :]).then_inc(OS.h, 16)
                e.dma_start(out=dbg["d_s"][:, :], in_=s_spill[:, :]).then_inc(OS.h, 16)
                e.dma_start(out=dbg["d_ym"][:, :], in_=y_main[:, :]).then_inc(OS.h, 16)
                return e.dma_start(out=dbg["d_yr"][:, :], in_=y_red[:, :])
            OS.n += 4 * 16
            b.op("sync", _dbg, waits=[rs_done, (OS, OS.n - 4 * 16)], incs=[(OS, 16)])
        b.op("sync", lambda e: e.nop(), waits=[(OS, OS.n)])

        with nc.Block() as block:
            b.emit(block)

    return nc



def kernel(**inputs):
    x = np.ascontiguousarray(np.asarray(inputs["L0_attn_x"])[0], np.float32)
    Wq = np.asarray(inputs["L0_attn_Wq"], np.float32)
    Wkv = np.asarray(inputs["L0_attn_Wkv"], np.float32)
    q_scale = np.asarray(inputs["L0_attn_q_scale"], np.float32)
    k_scale = np.asarray(inputs["L0_attn_k_scale"], np.float32)
    Wg = np.asarray(inputs["L0_attn_Wg"], np.float32)
    bg = np.asarray(inputs["L0_attn_bg"], np.float32)
    Wo = np.asarray(inputs["L0_attn_Wo"], np.float32)
    gate_w = np.asarray(inputs["L0_ffn_gate_weight"], np.float32)
    up_w = np.asarray(inputs["L0_ffn_up_weight"], np.float32)
    down_w = np.asarray(inputs["L0_ffn_down_weight"], np.float32)
    topk_idx = np.asarray(inputs["L0_ffn_topk_idx"])
    topk_weight = np.asarray(inputs["L0_ffn_topk_weight"], np.float32)

    experts, extw = _route(topk_idx, topk_weight)
    meta, cores = _plan(experts, extw)
    assert max(meta["RS"]) <= 14, "unexpectedly large expert group"
    assert meta["EWC"] <= 16

    nc = _build(meta)
    mybir.codegen_inst_isa_subclasses(nc)

    wqT = np.ascontiguousarray(Wq.T)
    wkvT = np.ascontiguousarray(Wkv.T)
    wgaT = np.ascontiguousarray(Wg.T)
    woT = np.ascontiguousarray(Wo.T)
    bgr = np.ascontiguousarray(bg.reshape(1, H))
    qsc = np.ascontiguousarray(q_scale.reshape(128, 1))
    ksc = np.ascontiguousarray(k_scale.reshape(128, 1))
    identb = np.eye(128, dtype=BF)
    onesc = np.ones((128, 1), BF)
    onesr = np.ones((1, 128), BF)

    qi = np.arange(128)[:, None]
    jj = np.arange(640)[None, :]
    base_ok = (jj >= qi) & (jj <= qi + WIN)
    masks0 = np.zeros((5, 128, 640), np.float32)
    for t in range(4):
        ok = base_ok & ((t * 128 + jj - WIN) >= 0)
        masks0[t] = np.where(ok, 0.0, MASKVAL)
    masks0[4] = np.where(base_ok, 0.0, MASKVAL)
    masks_other = np.broadcast_to(masks0[4], (5, 128, 640)).copy()
    masks_other = np.ascontiguousarray(masks_other)

    in_maps = []
    for c in range(NCORES):
        lo = c * TPC - WIN
        if lo < 0:
            x_slc = np.concatenate(
                [np.zeros((WIN, D), np.float32), x[0:TPC]], axis=0)
        else:
            x_slc = x[lo:c * TPC + TPC]
        x_slc = np.ascontiguousarray(x_slc)
        eo = cores[c]["expert_of"]
        wgS = np.ascontiguousarray(np.stack([gate_w[e].T for e in eo]))
        wuS = np.ascontiguousarray(np.stack([up_w[e].T for e in eo]))
        wdS = np.ascontiguousarray(np.stack([down_w[e].T for e in eo]))
        in_maps.append({
            "x_sl": x_slc, "wqT": wqT, "wkvT": wkvT, "wgaT": wgaT,
            "woT": woT, "bgr": bgr, "qsc": qsc, "ksc": ksc,
            "maskt": masks0 if c == 0 else masks_other,
            "identb": identb, "onesc": onesc, "onesr": onesr,
            "wgS": wgS, "wuS": wuS, "wdS": wdS,
            "gidx": cores[c]["gidx"], "sidx": cores[c]["sidx"],
            "wrow": cores[c]["wrow"], "extw": cores[c]["extw"],
        })

    res = run_bass_kernel_spmd(nc, in_maps, core_ids=list(range(NCORES)))
    shards = [res.results[c]["out"] for c in range(NCORES)]
    full = np.concatenate(shards, axis=0).reshape(1, NTOK, D)
    return full


# version 09
# speedup vs baseline: 1.1435x; 1.0350x over previous
import numpy as np
import ml_dtypes
import concourse.bass as bass
import concourse.mybir as mybir
from concourse.bass_utils import run_bass_kernel_spmd
from concourse.library_config import mlp as _mlp_lib

F32 = mybir.dt.float32
BF16 = mybir.dt.bfloat16
I16 = mybir.dt.int16
AF = mybir.ActivationFunctionType
ALU = mybir.AluOpType
AX = mybir.AxisListType
BF = ml_dtypes.bfloat16

NCORES = 8
NTOK = 8192
D = 2048
H = 8
DH = 128
WIN = 512
NE = 32
TPC = NTOK // NCORES
SL = TPC + WIN
NT = SL // 128
NSLOT = 4
MASKVAL = -30000.0
YROWS = NTOK + 128

ENGINES = ("sync", "tensor", "vector", "scalar", "gpsimd")
DEBUG_DUMPS = False
SKIP_ATTN = False
SKIP_MOE = False



def _pack_idx(vals, ncols):
    arr = np.zeros((128, ncols), np.int16)
    j = np.arange(len(vals))
    v = np.asarray(vals, np.int16)
    arr[j % 16, j // 16] = v
    arr[16 + j % 16, j // 16] = v
    return arr


def _route(topk_idx, topk_weight):
    ti = np.asarray(topk_idx).reshape(-1).astype(np.int64)
    tw = np.asarray(topk_weight).reshape(-1).astype(np.float32)
    nreg = NTOK * 4
    reg_e, reg_w = ti[:nreg], tw[:nreg]
    reg_t = np.repeat(np.arange(NTOK), 4)
    ext_e, ext_w = ti[nreg:], tw[nreg:]

    experts = []
    for e in range(NE):
        m = reg_e == e
        toks = reg_t[m]
        ws = reg_w[m]
        rows = [(NTOK - 1, 0.0, True)]
        rows += list(zip(toks.tolist(), ws.tolist(), [False] * len(toks)))
        groups, group_toks = [], []
        for r in rows:
            t = r[0]
            for ci in range(len(groups)):
                if len(groups[ci]) < 256 and t not in group_toks[ci]:
                    groups[ci].append(r)
                    group_toks[ci].add(t)
                    break
            else:
                groups.append([r])
                group_toks.append({t})
        chunks = []
        for g in groups:
            chunks.append(g[0:128])
            chunks.append(g[128:256])
        experts.append(chunks)
    extw = [ext_w[ext_e == e] for e in range(NE)]
    return experts, extw


def _plan(experts, extw):
    nch = np.array([len(c) for c in experts])
    order = np.argsort(-nch, kind="stable")
    expert_of = [[int(order[s * NCORES + c]) for s in range(NSLOT)]
                 for c in range(NCORES)]
    RS = [int(max(nch[order[s * NCORES + c]] for c in range(NCORES)))
          for s in range(NSLOT)]
    EWC = max(1, -(-int(max(len(w) for w in extw)) // 128))
    NCH = sum(RS)

    batches = []
    base = 0
    for s in range(NSLOT):
        c0 = 0
        while c0 < RS[s]:
            nb = min(4, RS[s] - c0)
            batches.append((s, base + c0, nb))
            c0 += nb
        base += RS[s]

    cores = []
    spec_pos = None
    for c in range(NCORES):
        gi = np.zeros(NCH * 128, np.int64)
        si = np.zeros(NCH * 128, np.int64)
        wr = np.zeros((128, NCH), np.float32)
        sp = []
        gch = 0
        for s in range(NSLOT):
            e = expert_of[c][s]
            chunks = experts[e]
            for lc in range(RS[s]):
                rows = chunks[lc] if lc < len(chunks) else []
                for j in range(128):
                    p = gch * 128 + j
                    if j < len(rows):
                        t, w, spc = rows[j]
                        gi[p] = t
                        si[p] = t
                        wr[j, gch] = w
                        if spc:
                            sp.append((gch, j))
                    else:
                        gi[p] = 0
                        si[p] = NTOK + j
                gch += 1
        for g2 in range(0, NCH, 2):
            seg = si[g2 * 128:(g2 + 2) * 128]
            used = set(seg[seg < NTOK].tolist())
            cand = 0
            for kk in range(256):
                if seg[kk] >= NTOK:
                    while cand in used:
                        cand += 1
                    seg[kk] = cand
                    used.add(cand)
        assert len(sp) == NSLOT
        ew = np.zeros((128, NSLOT * EWC), np.float32)
        for s in range(NSLOT):
            e = expert_of[c][s]
            assert sp[s][1] == 0
            vals = extw[e]
            tmp = np.zeros(128 * EWC, np.float32)
            tmp[:len(vals)] = vals
            ew[:, s * EWC:(s + 1) * EWC] = tmp.reshape(EWC, 128).T
        if spec_pos is None:
            spec_pos = sp
        else:
            assert sp == spec_pos, "special positions must be core-uniform"
        cores.append(dict(expert_of=expert_of[c],
                          gidx=_pack_idx(gi, NCH * 8),
                          sidx=_pack_idx(si, NCH * 8),
                          wrow=wr, extw=ew))
    return dict(RS=RS, NCH=NCH, EWC=EWC, batches=batches, spec_pos=spec_pos), cores



class _Sem:
    __slots__ = ("h", "n", "name")

    def __init__(self, h, name):
        self.h = h
        self.n = 0
        self.name = name

    @property
    def done(self):
        return (self, self.n)


class _Builder:
    def __init__(self):
        self.ops = {e: [] for e in ENGINES}

    def op(self, eng, fn, waits=(), incs=()):
        for s, k in incs:
            s.n += k
        self.ops[eng].append((tuple(w for w in waits if w is not None),
                              fn, tuple(incs)))

    def emit(self, block):
        for eng in ENGINES:
            oplist = self.ops[eng]

            def body(e, _ops=oplist):
                for waits, fn, incs in _ops:
                    for s, v in waits:
                        if v > 0:
                            e.wait_ge(s.h, v)
                    ins = fn(e)
                    for s, k in incs:
                        ins.then_inc(s.h, k)
            getattr(block, eng)(body)


class _Span:
    def __init__(self, t, cols):
        self.t = t
        self.cols = cols
        self.pos = 0

    def take(self, n):
        assert self.pos + n <= self.cols, f"arena overflow {self.pos}+{n}"
        o = self.pos
        self.pos += n
        return o

    def mark(self):
        return self.pos

    def reset(self, pos):
        self.pos = pos



def _build(meta):
    RS = meta["RS"]
    NCH = meta["NCH"]
    EWC = meta["EWC"]
    batches = meta["batches"]
    spec_pos = meta["spec_pos"]
    RSMAX = max(RS)

    nc = bass.Bass("TRN2", target_bir_lowering=False, num_devices=NCORES)

    def din(name, shape, dt):
        return nc.dram_tensor(name, list(shape), dt, kind="ExternalInput").ap()

    x_sl = din("x_sl", (SL, D), F32)
    wqT = din("wqT", (D, H * DH), F32)
    wkvT = din("wkvT", (D, 2 * H * DH), F32)
    wgaT = din("wgaT", (D, H), F32)
    woT = din("woT", (H * DH, D), F32)
    bgr = din("bgr", (1, H), F32)
    qsc = din("qsc", (128, 1), F32)
    ksc = din("ksc", (128, 1), F32)
    maskt = din("maskt", (5, 128, 640), F32)
    identb = din("identb", (128, 128), BF16)
    onesc = din("onesc", (128, 1), BF16)
    onesr = din("onesr", (1, 128), BF16)
    wgS = din("wgS", (NSLOT, D, 1024), F32)
    wuS = din("wuS", (NSLOT, D, 1024), F32)
    wdS = din("wdS", (NSLOT, 1024, D), F32)
    gidx = din("gidx", (128, NCH * 8), I16)
    sidx = din("sidx", (128, NCH * 8), I16)
    wrow = din("wrow", (128, NCH), F32)
    extw = din("extw", (128, NSLOT * EWC), F32)

    out = nc.dram_tensor("out", [TPC, D], F32, kind="ExternalOutput").ap()
    dbg = {}
    if DEBUG_DUMPS:
        for nm, shp in (("d_xn", [SL, D]), ("d_ag", [NTOK, D]),
                        ("d_s", [TPC, D]), ("d_ym", [NTOK, D]),
                        ("d_yr", [TPC, D])):
            dbg[nm] = nc.dram_tensor(nm, shp, BF16, kind="ExternalOutput").ap()

    xn_loc = nc.dram_tensor("xn_loc", [SL, D], BF16).ap()
    ag_in = nc.dram_tensor("ag_in", [TPC, D], BF16).ap()
    ag_out = nc.dram_tensor("ag_out", [NTOK, D], BF16).ap()
    y_main = nc.dram_tensor("y_main", [NTOK, D], BF16).ap()
    y_red = nc.dram_tensor("y_red", [TPC, D], BF16).ap()
    s_spill = nc.dram_tensor("s_spill", [TPC, D], BF16).ap()

    from contextlib import ExitStack
    stack = ExitStack()
    with stack:
        def sem(name):
            return _Sem(stack.enter_context(nc.semaphore(name)), name)

        def sbuf(name, shape, dt):
            return stack.enter_context(nc.sbuf_tensor(name, list(shape), dt))

        def psum(name, shape, dt):
            return stack.enter_context(nc.psum_tensor(name, list(shape), dt))

        LD = sem("LD")
        LD2 = sem("LD2")
        WSM = sem("WSM")
        GSM = sem("GSM")
        SSM = sem("SSM")
        CC = sem("CC")
        TEN = sem("TEN")
        VEC = sem("VEC")
        SCA = sem("SCA")
        XLW = sem("XLW")
        AGW = sem("AGW")
        TRS = sem("TRS")
        SSP = sem("SSP")
        OS = sem("OS")
        ZW = sem("ZW")

        ident_sb = sbuf("ident_sb", (128, 128), BF16)
        ones128_sb = sbuf("ones128_sb", (128, 128), BF16)
        onesc_sb = sbuf("onesc_sb", (128, 1), BF16)
        onesr_sb = sbuf("onesr_sb", (1, 128), BF16)
        bgr_sb = sbuf("bgr_sb", (1, H), BF16)
        wga_sb = sbuf("wga_sb", (128, 16 * H), BF16)
        ccol_sb = sbuf("ccol_sb", (128, 1), F32)
        sc1_sb = sbuf("sc1_sb", (128, 4), F32)
        gidx_sb = sbuf("gidx_sb", (128, NCH * 8), I16)
        sidx_sb = sbuf("sidx_sb", (128, NCH * 8), I16)
        wrow_sb = sbuf("wrow_sb", (128, NCH), F32)
        cext_sb = sbuf("cext_sb", (128, NSLOT), F32)
        extw_sb = sbuf("extw_sb", (128, NSLOT * EWC), F32)
        extb_sb = sbuf("extb_sb", (128, NSLOT * EWC), BF16)
        qsc_sb = sbuf("qsc_sb", (128, 1), F32)
        ksc_sb = sbuf("ksc_sb", (128, 1), F32)

        ABF_COLS = 88100
        AF32_COLS = 8000
        abf_t = sbuf("abf", (128, ABF_COLS), BF16)
        af32_t = sbuf("af32", (128, AF32_COLS), F32)
        abf = _Span(abf_t, ABF_COLS)
        af32 = _Span(af32_t, AF32_COLS)

        psA = psum("psA", (128, 512), F32)
        psB = psum("psB", (128, 512), F32)
        psC = psum("psC", (128, 512), F32)
        psD = psum("psD", (128, 512), F32)
        psE = psum("psE", (128, 640), F32)
        psF1 = psum("psF1", (128, 128), BF16)
        psF2 = psum("psF2", (128, 128), BF16)

        b = _Builder()

        for dst, src in ((ident_sb, identb), (onesc_sb, onesc),
                         (onesr_sb, onesr), (gidx_sb, gidx), (sidx_sb, sidx),
                         (wrow_sb, wrow), (extw_sb, extw), (qsc_sb, qsc),
                         (ksc_sb, ksc)):
            b.op("sync", lambda e, d=dst, s=src: e.dma_start(out=d[:, :], in_=s[:, :]),
                 incs=[(LD, 16)])
        ph0_ld = LD.done

        b.op("gpsimd", lambda e: e.load_library(_mlp_lib))
        b.op("gpsimd", lambda e: e.dma_start(out=bgr_sb[:, :], in_=bgr[:, :]),
             incs=[(WSM, 16)])
        b.op("gpsimd", lambda e: e.dma_start(
            out=wga_sb[:, :].rearrange("p (c m) -> p c m", m=H),
            in_=wgaT.rearrange("(c p) m -> p c m", p=128)),
            incs=[(WSM, 16)])
        wga_done = WSM.done

        wreg_o = abf.take(8192)
        reg2_o = abf.take(6144)
        xnT_o = abf.take(16 * SL)
        q_o = abf.take(H * TPC)
        k_o = abf.take(H * SL)
        v_o = abf.take(12 * 1024)
        o_o = abf.take(H * TPC)

        wreg = abf_t[:, wreg_o:wreg_o + 8192]
        xnT = [abf_t[:, xnT_o + hc * SL: xnT_o + (hc + 1) * SL] for hc in range(16)]
        q_sb = abf_t[:, q_o:q_o + H * TPC]
        k_sb = abf_t[:, k_o:k_o + H * SL]
        v_sb = abf_t[:, v_o:v_o + 12 * 1024]
        o_sb = abf_t[:, o_o:o_o + H * TPC]

        xn2 = [abf_t[:, reg2_o + i * D: reg2_o + (i + 1) * D] for i in range(2)]
        squt = abf_t[:, reg2_o + 2 * D: reg2_o + 3 * D]
        qsq = [abf_t[:, reg2_o + i * 512: reg2_o + (i + 1) * 512] for i in range(2)]
        ksq = [abf_t[:, reg2_o + 1024 + i * 512: reg2_o + 1024 + (i + 1) * 512]
               for i in range(2)]
        bcT = abf_t[:, reg2_o + 2048: reg2_o + 2176]
        attbf = [abf_t[:, reg2_o + 2176 + i * 640: reg2_o + 2176 + (i + 1) * 640]
                 for i in range(2)]
        attT = [abf_t[:, reg2_o + 3456 + i * 640: reg2_o + 3456 + (i + 1) * 640]
                for i in range(2)]
        s_half = [abf_t[:, reg2_o + i * 1024: reg2_o + (i + 1) * 1024]
                  for i in range(2)]

        x2_o = af32.take(2 * D)
        lst_o = af32.take(16)
        rq_o = af32.take(64)
        gt_o = af32.take(64)
        rk_o = af32.take(96)
        ms_o = af32.take(1280)
        atf_o = af32.take(1280)
        st_o = af32.take(32)
        sil_o = af32.take(1024)
        x2 = [af32_t[:, x2_o + i * D: x2_o + (i + 1) * D] for i in range(2)]
        lst = af32_t[:, lst_o:lst_o + 16]
        rq_tok = af32_t[:, rq_o:rq_o + 64]
        gate_tok = af32_t[:, gt_o:gt_o + 64]
        rk_tok = af32_t[:, rk_o:rk_o + 96]
        mask_sb = af32_t[:, x2_o:x2_o + 5 * 640]
        ms = [af32_t[:, ms_o + i * 640: ms_o + (i + 1) * 640] for i in range(2)]
        attf = [af32_t[:, atf_o + i * 640: atf_o + (i + 1) * 640] for i in range(2)]
        stt = [af32_t[:, st_o + i * 16: st_o + (i + 1) * 16] for i in range(2)]
        sil = [af32_t[:, sil_o:sil_o + 512], af32_t[:, sil_o + 512:sil_o + 1024]]

        b.op("vector", lambda e: e.memset(ones128_sb[:, :], 1.0),
             incs=[(VEC, 1)])
        zz_done = VEC.done

        b.op("vector", lambda e: e.tensor_mul(sc1_sb[:, 0:1], qsc_sb[:, :],
                                              ksc_sb[:, :]),
             waits=[ph0_ld], incs=[(VEC, 1)])
        b.op("vector", lambda e: e.tensor_scalar_mul(ccol_sb[:, :],
                                                     sc1_sb[:, 0:1], 8.0),
             waits=[VEC.done], incs=[(VEC, 1)])
        ccol_done = VEC.done

        b.op("vector", lambda e: e.tensor_copy(extb_sb[:, :], extw_sb[:, :]),
             waits=[ph0_ld], incs=[(VEC, 1)])
        extb_done = VEC.done

        def _extmm(e):
            last = None
            for s in range(NSLOT):
                last = e.matmul(psC[0:1, s * EWC:(s + 1) * EWC],
                                onesc_sb[:, :],
                                extb_sb[:, s * EWC:(s + 1) * EWC])
            return last
        b.op("tensor", _extmm, waits=[extb_done], incs=[(TEN, 1)])
        extmm_done = TEN.done

        def _extred(e):
            last = None
            for s in range(NSLOT):
                last = e.reduce_sum(cext_sb[0:1, s:s + 1],
                                    psC[0:1, s * EWC:(s + 1) * EWC], axis=AX.X)
            return last
        b.op("vector", _extred, waits=[extmm_done], incs=[(VEC, 1)])

        def _extcp(e):
            last = None
            for s in range(NSLOT):
                gch, p = spec_pos[s]
                last = e.tensor_copy(wrow_sb[0:1, gch:gch + 1],
                                     cext_sb[0:1, s:s + 1])
            return last
        b.op("vector", _extcp, waits=[VEC.done], incs=[(VEC, 1)])
        wfill_done = VEC.done

        zv = abf_t[:, o_o:o_o + D]
        b.op("vector", lambda e: e.memset(zv, 0.0), incs=[(VEC, 1)])
        zv_done = VEC.done
        for zi in range(NTOK // 128):
            b.op("gpsimd", lambda e, zi=zi: e.dma_start(
                out=y_main[zi * 128:(zi + 1) * 128, :], in_=zv),
                waits=([zv_done] if zi == 0 else []), incs=[(ZW, 16)])
        yzero_done = ZW.done

        xld, xvec, xsca, xwr = [None] * NT, [None] * NT, [None] * NT, [None] * NT
        trs_ten = [None] * NT
        evac_bank = {0: None, 1: None}

        def _emit_trs(i, xvec):
            for hc in range(16):
                pf = psF1 if hc % 2 == 0 else psF2
                wts_t = [xvec[i], ph0_ld]
                if evac_bank[hc % 2] is not None:
                    wts_t.append(evac_bank[hc % 2])
                b.op("tensor", lambda e, i=i, hc=hc, pf=pf: e.transpose(
                    pf[:, :], xn2[i % 2][:, hc * 128:(hc + 1) * 128],
                    ident_sb[:, :]),
                    waits=wts_t, incs=[(TEN, 1)])
                t_done = TEN.done
                b.op("scalar", lambda e, i=i, hc=hc, pf=pf: e.activation(
                    xnT[hc][:, i * 128:(i + 1) * 128], pf[:, :], AF.Copy),
                    waits=[t_done], incs=[(SCA, 1)])
                evac_bank[hc % 2] = SCA.done
            trs_ten[i] = (TEN, TEN.n)

        for i in range(NT):
            waits = [xvec[i - 2], xsca[i - 2]] if i >= 2 else []
            b.op("sync", lambda e, i=i: e.dma_start(
                out=x2[i % 2], in_=x_sl[i * 128:(i + 1) * 128, :]),
                waits=waits, incs=[(LD, 16)])
            xld[i] = LD.done

            sqc = 1 + 8 * (i % 2)
            b.op("scalar", lambda e, i=i, sqc=sqc: e.activation(
                squt, x2[i % 2], AF.Square, accum_out=lst[:, sqc:sqc + 1]),
                waits=[xld[i]], incs=[(SCA, 1)])
            ssq_done = SCA.done
            if i >= 1:
                _emit_trs(i - 1, xvec)

            wv = [xld[i], ssq_done]
            if i >= 2 and xwr[i - 2] is not None:
                wv.append(xwr[i - 2])
            b.op("vector", lambda e, i=i: e.reduce_sum(lst[:, 0:1], x2[i % 2],
                                                       axis=AX.X),
                 waits=wv, incs=[(VEC, 1)])
            b.op("vector", lambda e: e.tensor_scalar_mul(lst[:, 2:3],
                                                         lst[:, 0:1], 1.0 / D),
                 waits=[VEC.done], incs=[(VEC, 1)])
            b.op("vector", lambda e, sqc=sqc: e.tensor_scalar_mul(
                lst[:, 3:4], lst[:, sqc:sqc + 1], 1.0 / D),
                waits=[VEC.done], incs=[(VEC, 1)])
            b.op("vector", lambda e: e.tensor_mul(lst[:, 4:5], lst[:, 2:3],
                                                  lst[:, 2:3]),
                 waits=[VEC.done], incs=[(VEC, 1)])
            b.op("vector", lambda e: e.tensor_sub(lst[:, 5:6], lst[:, 3:4],
                                                  lst[:, 4:5]),
                 waits=[VEC.done], incs=[(VEC, 1)])
            var_done = VEC.done

            b.op("vector", lambda e: e.tensor_scalar_add(
                lst[:, 8:9], lst[:, 5:6], 1e-5),
                waits=[var_done], incs=[(VEC, 1)])
            epsd = VEC.done
            b.op("scalar", lambda e: e.activation(
                lst[:, 6:7], lst[:, 8:9], AF.Sqrt),
                waits=[epsd], incs=[(SCA, 1)])
            xsca[i] = SCA.done

            b.op("vector", lambda e: e.reciprocal(lst[:, 7:8], lst[:, 6:7]),
                 waits=[xsca[i]], incs=[(VEC, 1)])
            rstd_done = VEC.done
            wts_ap = [rstd_done]
            if i >= 2:
                wts_ap.append(trs_ten[i - 2])
            b.op("vector", lambda e, i=i: e.tensor_scalar(
                xn2[i % 2], x2[i % 2], lst[:, 2:3], lst[:, 7:8],
                op0=ALU.subtract, op1=ALU.mult),
                waits=wts_ap, incs=[(VEC, 1)])
            xvec[i] = VEC.done

            if i >= 4:
                def _lnw(e, i=i):
                    return e.dma_start(
                        out=ag_in[(i - 4) * 128:(i - 3) * 128, :],
                        in_=xn2[i % 2])
                b.op("sync", _lnw, waits=[xvec[i]], incs=[(AGW, 16)])
                xwr[i] = AGW.done
        _emit_trs(NT - 1, xvec)
        ln_done = (VEC, VEC.n)
        ln_sca_done = (SCA, SCA.n)
        xnT_done = (SCA, SCA.n)

        b.op("gpsimd", lambda e: e.collective_compute(
            "AllGather", ALU.bypass, replica_groups=[list(range(NCORES))],
            ins=[ag_in.opt()], outs=[ag_out.opt()]),
            waits=[(AGW, AGW.n)], incs=[(CC, 1)])
        ag_done = CC.done

        if SKIP_ATTN:
            s_done = (SSP, 0)
            attn_skip = True
        else:
            attn_skip = False
        if not attn_skip:
            prev_half_last = None
            ev_q = {}
            for half in range(2):
                wsrc = wqT.rearrange("(c p) m -> p c m", p=128)[:, :, half * 512:(half + 1) * 512]
                wts = [] if half == 0 else [prev_half_last]
                b.op("gpsimd", lambda e, s=wsrc: e.dma_start(
                    out=wreg[:, :].rearrange("p (c m) -> p c m", m=512), in_=s),
                    waits=wts, incs=[(WSM, 16)])
                wq_ld = WSM.done
                for hh in range(4):
                    h = half * 4 + hh
                    for bb in range(2):
                        u = h * 2 + bb
                        ps = psA if u % 2 == 0 else psB

                        def _qmm(e, hh=hh, bb=bb, ps=ps):
                            last = None
                            for hc in range(16):
                                last = e.matmul(
                                    ps[:, :],
                                    wreg[:, hc * 512 + hh * 128: hc * 512 + (hh + 1) * 128],
                                    xnT[hc][:, WIN + bb * 512: WIN + (bb + 1) * 512],
                                    start=(hc == 0), stop=(hc == 15))
                            return last
                        wts = [wq_ld, ccol_done, xnT_done]
                        if u >= 2:
                            wts.append(ev_q[u - 2])
                        b.op("tensor", _qmm, waits=wts, incs=[(TEN, 1)])
                        mm_done = TEN.done

                        b.op("vector", lambda e, h=h, bb=bb, ps=ps: e.tensor_scalar_mul(
                            q_sb[:, h * TPC + bb * 512: h * TPC + (bb + 1) * 512],
                            ps[:, :], ccol_sb[:, :]),
                            waits=[mm_done], incs=[(VEC, 1)])
                        evd = VEC.done

                        b.op("scalar", lambda e, bb=bb, ps=ps: e.activation(
                            qsq[bb], ps[:, :], AF.Square),
                            waits=[evd], incs=[(SCA, 1)])
                        sqd = SCA.done

                        def _ssmm(e, bb=bb):
                            last = None
                            for j in range(4):
                                last = e.matmul(psC[:, j:j + 1],
                                                qsq[bb][:, j * 128:(j + 1) * 128],
                                                onesc_sb[:, :])
                            return last
                        b.op("tensor", _ssmm, waits=[sqd], incs=[(TEN, 1)])
                        ssd = TEN.done

                        def _ssev(e, h=h, bb=bb):
                            last = None
                            for j in range(4):
                                t = bb * 4 + j
                                last = e.tensor_copy(
                                    rq_tok[:, t * 8 + h: t * 8 + h + 1],
                                    psC[:, j:j + 1])
                            return last
                        b.op("vector", _ssev, waits=[ssd], incs=[(VEC, 1)])
                        ev_q[u] = VEC.done
                prev_half_last = ev_q[half * 8 + 7]

            b.op("scalar", lambda e: e.activation(rq_tok[:, :], rq_tok[:, :], AF.Sqrt),
                 waits=[ev_q[15]], incs=[(SCA, 1)])

            b.op("vector", lambda e: e.tensor_scalar_max(
                rq_tok[:, :], rq_tok[:, :], 1e-12),
                waits=[SCA.done], incs=[(VEC, 1)])
            b.op("vector", lambda e: e.reciprocal(rq_tok[:, :], rq_tok[:, :]),
                 waits=[VEC.done], incs=[(VEC, 1)])
            rq_done = VEC.done

            prev_gate = None
            for t in range(8):
                def _gmm(e, t=t):
                    for hc in range(16):
                        e.matmul(psD[:, 0:8],
                                 xnT[hc][:, WIN + t * 128: WIN + (t + 1) * 128],
                                 wga_sb[:, hc * H:(hc + 1) * H],
                                 start=(hc == 0), stop=False)
                    return e.matmul(psD[:, 0:8], onesr_sb[:, :], bgr_sb[:, :],
                                    start=False, stop=True)
                wts = [xnT_done, wga_done, ph0_ld]
                if prev_gate is not None:
                    wts.append(prev_gate)
                b.op("tensor", _gmm, waits=wts, incs=[(TEN, 1)])
                b.op("scalar", lambda e, t=t: e.activation(
                    gate_tok[:, t * 8:(t + 1) * 8], psD[:, 0:8], AF.Sigmoid),
                    waits=[TEN.done], incs=[(SCA, 1)])
                prev_gate = SCA.done
            gates_done = prev_gate

            ev_k = {}
            for half in range(2):
                wsrc = wkvT.rearrange("(c p) m -> p c m", p=128)[:, :, half * 512:(half + 1) * 512]
                b.op("gpsimd", lambda e, s=wsrc: e.dma_start(
                    out=wreg[:, :].rearrange("p (c m) -> p c m", m=512), in_=s),
                    waits=[ev_q[15] if half == 0 else ev_k[11]],
                    incs=[(WSM, 16)])
                wk_ld = WSM.done
                for hh in range(4):
                    h = half * 4 + hh
                    for b3 in range(3):
                        u = h * 3 + b3
                        ps = psA if u % 2 == 0 else psB

                        def _kmm(e, hh=hh, b3=b3, ps=ps):
                            last = None
                            for hc in range(16):
                                last = e.matmul(
                                    ps[:, :],
                                    wreg[:, hc * 512 + hh * 128: hc * 512 + (hh + 1) * 128],
                                    xnT[hc][:, b3 * 512:(b3 + 1) * 512],
                                    start=(hc == 0), stop=(hc == 15))
                            return last
                        wts = [wk_ld]
                        if u >= 2:
                            wts.append(ev_k[u - 2])
                        b.op("tensor", _kmm, waits=wts, incs=[(TEN, 1)])
                        mm_done = TEN.done

                        b.op("vector", lambda e, h=h, b3=b3, ps=ps: e.tensor_copy(
                            k_sb[:, h * SL + b3 * 512: h * SL + (b3 + 1) * 512],
                            ps[:, :]),
                            waits=[mm_done], incs=[(VEC, 1)])
                        evd = VEC.done

                        b.op("scalar", lambda e, b3=b3, ps=ps: e.activation(
                            ksq[b3 % 2], ps[:, :], AF.Square),
                            waits=[evd], incs=[(SCA, 1)])
                        sqd = SCA.done

                        def _ssmm(e, b3=b3):
                            last = None
                            for j in range(4):
                                last = e.matmul(psC[:, j:j + 1],
                                                ksq[b3 % 2][:, j * 128:(j + 1) * 128],
                                                onesc_sb[:, :])
                            return last
                        b.op("tensor", _ssmm, waits=[sqd], incs=[(TEN, 1)])
                        ssd = TEN.done

                        def _ssev(e, h=h, b3=b3):
                            last = None
                            for j in range(4):
                                ch = b3 * 4 + j
                                last = e.tensor_copy(
                                    rk_tok[:, ch * 8 + h: ch * 8 + h + 1],
                                    psC[:, j:j + 1])
                            return last
                        b.op("vector", _ssev, waits=[ssd], incs=[(VEC, 1)])
                        ev_k[u] = VEC.done
            k_done = ev_k[23]

            ev_v = {}
            for half in range(2):
                wsrc = wkvT.rearrange("(c p) m -> p c m", p=128)[:, :, 1024 + half * 512: 1024 + (half + 1) * 512]
                b.op("gpsimd", lambda e, s=wsrc: e.dma_start(
                    out=wreg[:, :].rearrange("p (c m) -> p c m", m=512), in_=s),
                    waits=[k_done if half == 0 else ev_v[11]],
                    incs=[(WSM, 16)])
                wv_ld = WSM.done
                for ch in range(12):
                    u = half * 12 + ch
                    ps = psA if u % 2 == 0 else psB

                    def _vmm(e, ch=ch, ps=ps):
                        last = None
                        for hc in range(16):
                            last = e.matmul(
                                ps[:, :],
                                xnT[hc][:, ch * 128:(ch + 1) * 128],
                                wreg[:, hc * 512:(hc + 1) * 512],
                                start=(hc == 0), stop=(hc == 15))
                        return last
                    wts = [wv_ld]
                    if u >= 2:
                        wts.append(ev_v[u - 2])
                    b.op("tensor", _vmm, waits=wts, incs=[(TEN, 1)])
                    b.op("vector", lambda e, ch=ch, half=half, ps=ps: e.tensor_copy(
                        v_sb[:, ch * 1024 + half * 512: ch * 1024 + (half + 1) * 512],
                        ps[:, :]),
                        waits=[TEN.done], incs=[(VEC, 1)])
                    ev_v[u] = VEC.done
            v_done = ev_v[23]

            b.op("scalar", lambda e: e.activation(rk_tok[:, :], rk_tok[:, :], AF.Sqrt),
                 waits=[k_done], incs=[(SCA, 1)])

            b.op("vector", lambda e: e.tensor_scalar_max(
                rk_tok[:, :], rk_tok[:, :], 1e-12),
                waits=[SCA.done], incs=[(VEC, 1)])
            b.op("vector", lambda e: e.reciprocal(rk_tok[:, :], rk_tok[:, :]),
                 waits=[VEC.done], incs=[(VEC, 1)])
            rk_done = VEC.done

            prev_kh = {}
            for h in range(H):
                for ch in range(12):
                    u = h * 12 + ch
                    psf = psF1 if u % 2 == 0 else psF2

                    wts = [rk_done, zz_done]
                    if u >= 1:
                        wts.append(prev_kh[u - 1][0])
                    b.op("vector", lambda e, h=h, ch=ch: e.tensor_scalar_mul(
                        bcT, ones128_sb[:, :],
                        rk_tok[:, (ch * 8 + h):(ch * 8 + h) + 1]),
                        waits=wts, incs=[(VEC, 1)])
                    bct_done = VEC.done

                    wts = [bct_done, ph0_ld]
                    if u >= 2:
                        wts.append(prev_kh[u - 2][1])
                    else:
                        wts.append(xnT_done)
                    if u == 0:
                        wts.append(gates_done)
                    b.op("tensor", lambda e, psf=psf: e.transpose(
                        psf[:, :], bcT, ident_sb[:, :]),
                        waits=wts, incs=[(TEN, 1)])
                    tr_done = TEN.done

                    b.op("vector", lambda e, h=h, ch=ch, psf=psf: e.tensor_mul(
                        k_sb[:, h * SL + ch * 128: h * SL + (ch + 1) * 128],
                        k_sb[:, h * SL + ch * 128: h * SL + (ch + 1) * 128],
                        psf[:, :]),
                        waits=[tr_done], incs=[(VEC, 1)])
                    prev_kh[u] = (tr_done, VEC.done)
            kh_done = prev_kh[H * 12 - 1][1]

            b.op("sync", lambda e: e.dma_start(
                out=mask_sb[:, :].rearrange("p (c m) -> p c m", m=640),
                in_=maskt.rearrange("c p m -> p c m")),
                waits=[ln_done, ln_sca_done], incs=[(LD, 16)])
            mask_done = LD.done

            s_phase_guard = v_done
            prev_madd = {}
            prev_exp2 = {}
            prev_oev = None
            prev_t_ev = {}
            for h in range(H):
                for t in range(8):
                    u = h * 8 + t
                    pp = u % 2
                    msP, attfP, sttP = ms[pp], attf[pp], stt[pp]
                    attbfP, attTP = attbf[pp], attT[pp]

                    if pp == 0:
                        def _simm(e, h=h, t=t):
                            e.matmul(psE[:, 0:512],
                                     q_sb[:, h * TPC + t * 128: h * TPC + (t + 1) * 128],
                                     k_sb[:, h * SL + t * 128: h * SL + t * 128 + 512],
                                     start=True, stop=True)
                            return e.matmul(
                                psE[:, 512:640],
                                q_sb[:, h * TPC + t * 128: h * TPC + (t + 1) * 128],
                                k_sb[:, h * SL + t * 128 + 512: h * SL + t * 128 + 640],
                                start=True, stop=True)
                    else:
                        def _simm(e, h=h, t=t):
                            e.matmul(psA[:, :],
                                     q_sb[:, h * TPC + t * 128: h * TPC + (t + 1) * 128],
                                     k_sb[:, h * SL + t * 128: h * SL + t * 128 + 512],
                                     start=True, stop=True)
                            return e.matmul(
                                psB[:, 0:128],
                                q_sb[:, h * TPC + t * 128: h * TPC + (t + 1) * 128],
                                k_sb[:, h * SL + t * 128 + 512: h * SL + t * 128 + 640],
                                start=True, stop=True)
                    wts = [kh_done, mask_done]
                    if u == 0 or u == 1:
                        wts.append(s_phase_guard)
                    if u - 2 in prev_madd:
                        wts.append(prev_madd[u - 2])
                    b.op("tensor", _simm, waits=wts, incs=[(TEN, 1)])
                    sim_done = TEN.done

                    mslot = t if t < 4 else 4
                    wts = [sim_done, rq_done]
                    if u - 2 in prev_exp2:
                        wts.append(prev_exp2[u - 2])
                    if pp == 0:
                        b.op("vector", lambda e, mslot=mslot, msP=msP: e.tensor_add(
                            msP, psE[:, :],
                            mask_sb[:, mslot * 640:(mslot + 1) * 640]),
                            waits=wts, incs=[(VEC, 1)])
                    else:
                        b.op("vector", lambda e, mslot=mslot, msP=msP: e.tensor_add(
                            msP[:, 0:512], psA[:, :],
                            mask_sb[:, mslot * 640: mslot * 640 + 512]),
                            waits=wts, incs=[(VEC, 1)])
                        b.op("vector", lambda e, mslot=mslot, msP=msP: e.tensor_add(
                            msP[:, 512:640], psB[:, 0:128],
                            mask_sb[:, mslot * 640 + 512:(mslot + 1) * 640]),
                            incs=[(VEC, 1)])
                    prev_madd[u] = VEC.done
                    b.op("vector", lambda e, msP=msP, sttP=sttP: e.reduce_max(
                        sttP[:, 0:1], msP, axis=AX.X, negate=True),
                        waits=[VEC.done], incs=[(VEC, 1)])
                    b.op("vector", lambda e, h=h, t=t, sttP=sttP: e.tensor_mul(
                        sttP[:, 1:2], sttP[:, 0:1],
                        rq_tok[:, t * 8 + h: t * 8 + h + 1]),
                        waits=[VEC.done], incs=[(VEC, 1)])
                    bias_done = VEC.done

                    b.op("scalar", lambda e, h=h, t=t, msP=msP, attfP=attfP,
                         sttP=sttP: e.activation(
                             attfP, msP, AF.Exp, bias=sttP[:, 1:2],
                             scale=rq_tok[:, t * 8 + h: t * 8 + h + 1],
                             accum_out=sttP[:, 2:3]),
                         waits=[bias_done], incs=[(SCA, 1)])
                    prev_exp2[u] = SCA.done
                    exp_done = SCA.done

                    b.op("vector", lambda e, sttP=sttP: e.reciprocal(
                        sttP[:, 3:4], sttP[:, 2:3]),
                        waits=[exp_done], incs=[(VEC, 1)])
                    b.op("vector", lambda e, h=h, t=t, attfP=attfP, sttP=sttP,
                         attbfP=attbfP: e.tensor_scalar(
                             attbfP, attfP, sttP[:, 3:4],
                             gate_tok[:, t * 8 + h: t * 8 + h + 1],
                             op0=ALU.mult, op1=ALU.mult),
                         waits=[VEC.done], incs=[(VEC, 1)])
                    attbf_done = VEC.done

                    tev = {}
                    for kk in range(5):
                        psf = psF1 if kk % 2 == 0 else psF2
                        wts = [attbf_done]
                        if kk >= 2:
                            wts.append(tev[kk - 2])
                        elif u > 0:
                            wts.append(prev_t_ev[(u - 1, 4 if kk == 0 else 3)])
                        b.op("tensor", lambda e, kk=kk, psf=psf, attbfP=attbfP:
                             e.transpose(psf[:, :],
                                         attbfP[:, kk * 128:(kk + 1) * 128],
                                         ident_sb[:, :]),
                             waits=wts, incs=[(TEN, 1)])
                        b.op("vector", lambda e, kk=kk, psf=psf, attTP=attTP:
                             e.tensor_copy(attTP[:, kk * 128:(kk + 1) * 128],
                                           psf[:, :]),
                             waits=[TEN.done], incs=[(VEC, 1)])
                        tev[kk] = VEC.done
                        prev_t_ev[(u, kk)] = VEC.done

                    def _avmm(e, h=h, t=t, attTP=attTP):
                        last = None
                        for kk in range(5):
                            last = e.matmul(
                                psD[:, 0:128],
                                v_sb[:, (t + kk) * 1024 + h * 128:
                                     (t + kk) * 1024 + (h + 1) * 128],
                                attTP[:, kk * 128:(kk + 1) * 128],
                                start=(kk == 0), stop=(kk == 4))
                        return last
                    wts = [tev[4], v_done]
                    if u == 0:
                        wts.append(gates_done)
                    if prev_oev is not None:
                        wts.append(prev_oev)
                    b.op("tensor", _avmm, waits=wts, incs=[(TEN, 1)])
                    wts_oev = [TEN.done] + ([yzero_done] if prev_oev is None
                                            else [])
                    b.op("vector", lambda e, h=h, t=t: e.tensor_copy(
                        o_sb[:, h * TPC + t * 128: h * TPC + (t + 1) * 128],
                        psD[:, 0:128]),
                        waits=wts_oev, incs=[(VEC, 1)])
                    prev_oev = VEC.done
            attn_done = prev_oev

            prev_sev = {}
            for half in range(2):
                wsrc = woT.rearrange("(c p) m -> p c m", p=128)[:, :, half * 1024:(half + 1) * 1024]
                b.op("gpsimd", lambda e, s=wsrc: e.dma_start(
                    out=wreg[:, :].rearrange("p (c m) -> p c m", m=1024), in_=s),
                    waits=[v_done if half == 0 else prev_sev[(0, 15)]],
                    incs=[(WSM, 16)])
                wo_ld = WSM.done
                for t in range(8):
                    for dci in range(2):
                        u = t * 2 + dci
                        ps = psA if u % 2 == 0 else psB

                        def _smm(e, t=t, dci=dci, ps=ps):
                            last = None
                            for ic in range(8):
                                last = e.matmul(
                                    ps[:, :],
                                    o_sb[:, ic * TPC + t * 128: ic * TPC + (t + 1) * 128],
                                    wreg[:, ic * 1024 + dci * 512: ic * 1024 + (dci + 1) * 512],
                                    start=(ic == 0), stop=(ic == 7))
                            return last
                        wts = [wo_ld, attn_done]
                        if u >= 2:
                            wts.append(prev_sev[(half, u - 2)])
                        b.op("tensor", _smm, waits=wts, incs=[(TEN, 1)])
                        b.op("vector", lambda e, t=t, dci=dci, ps=ps: e.tensor_copy(
                            s_half[t % 2][:, dci * 512:(dci + 1) * 512], ps[:, :]),
                            waits=[TEN.done, (SSP, max(0, SSP.n - 16))],
                            incs=[(VEC, 1)])
                        prev_sev[(half, u)] = VEC.done
                    b.op("sync", lambda e, t=t, half=half: e.dma_start(
                        out=s_spill[t * 128:(t + 1) * 128,
                                    half * 1024:(half + 1) * 1024],
                        in_=s_half[t % 2]),
                        waits=[prev_sev[(half, t * 2 + 1)]], incs=[(SSP, 16)])
            s_done = SSP.done

        if SKIP_MOE:
            rs_done = (CC, CC.n)
            moe_skip = True
        else:
            moe_skip = False
        abf.reset(0)
        if not moe_skip:
            wg_o = abf.take(16 * 1024)
            wu_o = abf.take(16 * 1024)
            wd_o = abf.take(16 * 1024)
            xs_o = abf.take(2 * 16 * 512)
            mid_o = abf.take(8 * 128 * RSMAX)
            or_o = abf.take(2 * 2 * D)
            wg_sb = abf_t[:, wg_o:wg_o + 16 * 1024]
            wu_sb = abf_t[:, wu_o:wu_o + 16 * 1024]
            wd_sb = abf_t[:, wd_o:wd_o + 16 * 1024]
            xsT = [abf_t[:, xs_o + i * 8192: xs_o + (i + 1) * 8192] for i in range(2)]
            orow = [abf_t[:, or_o + i * 2 * D: or_o + (i + 1) * 2 * D] for i in range(2)]

            p2_bar = [(TEN, TEN.n), (VEC, VEC.n), (SCA, SCA.n), (SSP, SSP.n),
                  (TRS, TRS.n), (LD, LD.n)]

            slot_batches = [[bt for bt in batches if bt[0] == s] for s in range(NSLOT)]
            gather_ten = {}
            prev_scatter = None
            last_gu_mm = None
            last_dn_mm = None
            bid_global = 0
            ch_base = [0]
            for s in range(1, NSLOT):
                ch_base.append(ch_base[-1] + RS[s - 1])

            for s in range(NSLOT):
                wgsrc = wgS[s].rearrange("(c p) m -> p c m", p=128)
                wusrc = wuS[s].rearrange("(c p) m -> p c m", p=128)
                wdsrc = wdS[s].rearrange("(c p) m -> p c m", p=128)
                wts_w = list(p2_bar) if s == 0 else [last_gu_mm]
                b.op("gpsimd", lambda e, src=wgsrc: e.dma_start(
                    out=wg_sb[:, :].rearrange("p (c m) -> p c m", m=1024), in_=src),
                    waits=wts_w, incs=[(WSM, 16)])
                b.op("gpsimd", lambda e, src=wusrc: e.dma_start(
                    out=wu_sb[:, :].rearrange("p (c m) -> p c m", m=1024), in_=src),
                    incs=[(WSM, 16)])
                wgu_ld = WSM.done
                wts_d = [] if s == 0 else [last_dn_mm]
                b.op("gpsimd", lambda e, src=wdsrc: e.dma_start(
                    out=wd_sb[:, :].rearrange("p (c m) -> p c m", m=1024), in_=src),
                    waits=wts_d, incs=[(WSM, 16)])
                wd_ld = WSM.done

                dn_barrier = last_dn_mm
                prev_ev = {}
                for bi, (s_, gch0, nb) in enumerate(slot_batches[s]):
                    nidx = nb * 128
                    buf = bid_global % 2
                    wts_g = [ag_done, ph0_ld]
                    if s == 0 and bi == 0:
                        wts_g += p2_bar
                    if bid_global >= 2:
                        wts_g.append(gather_ten[bid_global - 2])
                    b.op("gpsimd", lambda e, buf=buf, gch0=gch0, nidx=nidx: e.dma_gather(
                        xsT[buf][:, 0:16 * nidx].rearrange("p (f i) -> p f i", f=16),
                        ag_out[:, :],
                        gidx_sb[:, gch0 * 8: gch0 * 8 + (nidx + 15) // 16],
                        nidx, nidx, 2048, transpose=True),
                        waits=wts_g, incs=[(GSM, 16)])
                    g_done = GSM.done

                    lc0 = gch0 - ch_base[s]
                    for ic in range(8):
                        psg = psA if ic % 2 == 0 else psC
                        psu = psB if ic % 2 == 0 else psD

                        def _gumm(e, buf=buf, ic=ic, nidx=nidx, psg=psg, psu=psu):
                            for hc in range(16):
                                e.matmul(psg[:, 0:nidx],
                                         wg_sb[:, hc * 1024 + ic * 128: hc * 1024 + (ic + 1) * 128],
                                         xsT[buf][:, hc * nidx:(hc + 1) * nidx],
                                         start=(hc == 0), stop=(hc == 15))
                            last = None
                            for hc in range(16):
                                last = e.matmul(psu[:, 0:nidx],
                                                wu_sb[:, hc * 1024 + ic * 128: hc * 1024 + (ic + 1) * 128],
                                                xsT[buf][:, hc * nidx:(hc + 1) * nidx],
                                                start=(hc == 0), stop=(hc == 15))
                            return last
                        wts = [g_done, wgu_ld]
                        if ic >= 2:
                            wts.append(prev_ev[(bi, ic - 2)])
                        elif bi > 0:
                            wts.append(prev_ev[(bi - 1, 6 + ic)])
                        b.op("tensor", _gumm, waits=wts, incs=[(TEN, 1)])
                        gu_done = TEN.done
                        if ic == 7:
                            gather_ten[bid_global] = gu_done
                            last_gu_mm = gu_done

                        sb_ = sil[ic % 2]
                        b.op("scalar", lambda e, nidx=nidx, psg=psg, sb_=sb_: e.activation(
                            sb_[:, 0:nidx], psg[:, 0:nidx], AF.Silu),
                            waits=[gu_done], incs=[(SCA, 1)])
                        sil_done = SCA.done

                        mo = mid_o + ic * 128 * RS[s] + lc0 * 128
                        wts_m = [sil_done]
                        if dn_barrier is not None and bi == 0:
                            wts_m.append(dn_barrier)
                        b.op("vector", lambda e, mo=mo, nidx=nidx, psu=psu, sb_=sb_:
                             e.tensor_mul(abf_t[:, mo:mo + nidx],
                                          sb_[:, 0:nidx], psu[:, 0:nidx]),
                             waits=wts_m, incs=[(VEC, 1)])
                        prev_ev[(bi, ic)] = VEC.done
                    bid_global += 1
                mid_done_of = {}
                for bi2, (s2, gch02, nb2) in enumerate(slot_batches[s]):
                    for lc2 in range(gch02 - ch_base[s], gch02 - ch_base[s] + nb2):
                        mid_done_of[lc2] = prev_ev[(bi2, 7)]
                mid_done = prev_ev[(len(slot_batches[s]) - 1, 7)]

                assert RS[s] % 2 == 0
                prev_dn = {}
                for rc in range(RS[s]):
                    gch = ch_base[s] + rc
                    grp = gch // 2
                    ob = orow[grp % 2]
                    oco = (rc % 2) * D
                    for dc in range(4):
                        ps = psA if dc % 2 == 0 else psB

                        def _dmm(e, rc=rc, dc=dc, ps=ps, s=s):
                            last = None
                            for ic in range(8):
                                last = e.matmul(
                                    ps[:, :],
                                    abf_t[:, mid_o + ic * 128 * RS[s] + rc * 128:
                                          mid_o + ic * 128 * RS[s] + (rc + 1) * 128],
                                    wd_sb[:, ic * 2048 + dc * 512: ic * 2048 + (dc + 1) * 512],
                                    start=(ic == 0), stop=(ic == 7))
                            return last
                        wts = [mid_done, wd_ld]
                        if rc > 0 or dc > 1:
                            pk = (rc, dc - 2) if dc >= 2 else (rc - 1, 2 + dc)
                            wts.append(prev_dn[pk])
                        b.op("tensor", _dmm, waits=wts, incs=[(TEN, 1)])
                        dn_done = TEN.done
                        if rc == RS[s] - 1 and dc == 3:
                            last_dn_mm = dn_done

                        wts_e = [dn_done, wfill_done]
                        if dc == 0 and rc % 2 == 0 and grp >= 2:
                            wts_e.append((SSM, SSM.n - 16))
                        b.op("vector", lambda e, gch=gch, dc=dc, ps=ps, ob=ob, oco=oco:
                             e.tensor_scalar_mul(ob[:, oco + dc * 512: oco + (dc + 1) * 512],
                                                 ps[:, :],
                                                 wrow_sb[:, gch:gch + 1]),
                             waits=wts_e, incs=[(VEC, 1)])
                        prev_dn[(rc, dc)] = VEC.done

                    if rc % 2 == 1:
                        wts_s = [prev_dn[(rc - 1, 3)], prev_dn[(rc, 3)], yzero_done]
                        if prev_scatter is not None:
                            wts_s.append(prev_scatter)
                        g0 = gch - 1
                        b.op("gpsimd", lambda e, g0=g0, ob=ob: e.dma_scatter_add(
                            y_main[:, :],
                            ob[:, :].rearrange("p (a m) -> p a m", a=2),
                            sidx_sb[:, g0 * 8:(g0 + 2) * 8], 256, 256, 2048),
                            waits=wts_s, incs=[(SSM, 16)])
                        prev_scatter = SSM.done

            b.op("gpsimd", lambda e: e.collective_compute(
                "ReduceScatter", ALU.add, replica_groups=[list(range(NCORES))],
                ins=[y_main.opt()], outs=[y_red.opt()]),
                waits=[prev_scatter], incs=[(CC, 1)])
            rs_done = CC.done

        abf.reset(0)
        yb_o = abf.take(2 * D)
        sb_o = abf.take(2 * D)
        yt = [abf_t[:, yb_o + i * D: yb_o + (i + 1) * D] for i in range(2)]
        st2 = [abf_t[:, sb_o + i * D: sb_o + (i + 1) * D] for i in range(2)]
        af32.reset(0)
        xf_o = af32.take(2 * D)
        g_o = af32.take(D)
        xt = [af32_t[:, xf_o + i * D: xf_o + (i + 1) * D] for i in range(2)]
        gsl = af32_t[:, g_o:g_o + D]

        prev_vec5 = {}
        prev_sca5 = {}
        for tt in range(8):
            buf = tt % 2

            def _fl(e, tt=tt, buf=buf):
                return e.dma_start(out=yt[buf],
                                   in_=y_red[tt * 128:(tt + 1) * 128, :])
            def _fl2(e, tt=tt, buf=buf):
                e.dma_start(out=st2[buf],
                            in_=s_spill[tt * 128:(tt + 1) * 128, :]).then_inc(LD2.h, 16)
                return e.dma_start(out=xt[buf],
                                   in_=x_sl[WIN + tt * 128: WIN + (tt + 1) * 128, :])
            LD2.n += 16
            wts = [rs_done, s_done]
            if tt >= 2:
                wts += [prev_vec5[tt - 2], prev_sca5[tt - 2]]
            b.op("sync", _fl, waits=wts, incs=[(LD, 16)])
            b.op("scalar", _fl2, waits=wts, incs=[(LD2, 16)])
            fl_done = LD.done
            fl2_done = LD2.done

            b.op("scalar", lambda e, buf=buf: e.activation(gsl, yt[buf], AF.Silu),
                 waits=[fl_done, fl2_done] + ([prev_vec5[tt - 1]] if tt >= 1 else []),
                 incs=[(SCA, 1)])
            prev_sca5[tt] = SCA.done

            b.op("vector", lambda e, buf=buf: e.tensor_mul(gsl, gsl, st2[buf]),
                 waits=[prev_sca5[tt], fl_done, fl2_done], incs=[(VEC, 1)])
            b.op("vector", lambda e, buf=buf: e.tensor_add(xt[buf], xt[buf], gsl),
                 waits=[VEC.done], incs=[(VEC, 1)])
            prev_vec5[tt] = VEC.done

            b.op("sync", lambda e, tt=tt, buf=buf: e.dma_start(
                out=out[tt * 128:(tt + 1) * 128, :], in_=xt[buf]),
                waits=[prev_vec5[tt]], incs=[(OS, 16)])

        if DEBUG_DUMPS:
            def _dbg(e):
                e.dma_start(out=dbg["d_xn"][:, :], in_=xn_loc[:, :]).then_inc(OS.h, 16)
                e.dma_start(out=dbg["d_ag"][:, :], in_=ag_out[:, :]).then_inc(OS.h, 16)
                e.dma_start(out=dbg["d_s"][:, :], in_=s_spill[:, :]).then_inc(OS.h, 16)
                e.dma_start(out=dbg["d_ym"][:, :], in_=y_main[:, :]).then_inc(OS.h, 16)
                return e.dma_start(out=dbg["d_yr"][:, :], in_=y_red[:, :])
            OS.n += 4 * 16
            b.op("sync", _dbg, waits=[rs_done, (OS, OS.n - 4 * 16)], incs=[(OS, 16)])
        b.op("sync", lambda e: e.nop(), waits=[(OS, OS.n)])

        with nc.Block() as block:
            b.emit(block)

    return nc



def kernel(**inputs):
    x = np.ascontiguousarray(np.asarray(inputs["L0_attn_x"])[0], np.float32)
    Wq = np.asarray(inputs["L0_attn_Wq"], np.float32)
    Wkv = np.asarray(inputs["L0_attn_Wkv"], np.float32)
    q_scale = np.asarray(inputs["L0_attn_q_scale"], np.float32)
    k_scale = np.asarray(inputs["L0_attn_k_scale"], np.float32)
    Wg = np.asarray(inputs["L0_attn_Wg"], np.float32)
    bg = np.asarray(inputs["L0_attn_bg"], np.float32)
    Wo = np.asarray(inputs["L0_attn_Wo"], np.float32)
    gate_w = np.asarray(inputs["L0_ffn_gate_weight"], np.float32)
    up_w = np.asarray(inputs["L0_ffn_up_weight"], np.float32)
    down_w = np.asarray(inputs["L0_ffn_down_weight"], np.float32)
    topk_idx = np.asarray(inputs["L0_ffn_topk_idx"])
    topk_weight = np.asarray(inputs["L0_ffn_topk_weight"], np.float32)

    experts, extw = _route(topk_idx, topk_weight)
    meta, cores = _plan(experts, extw)
    assert max(meta["RS"]) <= 14, "unexpectedly large expert group"
    assert meta["EWC"] <= 16

    nc = _build(meta)
    mybir.codegen_inst_isa_subclasses(nc)

    wqT = np.ascontiguousarray(Wq.T)
    wkvT = np.ascontiguousarray(Wkv.T)
    wgaT = np.ascontiguousarray(Wg.T)
    woT = np.ascontiguousarray(Wo.T)
    bgr = np.ascontiguousarray(bg.reshape(1, H))
    qsc = np.ascontiguousarray(q_scale.reshape(128, 1))
    ksc = np.ascontiguousarray(k_scale.reshape(128, 1))
    identb = np.eye(128, dtype=BF)
    onesc = np.ones((128, 1), BF)
    onesr = np.ones((1, 128), BF)

    qi = np.arange(128)[:, None]
    jj = np.arange(640)[None, :]
    base_ok = (jj >= qi) & (jj <= qi + WIN)
    masks0 = np.zeros((5, 128, 640), np.float32)
    for t in range(4):
        ok = base_ok & ((t * 128 + jj - WIN) >= 0)
        masks0[t] = np.where(ok, 0.0, MASKVAL)
    masks0[4] = np.where(base_ok, 0.0, MASKVAL)
    masks_other = np.broadcast_to(masks0[4], (5, 128, 640)).copy()
    masks_other = np.ascontiguousarray(masks_other)

    in_maps = []
    for c in range(NCORES):
        lo = c * TPC - WIN
        if lo < 0:
            x_slc = np.concatenate(
                [np.zeros((WIN, D), np.float32), x[0:TPC]], axis=0)
        else:
            x_slc = x[lo:c * TPC + TPC]
        x_slc = np.ascontiguousarray(x_slc)
        eo = cores[c]["expert_of"]
        wgS = np.ascontiguousarray(np.stack([gate_w[e].T for e in eo]))
        wuS = np.ascontiguousarray(np.stack([up_w[e].T for e in eo]))
        wdS = np.ascontiguousarray(np.stack([down_w[e].T for e in eo]))
        in_maps.append({
            "x_sl": x_slc, "wqT": wqT, "wkvT": wkvT, "wgaT": wgaT,
            "woT": woT, "bgr": bgr, "qsc": qsc, "ksc": ksc,
            "maskt": masks0 if c == 0 else masks_other,
            "identb": identb, "onesc": onesc, "onesr": onesr,
            "wgS": wgS, "wuS": wuS, "wdS": wdS,
            "gidx": cores[c]["gidx"], "sidx": cores[c]["sidx"],
            "wrow": cores[c]["wrow"], "extw": cores[c]["extw"],
        })

    res = run_bass_kernel_spmd(nc, in_maps, core_ids=list(range(NCORES)))
    shards = [res.results[c]["out"] for c in range(NCORES)]
    full = np.concatenate(shards, axis=0).reshape(1, NTOK, D)
    return full


# version 12 (final)
# speedup vs baseline: 1.2650x; 1.1063x over previous
import numpy as np
import ml_dtypes
import concourse.bass as bass
import concourse.mybir as mybir
from concourse.bass_utils import run_bass_kernel_spmd
from concourse.library_config import mlp as _mlp_lib

F32 = mybir.dt.float32
BF16 = mybir.dt.bfloat16
I16 = mybir.dt.int16
AF = mybir.ActivationFunctionType
ALU = mybir.AluOpType
AX = mybir.AxisListType
BF = ml_dtypes.bfloat16

NCORES = 8
NTOK = 8192
D = 2048
H = 8
DH = 128
WIN = 512
NE = 32
TPC = NTOK // NCORES
SL = TPC + WIN
NT = SL // 128
NSLOT = 4
MASKVAL = -30000.0
YROWS = NTOK + 128

ENGINES = ("sync", "tensor", "vector", "scalar", "gpsimd")
DEBUG_DUMPS = False
SKIP_ATTN = False
SKIP_MOE = False



def _pack_idx(vals, ncols):
    arr = np.zeros((128, ncols), np.int16)
    j = np.arange(len(vals))
    v = np.asarray(vals, np.int16)
    arr[j % 16, j // 16] = v
    arr[16 + j % 16, j // 16] = v
    return arr


def _route(topk_idx, topk_weight):
    ti = np.asarray(topk_idx).reshape(-1).astype(np.int64)
    tw = np.asarray(topk_weight).reshape(-1).astype(np.float32)
    nreg = NTOK * 4
    reg_e, reg_w = ti[:nreg], tw[:nreg]
    reg_t = np.repeat(np.arange(NTOK), 4)
    ext_e, ext_w = ti[nreg:], tw[nreg:]

    experts = []
    for e in range(NE):
        m = reg_e == e
        toks = reg_t[m]
        ws = reg_w[m]
        rows = [(NTOK - 1, 0.0, True)]
        rows += list(zip(toks.tolist(), ws.tolist(), [False] * len(toks)))
        groups, group_toks = [], []
        for r in rows:
            t = r[0]
            for ci in range(len(groups)):
                if len(groups[ci]) < 256 and t not in group_toks[ci]:
                    groups[ci].append(r)
                    group_toks[ci].add(t)
                    break
            else:
                groups.append([r])
                group_toks.append({t})
        chunks = []
        for g in groups:
            chunks.append(g[0:128])
            chunks.append(g[128:256])
        experts.append(chunks)
    extw = [ext_w[ext_e == e] for e in range(NE)]
    return experts, extw


def _plan(experts, extw):
    nch = np.array([len(c) for c in experts])
    order = np.argsort(-nch, kind="stable")
    expert_of = [[int(order[s * NCORES + c]) for s in range(NSLOT)]
                 for c in range(NCORES)]
    RS = [int(max(nch[order[s * NCORES + c]] for c in range(NCORES)))
          for s in range(NSLOT)]
    EWC = max(1, -(-int(max(len(w) for w in extw)) // 128))
    NCH = sum(RS)

    batches = []
    base = 0
    for s in range(NSLOT):
        c0 = 0
        while c0 < RS[s]:
            nb = min(4, RS[s] - c0)
            batches.append((s, base + c0, nb))
            c0 += nb
        base += RS[s]

    cores = []
    spec_pos = None
    for c in range(NCORES):
        gi = np.zeros(NCH * 128, np.int64)
        si = np.zeros(NCH * 128, np.int64)
        wr = np.zeros((128, NCH), np.float32)
        sp = []
        gch = 0
        for s in range(NSLOT):
            e = expert_of[c][s]
            chunks = experts[e]
            for lc in range(RS[s]):
                rows = chunks[lc] if lc < len(chunks) else []
                for j in range(128):
                    p = gch * 128 + j
                    if j < len(rows):
                        t, w, spc = rows[j]
                        gi[p] = t
                        si[p] = t
                        wr[j, gch] = w
                        if spc:
                            sp.append((gch, j))
                    else:
                        gi[p] = 0
                        si[p] = NTOK + j
                gch += 1
        for g2 in range(0, NCH, 2):
            seg = si[g2 * 128:(g2 + 2) * 128]
            used = set(seg[seg < NTOK].tolist())
            cand = 0
            for kk in range(256):
                if seg[kk] >= NTOK:
                    while cand in used:
                        cand += 1
                    seg[kk] = cand
                    used.add(cand)
        assert len(sp) == NSLOT
        ew = np.zeros((128, NSLOT * EWC), np.float32)
        for s in range(NSLOT):
            e = expert_of[c][s]
            assert sp[s][1] == 0
            vals = extw[e]
            tmp = np.zeros(128 * EWC, np.float32)
            tmp[:len(vals)] = vals
            ew[:, s * EWC:(s + 1) * EWC] = tmp.reshape(EWC, 128).T
        if spec_pos is None:
            spec_pos = sp
        else:
            assert sp == spec_pos, "special positions must be core-uniform"
        cores.append(dict(expert_of=expert_of[c],
                          gidx=_pack_idx(gi, NCH * 8),
                          sidx=_pack_idx(si, NCH * 8),
                          wrow=wr, extw=ew))
    return dict(RS=RS, NCH=NCH, EWC=EWC, batches=batches, spec_pos=spec_pos), cores



class _Sem:
    __slots__ = ("h", "n", "name")

    def __init__(self, h, name):
        self.h = h
        self.n = 0
        self.name = name

    @property
    def done(self):
        return (self, self.n)


class _Builder:
    def __init__(self):
        self.ops = {e: [] for e in ENGINES}

    def op(self, eng, fn, waits=(), incs=()):
        for s, k in incs:
            s.n += k
        self.ops[eng].append((tuple(w for w in waits if w is not None),
                              fn, tuple(incs)))

    def emit(self, block):
        for eng in ENGINES:
            oplist = self.ops[eng]

            def body(e, _ops=oplist):
                for waits, fn, incs in _ops:
                    for s, v in waits:
                        if v > 0:
                            e.wait_ge(s.h, v)
                    ins = fn(e)
                    for s, k in incs:
                        ins.then_inc(s.h, k)
            getattr(block, eng)(body)


class _Span:
    def __init__(self, t, cols):
        self.t = t
        self.cols = cols
        self.pos = 0

    def take(self, n):
        assert self.pos + n <= self.cols, f"arena overflow {self.pos}+{n}"
        o = self.pos
        self.pos += n
        return o

    def mark(self):
        return self.pos

    def reset(self, pos):
        self.pos = pos



def _build(meta):
    RS = meta["RS"]
    NCH = meta["NCH"]
    EWC = meta["EWC"]
    batches = meta["batches"]
    spec_pos = meta["spec_pos"]
    RSMAX = max(RS)

    nc = bass.Bass("TRN2", target_bir_lowering=False, num_devices=NCORES)

    def din(name, shape, dt):
        return nc.dram_tensor(name, list(shape), dt, kind="ExternalInput").ap()

    x_sl = din("x_sl", (SL, D), F32)
    wqT = din("wqT", (D, H * DH), F32)
    wkvT = din("wkvT", (D, 2 * H * DH), F32)
    wgaT = din("wgaT", (D, H), F32)
    woT = din("woT", (H * DH, D), F32)
    bgr = din("bgr", (1, H), F32)
    qsc = din("qsc", (128, 1), F32)
    ksc = din("ksc", (128, 1), F32)
    maskt = din("maskt", (5, 128, 640), F32)
    identb = din("identb", (128, 128), BF16)
    onesc = din("onesc", (128, 1), BF16)
    onesr = din("onesr", (1, 128), BF16)
    wgS = din("wgS", (NSLOT, D, 1024), F32)
    wuS = din("wuS", (NSLOT, D, 1024), F32)
    wdS = din("wdS", (NSLOT, 1024, D), F32)
    gidx = din("gidx", (128, NCH * 8), I16)
    sidx = din("sidx", (128, NCH * 8), I16)
    wrow = din("wrow", (128, NCH), F32)
    extw = din("extw", (128, NSLOT * EWC), F32)

    out = nc.dram_tensor("out", [TPC, D], F32, kind="ExternalOutput").ap()
    dbg = {}
    if DEBUG_DUMPS:
        for nm, shp in (("d_xn", [SL, D]), ("d_ag", [NTOK, D]),
                        ("d_s", [TPC, D]), ("d_ym", [NTOK, D]),
                        ("d_yr", [TPC, D])):
            dbg[nm] = nc.dram_tensor(nm, shp, BF16, kind="ExternalOutput").ap()

    xn_loc = nc.dram_tensor("xn_loc", [SL, D], BF16).ap()
    ag_in = nc.dram_tensor("ag_in", [TPC, D], BF16).ap()
    ag_out = nc.dram_tensor("ag_out", [NTOK, D], BF16).ap()
    y_main = nc.dram_tensor("y_main", [NTOK, D], BF16).ap()
    y_red = nc.dram_tensor("y_red", [TPC, D], BF16).ap()
    s_spill = nc.dram_tensor("s_spill", [TPC, D], BF16).ap()

    from contextlib import ExitStack
    stack = ExitStack()
    with stack:
        def sem(name):
            return _Sem(stack.enter_context(nc.semaphore(name)), name)

        def sbuf(name, shape, dt):
            return stack.enter_context(nc.sbuf_tensor(name, list(shape), dt))

        def psum(name, shape, dt):
            return stack.enter_context(nc.psum_tensor(name, list(shape), dt))

        LD = sem("LD")
        LD2 = sem("LD2")
        WSM = sem("WSM")
        GSM = sem("GSM")
        SSM = sem("SSM")
        CC = sem("CC")
        TEN = sem("TEN")
        VEC = sem("VEC")
        SCA = sem("SCA")
        XLW = sem("XLW")
        AGW = sem("AGW")
        TRS = sem("TRS")
        SSP = sem("SSP")
        OS = sem("OS")
        ZW = sem("ZW")

        ident_sb = sbuf("ident_sb", (128, 128), BF16)
        ones128_sb = sbuf("ones128_sb", (128, 128), BF16)
        onesc_sb = sbuf("onesc_sb", (128, 1), BF16)
        onesr_sb = sbuf("onesr_sb", (1, 128), BF16)
        bgr_sb = sbuf("bgr_sb", (1, H), BF16)
        wga_sb = sbuf("wga_sb", (128, 16 * H), BF16)
        ccol_sb = sbuf("ccol_sb", (128, 1), F32)
        sc1_sb = sbuf("sc1_sb", (128, 4), F32)
        gidx_sb = sbuf("gidx_sb", (128, NCH * 8), I16)
        sidx_sb = sbuf("sidx_sb", (128, NCH * 8), I16)
        wrow_sb = sbuf("wrow_sb", (128, NCH), F32)
        cext_sb = sbuf("cext_sb", (128, NSLOT), F32)
        extw_sb = sbuf("extw_sb", (128, NSLOT * EWC), F32)
        extb_sb = sbuf("extb_sb", (128, NSLOT * EWC), BF16)
        qsc_sb = sbuf("qsc_sb", (128, 1), F32)
        ksc_sb = sbuf("ksc_sb", (128, 1), F32)

        ABF_COLS = 88100
        AF32_COLS = 8000
        abf_t = sbuf("abf", (128, ABF_COLS), BF16)
        af32_t = sbuf("af32", (128, AF32_COLS), F32)
        abf = _Span(abf_t, ABF_COLS)
        af32 = _Span(af32_t, AF32_COLS)

        psA = psum("psA", (128, 512), F32)
        psB = psum("psB", (128, 512), F32)
        psC = psum("psC", (128, 512), F32)
        psD = psum("psD", (128, 512), F32)
        psE = psum("psE", (128, 640), F32)
        psF1 = psum("psF1", (128, 128), BF16)
        psF2 = psum("psF2", (128, 128), BF16)

        b = _Builder()

        for dst, src in ((ident_sb, identb), (onesc_sb, onesc),
                         (onesr_sb, onesr), (gidx_sb, gidx), (sidx_sb, sidx),
                         (wrow_sb, wrow), (extw_sb, extw), (qsc_sb, qsc),
                         (ksc_sb, ksc)):
            b.op("sync", lambda e, d=dst, s=src: e.dma_start(out=d[:, :], in_=s[:, :]),
                 incs=[(LD, 16)])
        ph0_ld = LD.done

        b.op("gpsimd", lambda e: e.load_library(_mlp_lib))
        b.op("gpsimd", lambda e: e.dma_start(out=bgr_sb[:, :], in_=bgr[:, :]),
             incs=[(WSM, 16)])
        b.op("gpsimd", lambda e: e.dma_start(
            out=wga_sb[:, :].rearrange("p (c m) -> p c m", m=H),
            in_=wgaT.rearrange("(c p) m -> p c m", p=128)),
            incs=[(WSM, 16)])
        wga_done = WSM.done

        wreg_o = abf.take(8192)
        reg2_o = abf.take(6144)
        xnT_o = abf.take(16 * SL)
        q_o = abf.take(H * TPC)
        k_o = abf.take(H * SL)
        v_o = abf.take(12 * 1024)
        o_o = abf.take(H * TPC)

        wreg = abf_t[:, wreg_o:wreg_o + 8192]
        xnT = [abf_t[:, xnT_o + hc * SL: xnT_o + (hc + 1) * SL] for hc in range(16)]
        q_sb = abf_t[:, q_o:q_o + H * TPC]
        k_sb = abf_t[:, k_o:k_o + H * SL]
        v_sb = abf_t[:, v_o:v_o + 12 * 1024]
        o_sb = abf_t[:, o_o:o_o + H * TPC]

        xn2 = [abf_t[:, reg2_o + i * D: reg2_o + (i + 1) * D] for i in range(2)]
        squt = abf_t[:, reg2_o + 2 * D: reg2_o + 3 * D]
        qsq = [abf_t[:, reg2_o + i * 512: reg2_o + (i + 1) * 512] for i in range(2)]
        ksq = [abf_t[:, reg2_o + 1024 + i * 512: reg2_o + 1024 + (i + 1) * 512]
               for i in range(2)]
        bcT = abf_t[:, reg2_o + 2048: reg2_o + 2176]
        attbf = [abf_t[:, reg2_o + 2176 + i * 640: reg2_o + 2176 + (i + 1) * 640]
                 for i in range(2)]
        attT = [abf_t[:, reg2_o + 3456 + i * 640: reg2_o + 3456 + (i + 1) * 640]
                for i in range(2)]
        s_half = [abf_t[:, reg2_o + i * 1024: reg2_o + (i + 1) * 1024]
                  for i in range(2)]

        x2_o = af32.take(2 * D)
        lst_o = af32.take(16)
        rq_o = af32.take(64)
        gt_o = af32.take(64)
        rk_o = af32.take(96)
        ms_o = af32.take(1280)
        atf_o = af32.take(1280)
        st_o = af32.take(32)
        sil_o = af32.take(1024)
        x2 = [af32_t[:, x2_o + i * D: x2_o + (i + 1) * D] for i in range(2)]
        lst = af32_t[:, lst_o:lst_o + 16]
        rq_tok = af32_t[:, rq_o:rq_o + 64]
        gate_tok = af32_t[:, gt_o:gt_o + 64]
        rk_tok = af32_t[:, rk_o:rk_o + 96]
        mask_sb = af32_t[:, x2_o:x2_o + 5 * 640]
        ms = [af32_t[:, ms_o + i * 640: ms_o + (i + 1) * 640] for i in range(2)]
        attf = [af32_t[:, atf_o + i * 640: atf_o + (i + 1) * 640] for i in range(2)]
        stt = [af32_t[:, st_o + i * 16: st_o + (i + 1) * 16] for i in range(2)]
        sil = [af32_t[:, sil_o:sil_o + 512], af32_t[:, sil_o + 512:sil_o + 1024]]

        b.op("vector", lambda e: e.memset(ones128_sb[:, :], 1.0),
             incs=[(VEC, 1)])
        zz_done = VEC.done

        b.op("vector", lambda e: e.tensor_mul(sc1_sb[:, 0:1], qsc_sb[:, :],
                                              ksc_sb[:, :]),
             waits=[ph0_ld], incs=[(VEC, 1)])
        b.op("vector", lambda e: e.tensor_scalar_mul(ccol_sb[:, :],
                                                     sc1_sb[:, 0:1], 8.0),
             waits=[VEC.done], incs=[(VEC, 1)])
        ccol_done = VEC.done

        b.op("vector", lambda e: e.tensor_copy(extb_sb[:, :], extw_sb[:, :]),
             waits=[ph0_ld], incs=[(VEC, 1)])
        extb_done = VEC.done

        def _extmm(e):
            last = None
            for s in range(NSLOT):
                last = e.matmul(psC[0:1, s * EWC:(s + 1) * EWC],
                                onesc_sb[:, :],
                                extb_sb[:, s * EWC:(s + 1) * EWC])
            return last
        b.op("tensor", _extmm, waits=[extb_done], incs=[(TEN, 1)])
        extmm_done = TEN.done

        def _extred(e):
            last = None
            for s in range(NSLOT):
                last = e.reduce_sum(cext_sb[0:1, s:s + 1],
                                    psC[0:1, s * EWC:(s + 1) * EWC], axis=AX.X)
            return last
        b.op("vector", _extred, waits=[extmm_done], incs=[(VEC, 1)])

        def _extcp(e):
            last = None
            for s in range(NSLOT):
                gch, p = spec_pos[s]
                last = e.tensor_copy(wrow_sb[0:1, gch:gch + 1],
                                     cext_sb[0:1, s:s + 1])
            return last
        b.op("vector", _extcp, waits=[VEC.done], incs=[(VEC, 1)])
        wfill_done = VEC.done

        zv = abf_t[:, o_o:o_o + D]
        b.op("vector", lambda e: e.memset(zv, 0.0), incs=[(VEC, 1)])
        zv_done = VEC.done
        for zi in range(NTOK // 128):
            b.op("gpsimd", lambda e, zi=zi: e.dma_start(
                out=y_main[zi * 128:(zi + 1) * 128, :], in_=zv),
                waits=([zv_done] if zi == 0 else []), incs=[(ZW, 16)])
        yzero_done = ZW.done

        xld, xvec, xsca, xwr = [None] * NT, [None] * NT, [None] * NT, [None] * NT
        trs_ten = [None] * NT
        evac_bank = {0: None, 1: None}

        def _emit_trs(i, xvec):
            for hc in range(16):
                pf = psF1 if hc % 2 == 0 else psF2
                wts_t = [xvec[i], ph0_ld]
                if evac_bank[hc % 2] is not None:
                    wts_t.append(evac_bank[hc % 2])
                b.op("tensor", lambda e, i=i, hc=hc, pf=pf: e.transpose(
                    pf[:, :], xn2[i % 2][:, hc * 128:(hc + 1) * 128],
                    ident_sb[:, :]),
                    waits=wts_t, incs=[(TEN, 1)])
                t_done = TEN.done
                b.op("scalar", lambda e, i=i, hc=hc, pf=pf: e.activation(
                    xnT[hc][:, i * 128:(i + 1) * 128], pf[:, :], AF.Copy),
                    waits=[t_done], incs=[(SCA, 1)])
                evac_bank[hc % 2] = SCA.done
            trs_ten[i] = (TEN, TEN.n)

        for i in range(NT):
            waits = [xvec[i - 2], xsca[i - 2]] if i >= 2 else []
            b.op("sync", lambda e, i=i: e.dma_start(
                out=x2[i % 2], in_=x_sl[i * 128:(i + 1) * 128, :]),
                waits=waits, incs=[(LD, 16)])
            xld[i] = LD.done

            sqc = 1 + 8 * (i % 2)
            b.op("scalar", lambda e, i=i, sqc=sqc: e.activation(
                squt, x2[i % 2], AF.Square, accum_out=lst[:, sqc:sqc + 1]),
                waits=[xld[i]], incs=[(SCA, 1)])
            ssq_done = SCA.done
            if i >= 1:
                _emit_trs(i - 1, xvec)

            wv = [xld[i], ssq_done]
            if i >= 2 and xwr[i - 2] is not None:
                wv.append(xwr[i - 2])
            b.op("vector", lambda e, i=i: e.reduce_sum(lst[:, 0:1], x2[i % 2],
                                                       axis=AX.X),
                 waits=wv, incs=[(VEC, 1)])
            b.op("vector", lambda e: e.tensor_scalar_mul(lst[:, 2:3],
                                                         lst[:, 0:1], 1.0 / D),
                 waits=[VEC.done], incs=[(VEC, 1)])
            b.op("vector", lambda e, sqc=sqc: e.tensor_scalar_mul(
                lst[:, 3:4], lst[:, sqc:sqc + 1], 1.0 / D),
                waits=[VEC.done], incs=[(VEC, 1)])
            b.op("vector", lambda e: e.tensor_mul(lst[:, 4:5], lst[:, 2:3],
                                                  lst[:, 2:3]),
                 waits=[VEC.done], incs=[(VEC, 1)])
            b.op("vector", lambda e: e.tensor_sub(lst[:, 5:6], lst[:, 3:4],
                                                  lst[:, 4:5]),
                 waits=[VEC.done], incs=[(VEC, 1)])
            var_done = VEC.done

            b.op("vector", lambda e: e.tensor_scalar_add(
                lst[:, 8:9], lst[:, 5:6], 1e-5),
                waits=[var_done], incs=[(VEC, 1)])
            epsd = VEC.done
            b.op("scalar", lambda e: e.activation(
                lst[:, 6:7], lst[:, 8:9], AF.Sqrt),
                waits=[epsd], incs=[(SCA, 1)])
            xsca[i] = SCA.done

            b.op("vector", lambda e: e.reciprocal(lst[:, 7:8], lst[:, 6:7]),
                 waits=[xsca[i]], incs=[(VEC, 1)])
            rstd_done = VEC.done
            wts_ap = [rstd_done]
            if i >= 2:
                wts_ap.append(trs_ten[i - 2])
            b.op("vector", lambda e, i=i: e.tensor_scalar(
                xn2[i % 2], x2[i % 2], lst[:, 2:3], lst[:, 7:8],
                op0=ALU.subtract, op1=ALU.mult),
                waits=wts_ap, incs=[(VEC, 1)])
            xvec[i] = VEC.done

            if i >= 4:
                def _lnw(e, i=i):
                    return e.dma_start(
                        out=ag_in[(i - 4) * 128:(i - 3) * 128, :],
                        in_=xn2[i % 2])
                b.op("sync", _lnw, waits=[xvec[i]], incs=[(AGW, 16)])
                xwr[i] = AGW.done
        _emit_trs(NT - 1, xvec)
        ln_done = (VEC, VEC.n)
        ln_sca_done = (SCA, SCA.n)
        xnT_done = (SCA, SCA.n)

        b.op("gpsimd", lambda e: e.collective_compute(
            "AllGather", ALU.bypass, replica_groups=[list(range(NCORES))],
            ins=[ag_in.opt()], outs=[ag_out.opt()]),
            waits=[(AGW, AGW.n)], incs=[(CC, 1)])
        ag_done = CC.done

        if SKIP_ATTN:
            s_done = (SSP, 0)
            attn_skip = True
        else:
            attn_skip = False
        if not attn_skip:
            prev_half_last = None
            ev_q = {}
            for half in range(2):
                wsrc = wqT.rearrange("(c p) m -> p c m", p=128)[:, :, half * 512:(half + 1) * 512]
                wts = [] if half == 0 else [prev_half_last]
                b.op("gpsimd", lambda e, s=wsrc: e.dma_start(
                    out=wreg[:, :].rearrange("p (c m) -> p c m", m=512), in_=s),
                    waits=wts, incs=[(WSM, 16)])
                wq_ld = WSM.done
                for hh in range(4):
                    h = half * 4 + hh
                    for bb in range(2):
                        u = h * 2 + bb
                        ps = psA if u % 2 == 0 else psB

                        def _qmm(e, hh=hh, bb=bb, ps=ps):
                            last = None
                            for hc in range(16):
                                last = e.matmul(
                                    ps[:, :],
                                    wreg[:, hc * 512 + hh * 128: hc * 512 + (hh + 1) * 128],
                                    xnT[hc][:, WIN + bb * 512: WIN + (bb + 1) * 512],
                                    start=(hc == 0), stop=(hc == 15))
                            return last
                        wts = [wq_ld, ccol_done, xnT_done]
                        if u >= 2:
                            wts.append(ev_q[u - 2])
                        b.op("tensor", _qmm, waits=wts, incs=[(TEN, 1)])
                        mm_done = TEN.done

                        b.op("vector", lambda e, h=h, bb=bb, ps=ps: e.tensor_scalar_mul(
                            q_sb[:, h * TPC + bb * 512: h * TPC + (bb + 1) * 512],
                            ps[:, :], ccol_sb[:, :]),
                            waits=[mm_done], incs=[(VEC, 1)])
                        evd = VEC.done

                        b.op("scalar", lambda e, bb=bb, ps=ps: e.activation(
                            qsq[bb], ps[:, :], AF.Square),
                            waits=[evd], incs=[(SCA, 1)])
                        sqd = SCA.done

                        def _ssmm(e, bb=bb):
                            last = None
                            for j in range(4):
                                last = e.matmul(psC[:, j:j + 1],
                                                qsq[bb][:, j * 128:(j + 1) * 128],
                                                onesc_sb[:, :])
                            return last
                        b.op("tensor", _ssmm, waits=[sqd], incs=[(TEN, 1)])
                        ssd = TEN.done

                        def _ssev(e, h=h, bb=bb):
                            last = None
                            for j in range(4):
                                t = bb * 4 + j
                                last = e.tensor_copy(
                                    rq_tok[:, t * 8 + h: t * 8 + h + 1],
                                    psC[:, j:j + 1])
                            return last
                        b.op("vector", _ssev, waits=[ssd], incs=[(VEC, 1)])
                        ev_q[u] = VEC.done
                prev_half_last = ev_q[half * 8 + 7]

            b.op("scalar", lambda e: e.activation(rq_tok[:, :], rq_tok[:, :], AF.Sqrt),
                 waits=[ev_q[15]], incs=[(SCA, 1)])

            b.op("vector", lambda e: e.tensor_scalar_max(
                rq_tok[:, :], rq_tok[:, :], 1e-12),
                waits=[SCA.done], incs=[(VEC, 1)])
            b.op("vector", lambda e: e.reciprocal(rq_tok[:, :], rq_tok[:, :]),
                 waits=[VEC.done], incs=[(VEC, 1)])
            rq_done = VEC.done

            prev_gate = None
            for t in range(8):
                def _gmm(e, t=t):
                    for hc in range(16):
                        e.matmul(psD[:, 0:8],
                                 xnT[hc][:, WIN + t * 128: WIN + (t + 1) * 128],
                                 wga_sb[:, hc * H:(hc + 1) * H],
                                 start=(hc == 0), stop=False)
                    return e.matmul(psD[:, 0:8], onesr_sb[:, :], bgr_sb[:, :],
                                    start=False, stop=True)
                wts = [xnT_done, wga_done, ph0_ld]
                if prev_gate is not None:
                    wts.append(prev_gate)
                b.op("tensor", _gmm, waits=wts, incs=[(TEN, 1)])
                b.op("scalar", lambda e, t=t: e.activation(
                    gate_tok[:, t * 8:(t + 1) * 8], psD[:, 0:8], AF.Sigmoid),
                    waits=[TEN.done], incs=[(SCA, 1)])
                prev_gate = SCA.done
            gates_done = prev_gate

            ev_k = {}
            for half in range(2):
                wsrc = wkvT.rearrange("(c p) m -> p c m", p=128)[:, :, half * 512:(half + 1) * 512]
                b.op("gpsimd", lambda e, s=wsrc: e.dma_start(
                    out=wreg[:, :].rearrange("p (c m) -> p c m", m=512), in_=s),
                    waits=[ev_q[15] if half == 0 else ev_k[11]],
                    incs=[(WSM, 16)])
                wk_ld = WSM.done
                for hh in range(4):
                    h = half * 4 + hh
                    for b3 in range(3):
                        u = h * 3 + b3
                        ps = psA if u % 2 == 0 else psB

                        def _kmm(e, hh=hh, b3=b3, ps=ps):
                            last = None
                            for hc in range(16):
                                last = e.matmul(
                                    ps[:, :],
                                    wreg[:, hc * 512 + hh * 128: hc * 512 + (hh + 1) * 128],
                                    xnT[hc][:, b3 * 512:(b3 + 1) * 512],
                                    start=(hc == 0), stop=(hc == 15))
                            return last
                        wts = [wk_ld]
                        if u >= 2:
                            wts.append(ev_k[u - 2])
                        b.op("tensor", _kmm, waits=wts, incs=[(TEN, 1)])
                        mm_done = TEN.done

                        b.op("vector", lambda e, h=h, b3=b3, ps=ps: e.tensor_copy(
                            k_sb[:, h * SL + b3 * 512: h * SL + (b3 + 1) * 512],
                            ps[:, :]),
                            waits=[mm_done], incs=[(VEC, 1)])
                        evd = VEC.done

                        b.op("scalar", lambda e, b3=b3, ps=ps: e.activation(
                            ksq[b3 % 2], ps[:, :], AF.Square),
                            waits=[evd], incs=[(SCA, 1)])
                        sqd = SCA.done

                        def _ssmm(e, b3=b3):
                            last = None
                            for j in range(4):
                                last = e.matmul(psC[:, j:j + 1],
                                                ksq[b3 % 2][:, j * 128:(j + 1) * 128],
                                                onesc_sb[:, :])
                            return last
                        b.op("tensor", _ssmm, waits=[sqd], incs=[(TEN, 1)])
                        ssd = TEN.done

                        def _ssev(e, h=h, b3=b3):
                            last = None
                            for j in range(4):
                                ch = b3 * 4 + j
                                last = e.tensor_copy(
                                    rk_tok[:, ch * 8 + h: ch * 8 + h + 1],
                                    psC[:, j:j + 1])
                            return last
                        b.op("vector", _ssev, waits=[ssd], incs=[(VEC, 1)])
                        ev_k[u] = VEC.done
            k_done = ev_k[23]

            ev_v = {}
            for half in range(2):
                wsrc = wkvT.rearrange("(c p) m -> p c m", p=128)[:, :, 1024 + half * 512: 1024 + (half + 1) * 512]
                b.op("gpsimd", lambda e, s=wsrc: e.dma_start(
                    out=wreg[:, :].rearrange("p (c m) -> p c m", m=512), in_=s),
                    waits=[k_done if half == 0 else ev_v[11]],
                    incs=[(WSM, 16)])
                wv_ld = WSM.done
                for ch in range(12):
                    u = half * 12 + ch
                    ps = psA if u % 2 == 0 else psB

                    def _vmm(e, ch=ch, ps=ps):
                        last = None
                        for hc in range(16):
                            last = e.matmul(
                                ps[:, :],
                                xnT[hc][:, ch * 128:(ch + 1) * 128],
                                wreg[:, hc * 512:(hc + 1) * 512],
                                start=(hc == 0), stop=(hc == 15))
                        return last
                    wts = [wv_ld]
                    if u >= 2:
                        wts.append(ev_v[u - 2])
                    b.op("tensor", _vmm, waits=wts, incs=[(TEN, 1)])
                    b.op("vector", lambda e, ch=ch, half=half, ps=ps: e.tensor_copy(
                        v_sb[:, ch * 1024 + half * 512: ch * 1024 + (half + 1) * 512],
                        ps[:, :]),
                        waits=[TEN.done], incs=[(VEC, 1)])
                    ev_v[u] = VEC.done
            v_done = ev_v[23]

            b.op("scalar", lambda e: e.activation(rk_tok[:, :], rk_tok[:, :], AF.Sqrt),
                 waits=[k_done], incs=[(SCA, 1)])

            b.op("vector", lambda e: e.tensor_scalar_max(
                rk_tok[:, :], rk_tok[:, :], 1e-12),
                waits=[SCA.done], incs=[(VEC, 1)])
            b.op("vector", lambda e: e.reciprocal(rk_tok[:, :], rk_tok[:, :]),
                 waits=[VEC.done], incs=[(VEC, 1)])
            rk_done = VEC.done

            prev_kh = {}
            for h in range(H):
                for ch in range(12):
                    u = h * 12 + ch
                    psf = psF1 if u % 2 == 0 else psF2

                    wts = [rk_done, zz_done]
                    if u >= 1:
                        wts.append(prev_kh[u - 1][0])
                    b.op("vector", lambda e, h=h, ch=ch: e.tensor_scalar_mul(
                        bcT, ones128_sb[:, :],
                        rk_tok[:, (ch * 8 + h):(ch * 8 + h) + 1]),
                        waits=wts, incs=[(VEC, 1)])
                    bct_done = VEC.done

                    wts = [bct_done, ph0_ld]
                    if u >= 2:
                        wts.append(prev_kh[u - 2][1])
                    else:
                        wts.append(xnT_done)
                    if u == 0:
                        wts.append(gates_done)
                    b.op("tensor", lambda e, psf=psf: e.transpose(
                        psf[:, :], bcT, ident_sb[:, :]),
                        waits=wts, incs=[(TEN, 1)])
                    tr_done = TEN.done

                    b.op("vector", lambda e, h=h, ch=ch, psf=psf: e.tensor_mul(
                        k_sb[:, h * SL + ch * 128: h * SL + (ch + 1) * 128],
                        k_sb[:, h * SL + ch * 128: h * SL + (ch + 1) * 128],
                        psf[:, :]),
                        waits=[tr_done], incs=[(VEC, 1)])
                    prev_kh[u] = (tr_done, VEC.done)
            kh_done = prev_kh[H * 12 - 1][1]

            b.op("sync", lambda e: e.dma_start(
                out=mask_sb[:, :].rearrange("p (c m) -> p c m", m=640),
                in_=maskt.rearrange("c p m -> p c m")),
                waits=[ln_done, ln_sca_done], incs=[(LD, 16)])
            mask_done = LD.done

            s_phase_guard = v_done
            NU = H * 8
            sim_of = {}
            madd_of = {}
            bias_of = {}
            exp_of = {}
            attbf_of = {}
            tv_of = {}
            cv_of = {}
            av_of = {}
            oev_of = {}

            sim_lo = [psE[:, 0:512], psA[:, :], psC[:, :]]
            sim_hi = [psE[:, 512:640], psB[:, 0:128], psD[:, 128:256]]

            def _e_sim(u):
                h, t = u // 8, u % 8
                cc = u % 3

                def _simm(e, h=h, t=t, cc=cc):
                    e.matmul(sim_lo[cc],
                             q_sb[:, h * TPC + t * 128: h * TPC + (t + 1) * 128],
                             k_sb[:, h * SL + t * 128: h * SL + t * 128 + 512],
                             start=True, stop=True)
                    return e.matmul(
                        sim_hi[cc],
                        q_sb[:, h * TPC + t * 128: h * TPC + (t + 1) * 128],
                        k_sb[:, h * SL + t * 128 + 512: h * SL + t * 128 + 640],
                        start=True, stop=True)
                wts = [kh_done, mask_done]
                if u < 3:
                    wts.append(s_phase_guard)
                if u - 3 in madd_of:
                    wts.append(madd_of[u - 3])
                b.op("tensor", _simm, waits=wts, incs=[(TEN, 1)])
                sim_of[u] = TEN.done

            def _e_front(u):
                h, t = u // 8, u % 8
                pp = u % 2
                cc = u % 3
                msP, sttP = ms[pp], stt[pp]
                mslot = t if t < 4 else 4
                wts = [sim_of[u], rq_done]
                if u - 2 in exp_of:
                    wts.append(exp_of[u - 2])
                b.op("vector", lambda e, mslot=mslot, msP=msP, cc=cc:
                     e.tensor_add(
                         msP[:, 0:512], sim_lo[cc],
                         mask_sb[:, mslot * 640: mslot * 640 + 512]),
                     waits=wts, incs=[(VEC, 1)])
                b.op("vector", lambda e, mslot=mslot, msP=msP, cc=cc:
                     e.tensor_add(
                         msP[:, 512:640], sim_hi[cc],
                         mask_sb[:, mslot * 640 + 512:(mslot + 1) * 640]),
                     incs=[(VEC, 1)])
                madd_of[u] = VEC.done
                b.op("vector", lambda e, msP=msP, sttP=sttP: e.reduce_max(
                    sttP[:, 0:1], msP, axis=AX.X, negate=True),
                    waits=[VEC.done], incs=[(VEC, 1)])
                b.op("vector", lambda e, h=h, t=t, sttP=sttP: e.tensor_mul(
                    sttP[:, 1:2], sttP[:, 0:1],
                    rq_tok[:, t * 8 + h: t * 8 + h + 1]),
                    waits=[VEC.done], incs=[(VEC, 1)])
                bias_of[u] = VEC.done

            def _e_exp(u):
                h, t = u // 8, u % 8
                pp = u % 2
                b.op("scalar", lambda e, h=h, t=t, msP=ms[pp], attfP=attf[pp],
                     sttP=stt[pp]: e.activation(
                         attfP, msP, AF.Exp, bias=sttP[:, 1:2],
                         scale=rq_tok[:, t * 8 + h: t * 8 + h + 1],
                         accum_out=sttP[:, 2:3]),
                     waits=[bias_of[u]], incs=[(SCA, 1)])
                exp_of[u] = SCA.done

            def _e_back(u):
                h, t = u // 8, u % 8
                pp = u % 2
                sttP = stt[pp]
                b.op("vector", lambda e, sttP=sttP: e.reciprocal(
                    sttP[:, 3:4], sttP[:, 2:3]),
                    waits=[exp_of[u]], incs=[(VEC, 1)])
                b.op("vector", lambda e, h=h, t=t, attfP=attf[pp], sttP=sttP,
                     attbfP=attbf[pp]: e.tensor_scalar(
                         attbfP, attfP, sttP[:, 3:4],
                         gate_tok[:, t * 8 + h: t * 8 + h + 1],
                         op0=ALU.mult, op1=ALU.mult),
                     waits=[VEC.done], incs=[(VEC, 1)])
                attbf_of[u] = VEC.done

            def _e_trscopy(u):
                pp = u % 2
                for kk in range(5):
                    psf = psF1 if kk % 2 == 0 else psF2
                    wts = [attbf_of[u]]
                    if kk >= 2:
                        wts.append(cv_of[(u, kk - 2)])
                    elif u >= 1:
                        wts.append(cv_of[(u - 1, 4 if kk == 0 else 3)])
                    b.op("tensor", lambda e, kk=kk, psf=psf, attbfP=attbf[pp]:
                         e.transpose(psf[:, :],
                                     attbfP[:, kk * 128:(kk + 1) * 128],
                                     ident_sb[:, :]),
                         waits=wts, incs=[(TEN, 1)])
                    tv_of[(u, kk)] = TEN.done
                    b.op("scalar", lambda e, kk=kk, psf=psf, attTP=attT[pp]:
                         e.activation(attTP[:, kk * 128:(kk + 1) * 128],
                                      psf[:, :], AF.Copy),
                         waits=[tv_of[(u, kk)]], incs=[(SCA, 1)])
                    cv_of[(u, kk)] = SCA.done

            av_dst = [psD[:, 0:128], psB[:, 128:256]]

            def _e_avmm(u):
                h, t = u // 8, u % 8
                pd = av_dst[u % 2]

                def _avmm(e, h=h, t=t, attTP=attT[u % 2], pd=pd):
                    last = None
                    for kk in range(5):
                        last = e.matmul(
                            pd,
                            v_sb[:, (t + kk) * 1024 + h * 128:
                                 (t + kk) * 1024 + (h + 1) * 128],
                            attTP[:, kk * 128:(kk + 1) * 128],
                            start=(kk == 0), stop=(kk == 4))
                    return last
                wts = [cv_of[(u, 4)], v_done]
                if u == 0:
                    wts.append(gates_done)
                if u - 2 in oev_of:
                    wts.append(oev_of[u - 2])
                b.op("tensor", _avmm, waits=wts, incs=[(TEN, 1)])
                av_of[u] = TEN.done

            def _e_oev(u):
                h, t = u // 8, u % 8
                wts = [av_of[u]] + ([yzero_done] if u == 0 else [])
                b.op("vector", lambda e, h=h, t=t, pd=av_dst[u % 2]:
                     e.tensor_copy(
                         o_sb[:, h * TPC + t * 128: h * TPC + (t + 1) * 128],
                         pd),
                     waits=wts, incs=[(VEC, 1)])
                oev_of[u] = VEC.done

            _e_sim(0)
            _e_sim(1)
            for u in range(NU):
                _e_front(u)
                if u >= 1:
                    _e_back(u - 1)
                _e_exp(u)
                if u + 2 < NU:
                    _e_sim(u + 2)
                if u >= 1:
                    _e_trscopy(u - 1)
                    _e_avmm(u - 1)
                if u >= 2:
                    _e_oev(u - 2)
            _e_back(NU - 1)
            _e_trscopy(NU - 1)
            _e_avmm(NU - 1)
            _e_oev(NU - 2)
            _e_oev(NU - 1)
            attn_done = oev_of[NU - 1]

            prev_sev = {}
            for half in range(2):
                wsrc = woT.rearrange("(c p) m -> p c m", p=128)[:, :, half * 1024:(half + 1) * 1024]
                b.op("gpsimd", lambda e, s=wsrc: e.dma_start(
                    out=wreg[:, :].rearrange("p (c m) -> p c m", m=1024), in_=s),
                    waits=[v_done if half == 0 else prev_sev[(0, 15)]],
                    incs=[(WSM, 16)])
                wo_ld = WSM.done
                for t in range(8):
                    for dci in range(2):
                        u = t * 2 + dci
                        ps = psA if u % 2 == 0 else psB

                        def _smm(e, t=t, dci=dci, ps=ps):
                            last = None
                            for ic in range(8):
                                last = e.matmul(
                                    ps[:, :],
                                    o_sb[:, ic * TPC + t * 128: ic * TPC + (t + 1) * 128],
                                    wreg[:, ic * 1024 + dci * 512: ic * 1024 + (dci + 1) * 512],
                                    start=(ic == 0), stop=(ic == 7))
                            return last
                        wts = [wo_ld, attn_done]
                        if u >= 2:
                            wts.append(prev_sev[(half, u - 2)])
                        b.op("tensor", _smm, waits=wts, incs=[(TEN, 1)])
                        b.op("vector", lambda e, t=t, dci=dci, ps=ps: e.tensor_copy(
                            s_half[t % 2][:, dci * 512:(dci + 1) * 512], ps[:, :]),
                            waits=[TEN.done, (SSP, max(0, SSP.n - 16))],
                            incs=[(VEC, 1)])
                        prev_sev[(half, u)] = VEC.done
                    b.op("sync", lambda e, t=t, half=half: e.dma_start(
                        out=s_spill[t * 128:(t + 1) * 128,
                                    half * 1024:(half + 1) * 1024],
                        in_=s_half[t % 2]),
                        waits=[prev_sev[(half, t * 2 + 1)]], incs=[(SSP, 16)])
            s_done = SSP.done

        if SKIP_MOE:
            rs_done = (CC, CC.n)
            moe_skip = True
        else:
            moe_skip = False
        abf.reset(0)
        if not moe_skip:
            wg_o = abf.take(16 * 1024)
            wu_o = abf.take(16 * 1024)
            wd_o = abf.take(16 * 1024)
            xs_o = abf.take(2 * 16 * 512)
            mid_o = abf.take(8 * 128 * RSMAX)
            or_o = abf.take(2 * 2 * D)
            wg_sb = abf_t[:, wg_o:wg_o + 16 * 1024]
            wu_sb = abf_t[:, wu_o:wu_o + 16 * 1024]
            wd_sb = abf_t[:, wd_o:wd_o + 16 * 1024]
            xsT = [abf_t[:, xs_o + i * 8192: xs_o + (i + 1) * 8192] for i in range(2)]
            orow = [abf_t[:, or_o + i * 2 * D: or_o + (i + 1) * 2 * D] for i in range(2)]

            p2_bar = [(TEN, TEN.n), (VEC, VEC.n), (SCA, SCA.n), (SSP, SSP.n),
                  (TRS, TRS.n), (LD, LD.n)]

            slot_batches = [[bt for bt in batches if bt[0] == s] for s in range(NSLOT)]
            gather_ten = {}
            prev_scatter = None
            last_gu_mm = None
            last_dn_mm = None
            bid_global = 0
            ch_base = [0]
            for s in range(1, NSLOT):
                ch_base.append(ch_base[-1] + RS[s - 1])

            for s in range(NSLOT):
                wgsrc = wgS[s].rearrange("(c p) m -> p c m", p=128)
                wusrc = wuS[s].rearrange("(c p) m -> p c m", p=128)
                wdsrc = wdS[s].rearrange("(c p) m -> p c m", p=128)
                wts_w = list(p2_bar) if s == 0 else [last_gu_mm]
                b.op("gpsimd", lambda e, src=wgsrc: e.dma_start(
                    out=wg_sb[:, :].rearrange("p (c m) -> p c m", m=1024), in_=src),
                    waits=wts_w, incs=[(WSM, 16)])
                b.op("gpsimd", lambda e, src=wusrc: e.dma_start(
                    out=wu_sb[:, :].rearrange("p (c m) -> p c m", m=1024), in_=src),
                    incs=[(WSM, 16)])
                wgu_ld = WSM.done
                wts_d = [] if s == 0 else [last_dn_mm]
                b.op("gpsimd", lambda e, src=wdsrc: e.dma_start(
                    out=wd_sb[:, :].rearrange("p (c m) -> p c m", m=1024), in_=src),
                    waits=wts_d, incs=[(WSM, 16)])
                wd_ld = WSM.done

                dn_barrier = last_dn_mm
                prev_ev = {}
                for bi, (s_, gch0, nb) in enumerate(slot_batches[s]):
                    nidx = nb * 128
                    buf = bid_global % 2
                    wts_g = [ag_done, ph0_ld]
                    if s == 0 and bi == 0:
                        wts_g += p2_bar
                    if bid_global >= 2:
                        wts_g.append(gather_ten[bid_global - 2])
                    b.op("gpsimd", lambda e, buf=buf, gch0=gch0, nidx=nidx: e.dma_gather(
                        xsT[buf][:, 0:16 * nidx].rearrange("p (f i) -> p f i", f=16),
                        ag_out[:, :],
                        gidx_sb[:, gch0 * 8: gch0 * 8 + (nidx + 15) // 16],
                        nidx, nidx, 2048, transpose=True),
                        waits=wts_g, incs=[(GSM, 16)])
                    g_done = GSM.done

                    lc0 = gch0 - ch_base[s]
                    for ic in range(8):
                        psg = psA if ic % 2 == 0 else psC
                        psu = psB if ic % 2 == 0 else psD

                        def _gumm(e, buf=buf, ic=ic, nidx=nidx, psg=psg, psu=psu):
                            for hc in range(16):
                                e.matmul(psg[:, 0:nidx],
                                         wg_sb[:, hc * 1024 + ic * 128: hc * 1024 + (ic + 1) * 128],
                                         xsT[buf][:, hc * nidx:(hc + 1) * nidx],
                                         start=(hc == 0), stop=(hc == 15))
                            last = None
                            for hc in range(16):
                                last = e.matmul(psu[:, 0:nidx],
                                                wu_sb[:, hc * 1024 + ic * 128: hc * 1024 + (ic + 1) * 128],
                                                xsT[buf][:, hc * nidx:(hc + 1) * nidx],
                                                start=(hc == 0), stop=(hc == 15))
                            return last
                        wts = [g_done, wgu_ld]
                        if ic >= 2:
                            wts.append(prev_ev[(bi, ic - 2)])
                        elif bi > 0:
                            wts.append(prev_ev[(bi - 1, 6 + ic)])
                        b.op("tensor", _gumm, waits=wts, incs=[(TEN, 1)])
                        gu_done = TEN.done
                        if ic == 7:
                            gather_ten[bid_global] = gu_done
                            last_gu_mm = gu_done

                        sb_ = sil[ic % 2]
                        b.op("scalar", lambda e, nidx=nidx, psg=psg, sb_=sb_: e.activation(
                            sb_[:, 0:nidx], psg[:, 0:nidx], AF.Silu),
                            waits=[gu_done], incs=[(SCA, 1)])
                        sil_done = SCA.done

                        mo = mid_o + ic * 128 * RS[s] + lc0 * 128
                        wts_m = [sil_done]
                        if dn_barrier is not None and bi == 0:
                            wts_m.append(dn_barrier)
                        b.op("vector", lambda e, mo=mo, nidx=nidx, psu=psu, sb_=sb_:
                             e.tensor_mul(abf_t[:, mo:mo + nidx],
                                          sb_[:, 0:nidx], psu[:, 0:nidx]),
                             waits=wts_m, incs=[(VEC, 1)])
                        prev_ev[(bi, ic)] = VEC.done
                    bid_global += 1
                mid_done_of = {}
                for bi2, (s2, gch02, nb2) in enumerate(slot_batches[s]):
                    for lc2 in range(gch02 - ch_base[s], gch02 - ch_base[s] + nb2):
                        mid_done_of[lc2] = prev_ev[(bi2, 7)]
                mid_done = prev_ev[(len(slot_batches[s]) - 1, 7)]

                assert RS[s] % 2 == 0
                prev_dn = {}
                for rc in range(RS[s]):
                    gch = ch_base[s] + rc
                    grp = gch // 2
                    ob = orow[grp % 2]
                    oco = (rc % 2) * D
                    for dc in range(4):
                        ps = psA if dc % 2 == 0 else psB

                        def _dmm(e, rc=rc, dc=dc, ps=ps, s=s):
                            last = None
                            for ic in range(8):
                                last = e.matmul(
                                    ps[:, :],
                                    abf_t[:, mid_o + ic * 128 * RS[s] + rc * 128:
                                          mid_o + ic * 128 * RS[s] + (rc + 1) * 128],
                                    wd_sb[:, ic * 2048 + dc * 512: ic * 2048 + (dc + 1) * 512],
                                    start=(ic == 0), stop=(ic == 7))
                            return last
                        wts = [mid_done, wd_ld]
                        if rc > 0 or dc > 1:
                            pk = (rc, dc - 2) if dc >= 2 else (rc - 1, 2 + dc)
                            wts.append(prev_dn[pk])
                        b.op("tensor", _dmm, waits=wts, incs=[(TEN, 1)])
                        dn_done = TEN.done
                        if rc == RS[s] - 1 and dc == 3:
                            last_dn_mm = dn_done

                        wts_e = [dn_done, wfill_done]
                        if dc == 0 and rc % 2 == 0 and grp >= 2:
                            wts_e.append((SSM, SSM.n - 16))
                        b.op("vector", lambda e, gch=gch, dc=dc, ps=ps, ob=ob, oco=oco:
                             e.tensor_scalar_mul(ob[:, oco + dc * 512: oco + (dc + 1) * 512],
                                                 ps[:, :],
                                                 wrow_sb[:, gch:gch + 1]),
                             waits=wts_e, incs=[(VEC, 1)])
                        prev_dn[(rc, dc)] = VEC.done

                    if rc % 2 == 1:
                        wts_s = [prev_dn[(rc - 1, 3)], prev_dn[(rc, 3)], yzero_done]
                        if prev_scatter is not None:
                            wts_s.append(prev_scatter)
                        g0 = gch - 1
                        b.op("gpsimd", lambda e, g0=g0, ob=ob: e.dma_scatter_add(
                            y_main[:, :],
                            ob[:, :].rearrange("p (a m) -> p a m", a=2),
                            sidx_sb[:, g0 * 8:(g0 + 2) * 8], 256, 256, 2048),
                            waits=wts_s, incs=[(SSM, 16)])
                        prev_scatter = SSM.done

            b.op("gpsimd", lambda e: e.collective_compute(
                "ReduceScatter", ALU.add, replica_groups=[list(range(NCORES))],
                ins=[y_main.opt()], outs=[y_red.opt()]),
                waits=[prev_scatter], incs=[(CC, 1)])
            rs_done = CC.done

        abf.reset(0)
        yb_o = abf.take(2 * D)
        sb_o = abf.take(2 * D)
        yt = [abf_t[:, yb_o + i * D: yb_o + (i + 1) * D] for i in range(2)]
        st2 = [abf_t[:, sb_o + i * D: sb_o + (i + 1) * D] for i in range(2)]
        af32.reset(0)
        xf_o = af32.take(2 * D)
        g_o = af32.take(D)
        xt = [af32_t[:, xf_o + i * D: xf_o + (i + 1) * D] for i in range(2)]
        gsl = af32_t[:, g_o:g_o + D]

        prev_vec5 = {}
        prev_sca5 = {}
        for tt in range(8):
            buf = tt % 2

            def _fl(e, tt=tt, buf=buf):
                return e.dma_start(out=yt[buf],
                                   in_=y_red[tt * 128:(tt + 1) * 128, :])
            def _fl2(e, tt=tt, buf=buf):
                e.dma_start(out=st2[buf],
                            in_=s_spill[tt * 128:(tt + 1) * 128, :]).then_inc(LD2.h, 16)
                return e.dma_start(out=xt[buf],
                                   in_=x_sl[WIN + tt * 128: WIN + (tt + 1) * 128, :])
            LD2.n += 16
            wts = [rs_done, s_done]
            if tt >= 2:
                wts += [prev_vec5[tt - 2], prev_sca5[tt - 2]]
            b.op("sync", _fl, waits=wts, incs=[(LD, 16)])
            b.op("scalar", _fl2, waits=wts, incs=[(LD2, 16)])
            fl_done = LD.done
            fl2_done = LD2.done

            b.op("scalar", lambda e, buf=buf: e.activation(gsl, yt[buf], AF.Silu),
                 waits=[fl_done, fl2_done] + ([prev_vec5[tt - 1]] if tt >= 1 else []),
                 incs=[(SCA, 1)])
            prev_sca5[tt] = SCA.done

            b.op("vector", lambda e, buf=buf: e.tensor_mul(gsl, gsl, st2[buf]),
                 waits=[prev_sca5[tt], fl_done, fl2_done], incs=[(VEC, 1)])
            b.op("vector", lambda e, buf=buf: e.tensor_add(xt[buf], xt[buf], gsl),
                 waits=[VEC.done], incs=[(VEC, 1)])
            prev_vec5[tt] = VEC.done

            b.op("sync", lambda e, tt=tt, buf=buf: e.dma_start(
                out=out[tt * 128:(tt + 1) * 128, :], in_=xt[buf]),
                waits=[prev_vec5[tt]], incs=[(OS, 16)])

        if DEBUG_DUMPS:
            def _dbg(e):
                e.dma_start(out=dbg["d_xn"][:, :], in_=xn_loc[:, :]).then_inc(OS.h, 16)
                e.dma_start(out=dbg["d_ag"][:, :], in_=ag_out[:, :]).then_inc(OS.h, 16)
                e.dma_start(out=dbg["d_s"][:, :], in_=s_spill[:, :]).then_inc(OS.h, 16)
                e.dma_start(out=dbg["d_ym"][:, :], in_=y_main[:, :]).then_inc(OS.h, 16)
                return e.dma_start(out=dbg["d_yr"][:, :], in_=y_red[:, :])
            OS.n += 4 * 16
            b.op("sync", _dbg, waits=[rs_done, (OS, OS.n - 4 * 16)], incs=[(OS, 16)])
        b.op("sync", lambda e: e.nop(), waits=[(OS, OS.n)])

        with nc.Block() as block:
            b.emit(block)

    return nc



def kernel(**inputs):
    x = np.ascontiguousarray(np.asarray(inputs["L0_attn_x"])[0], np.float32)
    Wq = np.asarray(inputs["L0_attn_Wq"], np.float32)
    Wkv = np.asarray(inputs["L0_attn_Wkv"], np.float32)
    q_scale = np.asarray(inputs["L0_attn_q_scale"], np.float32)
    k_scale = np.asarray(inputs["L0_attn_k_scale"], np.float32)
    Wg = np.asarray(inputs["L0_attn_Wg"], np.float32)
    bg = np.asarray(inputs["L0_attn_bg"], np.float32)
    Wo = np.asarray(inputs["L0_attn_Wo"], np.float32)
    gate_w = np.asarray(inputs["L0_ffn_gate_weight"], np.float32)
    up_w = np.asarray(inputs["L0_ffn_up_weight"], np.float32)
    down_w = np.asarray(inputs["L0_ffn_down_weight"], np.float32)
    topk_idx = np.asarray(inputs["L0_ffn_topk_idx"])
    topk_weight = np.asarray(inputs["L0_ffn_topk_weight"], np.float32)

    experts, extw = _route(topk_idx, topk_weight)
    meta, cores = _plan(experts, extw)
    assert max(meta["RS"]) <= 14, "unexpectedly large expert group"
    assert meta["EWC"] <= 16

    nc = _build(meta)
    mybir.codegen_inst_isa_subclasses(nc)

    wqT = np.ascontiguousarray(Wq.T)
    wkvT = np.ascontiguousarray(Wkv.T)
    wgaT = np.ascontiguousarray(Wg.T)
    woT = np.ascontiguousarray(Wo.T)
    bgr = np.ascontiguousarray(bg.reshape(1, H))
    qsc = np.ascontiguousarray(q_scale.reshape(128, 1))
    ksc = np.ascontiguousarray(k_scale.reshape(128, 1))
    identb = np.eye(128, dtype=BF)
    onesc = np.ones((128, 1), BF)
    onesr = np.ones((1, 128), BF)

    qi = np.arange(128)[:, None]
    jj = np.arange(640)[None, :]
    base_ok = (jj >= qi) & (jj <= qi + WIN)
    masks0 = np.zeros((5, 128, 640), np.float32)
    for t in range(4):
        ok = base_ok & ((t * 128 + jj - WIN) >= 0)
        masks0[t] = np.where(ok, 0.0, MASKVAL)
    masks0[4] = np.where(base_ok, 0.0, MASKVAL)
    masks_other = np.broadcast_to(masks0[4], (5, 128, 640)).copy()
    masks_other = np.ascontiguousarray(masks_other)

    in_maps = []
    for c in range(NCORES):
        lo = c * TPC - WIN
        if lo < 0:
            x_slc = np.concatenate(
                [np.zeros((WIN, D), np.float32), x[0:TPC]], axis=0)
        else:
            x_slc = x[lo:c * TPC + TPC]
        x_slc = np.ascontiguousarray(x_slc)
        eo = cores[c]["expert_of"]
        wgS = np.ascontiguousarray(np.stack([gate_w[e].T for e in eo]))
        wuS = np.ascontiguousarray(np.stack([up_w[e].T for e in eo]))
        wdS = np.ascontiguousarray(np.stack([down_w[e].T for e in eo]))
        in_maps.append({
            "x_sl": x_slc, "wqT": wqT, "wkvT": wkvT, "wgaT": wgaT,
            "woT": woT, "bgr": bgr, "qsc": qsc, "ksc": ksc,
            "maskt": masks0 if c == 0 else masks_other,
            "identb": identb, "onesc": onesc, "onesr": onesr,
            "wgS": wgS, "wuS": wuS, "wdS": wdS,
            "gidx": cores[c]["gidx"], "sidx": cores[c]["sidx"],
            "wrow": cores[c]["wrow"], "extw": cores[c]["extw"],
        })

    res = run_bass_kernel_spmd(nc, in_maps, core_ids=list(range(NCORES)))
    shards = [res.results[c]["out"] for c in range(NCORES)]
    full = np.concatenate(shards, axis=0).reshape(1, NTOK, D)
    return full
